# version 23
# baseline (speedup 1.0000x reference)
import numpy as np
from contextlib import ExitStack

import concourse.bass as bass
import concourse.mybir as mybir
from concourse.bass_utils import run_bass_kernel_spmd

F32 = mybir.dt.float32
F16 = mybir.dt.float16
AF = mybir.ActivationFunctionType
ALU = mybir.AluOpType
AX = mybir.AxisListType.X

T = 4096
C = 1024
STAGE = 6
DEBUG = ()
P5_HEADS = 8
P5_QB = 8
P5_YD = True
P5_TAIL = 4


class P:
    def __init__(self, nc, es):
        self.nc = nc
        self.sem = {}
        self.cnt = {}
        self.bar_lvl = 0
        self.es = es

    def mksem(self, *names):
        for n in names:
            self.sem[n] = self.es.enter_context(self.nc.semaphore(n))
            self.cnt[n] = 0

    def inc(self, name, amt=1):
        self.cnt[name] += amt
        return self.sem[name]

    def barrier(self, block):
        self.bar_lvl += 5
        h, lvl = self.sem["bar"], self.bar_lvl
        for dec, eng in ((block.sync, 0), (block.tensor, 0), (block.vector, 1),
                         (block.scalar, 0), (block.gpsimd, 1)):
            @dec
            def _(e, h=h, lvl=lvl, eng=eng):
                if eng:
                    e.engine_nop().then_inc(h, 1)
                else:
                    e.nop().then_inc(h, 1)
                e.wait_ge(h, lvl)


def build(stage=None, debug=None):
    stage = STAGE if stage is None else stage
    debug = DEBUG if debug is None else debug
    nc = bass.Bass(trn_type="TRN2", target_bir_lowering=False)

    def din(name, shape, dtype=F32):
        return nc.dram_tensor(name, shape, dtype, kind="ExternalInput").ap()

    x_d = din("x", [T, C])
    xh_d = din("xh", [T, 512])
    wdkvT_d = din("wdkvT", [C, 512])
    wdqT_d = din("wdqT", [C, 512])
    wuqT_d = din("wuqT", [512, C])
    wuqTh_d = din("wuqTh", [512, 512])
    wdqh_d = din("wdqh", [512, 512])
    wuk_d = din("wuk", [C, 512])
    wuv_d = din("wuv", [C, 512])
    woTh_d = din("woTh", [C, 512])
    wqT_d = din("wqT", [C, 8192])
    keysT_d = din("keysT", [2, 8, 512, 256])
    wd256T_d = din("wd256T", [C, 256])
    wup256h_d = din("wup256h", [256, 512])
    ident_d = din("ident", [128, 128], F16)
    masks_d = din("masks", [5, 128, 512], F16)
    cst_d = din("cst", [128, 8])

    out_d = nc.dram_tensor("out", [T, 512], F32, kind="ExternalOutput").ap()
    g_scr = nc.dram_tensor("g_scr", [T, 512], F16,
                           kind="ExternalOutput" if "g_scr" in debug else "Internal")
    keys16_scr = nc.dram_tensor("keys16_scr", [2, 8, 512, 256], F16).ap()
    scr16 = nc.dram_tensor("scr16", [128, 28 * 1024], F16,
                           kind="ExternalOutput" if "scr16" in debug
                           else "Internal").ap()

    es = ExitStack()
    with es:
        block = es.enter_context(nc.Block())
        p = P(nc, es)
        p.mksem("bar", "dx0", "dx1", "dcst", "dwq0", "dwq1", "dkk", "dg0", "dg1", "dxh0", "dxh1", "dout0", "dout1",
                "a1", "a2", "v2", "v3", "ac1", "pe1", "se1",
                "ge2", "pe2", "se2", "se2s", "ve2m", "ve2k",
                "ve3a", "ae3", "ve3t", "ve3r", "ae3g", "ve3m", "pe3t", "se3",
                "pe3f", "ae3s",
                "ge4", "pe4", "se4",
                "ae5", "pt", "pe5s", "pe5y", "pe5d", "ve5r", "pe5b", "dvt",
                "ve5q", "pe6", "ve6", "se6", "vch",
                "dpc", "gpc", "dps", "gwq", "dkl", "dks", "gks")

        ident = es.enter_context(nc.sbuf_tensor("identb", [128, 128], F16))
        cst = es.enter_context(nc.sbuf_tensor("cstb", [128, 8], F32))
        st1 = es.enter_context(nc.sbuf_tensor("st1b", [128, 64], F32))
        pb = [es.enter_context(nc.psum_tensor("pb%d" % i, [128, 512], F32))
              for i in range(8)]
        pbh = [b.ap().bitcast(F16) for b in pb]

        dbg = {}

        @block.sync
        def _(sync):
            sync.dma_start(ident[:], ident_d).then_inc(p.inc("dcst", 16), 16)
            sync.dma_start(cst[:], cst_d).then_inc(p.inc("dcst", 16), 16)

        ckvT = es.enter_context(nc.sbuf_tensor("ckvT", [128, 4 * T], F16))
        qattT = es.enter_context(nc.sbuf_tensor("qattT", [128, 4 * T], F16))
        keff = es.enter_context(nc.sbuf_tensor("keff", [128, 4 * 512], F16))
        veff = es.enter_context(nc.sbuf_tensor("veff", [128, 4 * 512], F16))
        for nm, t in (("ckvT", ckvT), ("qattT", qattT),
                      ("keff", keff), ("veff", veff)):
            if nm in debug:
                dbg[nm] = t

        with nc.sbuf_tensor("xnT", [128, 8 * T], F16, side="right") as xnT:
            if "xnT" in debug:
                dbg["xnT"] = xnT
            qf32 = qattT.ap().bitcast(F32)
            xbuf = qf32[:, 0:8192]
            sqscr = ckvT.ap().bitcast(F32)[:, 0:1024]
            xnt = ckvT[:, 2048:6144]
            emit_p1(nc, block, p, x_d, xnT, xbuf, sqscr, xnt, st1,
                    cst, ident, pbh)
            if stage >= 2:
                wsrc = (wdkvT_d, wdqT_d, wuqTh_d, wuqT_d, wdqh_d, wuk_d,
                        wuv_d, woTh_d)
                emit_p23(nc, block, p, xnT, qattT, wqT_d, keysT_d,
                         keys16_scr, wd256T_d, wup256h_d, g_scr, pb, pbh,
                         ident, dbg, debug, stage, ckvT, scr16, wsrc)
        if stage >= 4:
            emit_p4ab(nc, block, p, scr16, keff, veff, pb)
            p.barrier(block)

        if stage >= 5:
            masks = es.enter_context(nc.sbuf_tensor("masksb", [128, 5 * 512], F16))

            @block.sync
            def _(sync):
                sync.dma_start(masks[:].rearrange("q (m w) -> q m w", m=5),
                               masks_d.rearrange("m q w -> q m w")
                               ).then_inc(p.inc("dcst", 16), 16)
                sync.nop().then_inc(p.inc("bar2" if False else "dcst", 0), 0) if False else None

            @block.tensor
            def _(tensor):
                tensor.wait_ge(p.sem["dcst"], p.cnt["dcst"])

            @block.vector
            def _(vector):
                vector.wait_ge(p.sem["dcst"], p.cnt["dcst"])

            sT = es.enter_context(nc.sbuf_tensor("sT", [128, 4 * T], F16))
            if "sT" in debug:
                dbg["sT"] = sT
            u = es.enter_context(nc.sbuf_tensor("u", [128, 32 * 520], F16))
            qlat = es.enter_context(nc.sbuf_tensor("qlat", [128, 4 * T], F16))
            if "u" in debug:
                dbg["u"] = u
            if "qlat" in debug:
                dbg["qlat"] = qlat
            emit_p4b(nc, block, p, ckvT, veff, u, pb)
            p.barrier(block)
            emit_p5(nc, block, p, ckvT, qattT, keff, u, qlat, sT, masks, cst,
                    pb, pbh, ident, xh_d, g_scr, out_d, do_p6=(stage >= 6))
            p.barrier(block)

        if dbg:
            p.barrier(block)
            dbga = {nm: (t if getattr(t, "tensor", None) is not None else t[:])
                    for nm, t in dbg.items()}
            aps = {nm: nc.dram_tensor("dbg_" + nm, list(a.shape), a.dtype,
                                      kind="ExternalOutput").ap()
                   for nm, a in dbga.items()}

            @block.sync
            def _(sync):
                for nm, a in dbga.items():
                    sync.dma_start(aps[nm], a).then_inc(p.inc("dout0", 16), 16)
                sync.wait_ge(p.sem["dout0"], p.cnt["dout0"])
        if stage < 6:
            @block.gpsimd
            def _(gp):
                gp.wait_ge(p.sem["bar"], p.bar_lvl)

    return nc



def emit_p1(nc, block, p, x_d, xnT, xbuf, sqscr, xnt, st1, cst, ident, pbh):
    NT = T // 128
    x_t = x_d.rearrange("(n q) c -> n q c", q=128)
    stv = st1[:].rearrange("q (j f) -> q j f", j=8)

    def xsl(i):
        return xbuf[:, (i % 8) * C:(i % 8 + 1) * C]

    @block.scalar
    def _(scalar):
        scalar.wait_ge(p.sem["dcst"], 32)
        for i in range(8):
            scalar.dma_start(xsl(i), x_t[i]).then_inc(p.inc("dx0", 16), 16)
        for g in range(NT // 8):
            for j in range(8):
                i = 8 * g + j
                scalar.wait_ge(p.sem["dx0"], 16 * (i + 1))
                if g >= 1 and j == 0:
                    scalar.wait_ge(p.sem["v2"], g)
                scalar.activation(sqscr[:], xsl(i), AF.Square,
                                  accum_out=st1[:, j * 8 + 1:j * 8 + 2]
                                  ).then_inc(p.inc("a1"), 1)
            scalar.wait_ge(p.sem["v2"], g + 1)
            scalar.activation(stv[:, :, 4:5], stv[:, :, 3:4],
                              AF.Ln, bias=cst[:, 1:2])
            scalar.activation(stv[:, :, 5:6], stv[:, :, 4:5],
                              AF.Exp, scale=-0.5).then_inc(p.inc("a2"), 1)
            for j in range(8):
                i = 8 * g + j
                scalar.wait_ge(p.sem["pe1"], i + 1)
                scalar.activation(
                    xnT[:].rearrange("q (c t) -> q c t", c=8)[:, :, i * 128:(i + 1) * 128],
                    pbh[4 + i % 4][:, 0:1024].rearrange("q (c t) -> q c t", c=8),
                    AF.Copy).then_inc(p.inc("se1"), 1)
                if i + 8 < NT:
                    scalar.dma_start(xsl(i + 8), x_t[i + 8]).then_inc(
                        p.inc("dx0", 16), 16)

    @block.vector
    def _(vector):
        def step(inst):
            inst.then_inc(p.inc("vch"), 1)
            vector.wait_ge(p.sem["vch"], p.cnt["vch"])
        for g in range(NT // 8):
            for j in range(8):
                i = 8 * g + j
                vector.wait_ge(p.sem["dx0"], 16 * (i + 1))
                rs = vector.reduce_sum(st1[:, j * 8:j * 8 + 1], xsl(i),
                                       axis=AX)
            step(rs)
            vector.wait_ge(p.sem["a1"], 8 * (g + 1))
            step(vector.tensor_scalar_mul(stv[:, :, 2:3], stv[:, :, 0:1],
                                          1.0 / C))
            step(vector.tensor_tensor(stv[:, :, 3:4], stv[:, :, 2:3],
                                      stv[:, :, 2:3], op=ALU.mult))
            vector.scalar_tensor_tensor(stv[:, :, 3:4], stv[:, :, 1:2],
                                        1.0 / C, stv[:, :, 3:4], op0=ALU.mult,
                                        op1=ALU.subtract).then_inc(
                                            p.inc("v2"), 1)
            vector.wait_ge(p.sem["a2"], g + 1)
            step(vector.scalar_tensor_tensor(
                stv[:, :, 6:7], stv[:, :, 2:3], -1.0,
                stv[:, :, 5:6], op0=ALU.mult, op1=ALU.mult))
            for j in range(8):
                i = 8 * g + j
                if i >= 4:
                    vector.wait_ge(p.sem["pe1"], i - 3)
                vector.scalar_tensor_tensor(
                    xnt[:, (i % 4) * C:(i % 4 + 1) * C], xsl(i),
                    st1[:, j * 8 + 5:j * 8 + 6],
                    st1[:, j * 8 + 6:j * 8 + 7].to_broadcast([128, C]),
                    op0=ALU.mult, op1=ALU.add).then_inc(p.inc("ac1"), 1)

    @block.tensor
    def _(tensor):
        tensor.wait_ge(p.sem["dcst"], p.cnt["dcst"])
        for i in range(NT):
            tensor.wait_ge(p.sem["ac1"], i + 1)
            if i >= 4:
                tensor.wait_ge(p.sem["se1"], i - 3)
            for j in range(8):
                ti = tensor.transpose(
                    pbh[4 + i % 4][:, j * 128:(j + 1) * 128],
                    xnt[:, (i % 4) * C + j * 128:(i % 4) * C + (j + 1) * 128],
                    ident[:])
            ti.then_inc(p.inc("pe1"), 1)



def emit_p23(nc, block, p, xnT, big32, wqT_d, keysT_d, keys16_scr, wd256T_d,
             wup256h_d, g_scr, pb, pbh, ident, dbg, debug, stage,
             ckvT=None, scr16=None, wsrc=None):
    g_t = g_scr.ap().rearrange("(n q) c -> n q c", q=128)
    big_f = big32.ap().bitcast(F32)
    sxq = [big_f[:, 0:2048], big_f[:, 2048:4096]]
    p3f = big_f[:, 4096:6552]
    hfull = big32[:, 13104:15152]
    with (
        nc.sbuf_tensor("a16s", [128, 8 * 256], F16, side="right") as a16s,
        nc.sbuf_tensor("mrscr", [128, 256], F32, side="right") as mrscr,
        nc.sbuf_tensor("wdk16", [128, 3072], F16, side="right") as wdk16,
        nc.sbuf_tensor("gt16", [128, 2 * 512], F16, side="right") as gt16,
        nc.sbuf_tensor("p3h", [128, 2560], F16, side="right") as p3h,
        nc.sbuf_tensor("f16st", [128, 1024], F16, side="right") as f16st,
    ):
        es2 = ExitStack()
        wqf = es2.enter_context(
            nc.sbuf_tensor("wqf", [128, 2 * 512], F32, side="right"))
        wq32 = es2.enter_context(
            nc.sbuf_tensor("wq32", [128, 2 * 1024], F32, side="right"))
        wq16 = es2.enter_context(
            nc.sbuf_tensor("wq16", [128, 2 * 4096], F16, side="right"))
        keys16 = es2.enter_context(
            nc.sbuf_tensor("keys16", [128, 2 * 1024], F16, side="right"))
        qph = es2.enter_context(
            nc.sbuf_tensor("qph", [128, 4 * 1024], F16, side="right"))
        simf = es2.enter_context(
            nc.sbuf_tensor("simf", [128, 8 * 256], F32, side="right"))
        if "sx" in debug:
            dbg["sx"] = sxq[1]
        if "sx0" in debug:
            dbg["sx0"] = sxq[0]
        if "as" in debug:
            dbg["as"] = p3f
        if "hfull" in debug:
            dbg["hfull"] = hfull
        if "qph" in debug:
            dbg["qph"] = qph
        AS0, ACF, TH, NGM, DN, RC = 0, 2048, 2304, 2432, 2440, 2448
        WSEL, GH, AT2 = 0, 2048, 2304

        _wdt = wd256T_d.rearrange("(a q) k -> q a k", q=128)
        _wut = wup256h_d.rearrange("(a q) c -> q a c", q=128)
        WDK_PIECES = ([( _wdt[:, 2 * j:2 * j + 2, :], 2, 512 * j)
                       for j in range(4)] +
                      [(_wut[:, j:j + 1, :], 1, 2048 + 512 * j)
                       for j in range(2)])

        def p3_bases():
            return {n: p.cnt[n] for n in ("ve3a", "ae3", "ve3t", "ve3r",
                                          "ae3g", "ve3m", "pe3t", "se3",
                                          "pe3f", "ae3s", "dg0", "dg1")}

        def p3_vector_tile(vector, tt, B, sx):
            base = tt * 256
            v_i = sx[:, base:base + 128].rearrange(
                "q (h i) -> q h i", h=8).unsqueeze(3).to_broadcast(
                [128, 8, 16, 16])
            v_j = sx[:, base + 128:base + 256].rearrange(
                "q (h j) -> q h j", h=8).unsqueeze(2).to_broadcast(
                [128, 8, 16, 16])
            vector.tensor_tensor(
                p3f[:, AS0:AS0 + 2048].rearrange("q (h i j) -> q h i j",
                                                 h=8, i=16),
                v_i, v_j, op=ALU.add)
            sx_h = sx[:, base:base + 128].rearrange("q (h i) -> q h i",
                                                    h=8)[:, :, 0:1]
            sy_h = sx[:, base + 128:base + 256].rearrange("q (h i) -> q h i",
                                                          h=8)[:, :, 0:1]
            vector.scalar_tensor_tensor(
                p3f[:, NGM:NGM + 8].unsqueeze(2), sx_h, -1.0, sy_h,
                op0=ALU.mult, op1=ALU.subtract).then_inc(p.inc("ve3a"), 1)
            for hh in range(8):
                vi = vector.max(out=p3f[:, TH + hh * 16:TH + hh * 16 + 8],
                                in_=p3f[:, AS0 + hh * 256:AS0 + (hh + 1) * 256])
            vi.then_inc(p.inc("ve3t"), 1)
            vector.wait_ge(p.sem["ve3t"], p.cnt["ve3t"])
            for hh in range(8):
                vector.match_replace(
                    out=mrscr[:],
                    in_to_replace=p3f[:, TH + hh * 16:TH + hh * 16 + 8],
                    in_values=p3f[:, AS0 + hh * 256:AS0 + (hh + 1) * 256],
                    imm_value=-1e30)
                vi = vector.max(out=p3f[:, TH + hh * 16 + 8:TH + hh * 16 + 16],
                                in_=mrscr[:])
            vi.then_inc(p.inc("ve3t"), 1)
            vector.wait_ge(p.sem["ve3t"], p.cnt["ve3t"])
            vector.wait_ge(p.sem["ae3"], B["ae3"] + tt + 1)
            for hh in range(8):
                vi = vector.scalar_tensor_tensor(
                    p3h[:, WSEL + hh * 256:WSEL + (hh + 1) * 256],
                    p3f[:, AS0 + hh * 256:AS0 + (hh + 1) * 256],
                    p3f[:, TH + hh * 16 + 15:TH + hh * 16 + 16],
                    p3h[:, WSEL + hh * 256:WSEL + (hh + 1) * 256],
                    op0=ALU.is_ge, op1=ALU.mult,
                    accum_out=p3f[:, DN + hh:DN + hh + 1])
            vi.then_inc(p.inc("ve3r"), 1)
            vector.wait_ge(p.sem["ve3r"], p.cnt["ve3r"])
            vector.reciprocal(p3f[:, RC:RC + 8],
                              p3f[:, DN:DN + 8]).then_inc(p.inc("ve3r"), 1)
            vector.wait_ge(p.sem["ve3r"], p.cnt["ve3r"])
            for hh in range(8):
                if hh == 0:
                    vector.tensor_scalar(
                        p3f[:, ACF:ACF + 256], p3h[:, WSEL:WSEL + 256],
                        p3f[:, RC:RC + 1], None, op0=ALU.mult)
                else:
                    vector.scalar_tensor_tensor(
                        p3f[:, ACF:ACF + 256],
                        p3h[:, WSEL + hh * 256:WSEL + (hh + 1) * 256],
                        p3f[:, RC + hh:RC + hh + 1],
                        p3f[:, ACF:ACF + 256], op0=ALU.mult, op1=ALU.add)
            vector.wait_ge(p.sem["ae3g"], B["ae3g"] + tt + 1)
            if B["pe3t"] >= 8:
                vector.wait_ge(p.sem["pe3t"], B["pe3t"] - 8 + tt + 1)
            vector.tensor_tensor(a16s[:, tt * 256:(tt + 1) * 256],
                                 p3h[:, GH:GH + 256],
                                 p3f[:, ACF:ACF + 256],
                                 op=ALU.mult).then_inc(p.inc("ve3m"), 1)

        def p3_scalar_a(scalar, tt, B):
            scalar.wait_ge(p.sem["ve3a"], B["ve3a"] + tt + 1)
            for hh in range(8):
                si_ = scalar.activation(
                    p3h[:, WSEL + hh * 256:WSEL + (hh + 1) * 256],
                    p3f[:, AS0 + hh * 256:AS0 + (hh + 1) * 256],
                    AF.Exp, bias=p3f[:, NGM + hh:NGM + hh + 1])
            si_.then_inc(p.inc("ae3"), 1)
            scalar.activation(p3h[:, GH:GH + 256],
                              hfull[:, tt * 256:(tt + 1) * 256],
                              AF.Gelu).then_inc(p.inc("ae3g"), 1)

        def p3_tail(block, tq, B):
            @block.tensor
            def _(tensor):
                for tt in range(8):
                    tensor.wait_ge(p.sem["ve3m"], B["ve3m"] + tt + 1)
                    if B["se3"] + tt > 0:
                        tensor.wait_ge(p.sem["se3"], B["se3"] + tt)
                    for rc in range(2):
                        ti = tensor.transpose(
                            pbh[7][:, rc * 128:(rc + 1) * 128],
                            a16s[:, tt * 256 + rc * 128:tt * 256 + (rc + 1) * 128],
                            ident[:])
                    ti.then_inc(p.inc("pe3t"), 1)
                    tensor.wait_ge(p.sem["se3"], B["se3"] + tt + 1)
                    if B["ae3s"] + tt > 0:
                        tensor.wait_ge(p.sem["ae3s"], B["ae3s"] + tt)
                    for rc in range(2):
                        mm = tensor.matmul(
                            pb[6][:],
                            p3h[:, AT2 + rc * 128:AT2 + (rc + 1) * 128],
                            wdk16[:, 2048 + rc * 512:2048 + (rc + 1) * 512],
                            start=(rc == 0), stop=(rc == 1))
                    mm.then_inc(p.inc("pe3f"), 1)

            @block.scalar
            def _(scalar):
                for tt in range(8):
                    scalar.wait_ge(p.sem["pe3t"], B["pe3t"] + tt + 1)
                    scalar.activation(p3h[:, AT2:AT2 + 256], pbh[7][:, 0:256],
                                      AF.Copy).then_inc(p.inc("se3"), 1)
                    scalar.wait_ge(p.sem["pe3f"], B["pe3f"] + tt + 1)
                    par = tt % 2
                    if tt >= 2:
                        scalar.wait_ge(p.sem["dg%d" % par],
                                       B["dg%d" % par] + 16 * ((tt - 2) // 2 + 1))
                    elif B["dg%d" % par] > 0:
                        scalar.wait_ge(p.sem["dg%d" % par], B["dg%d" % par])
                    scalar.activation(gt16[:, par * 512:(par + 1) * 512],
                                      pb[6][:], AF.Silu).then_inc(
                                          p.inc("ae3s"), 1)

            @block.gpsimd
            def _(gp):
                for tt in range(8):
                    gp.wait_ge(p.sem["ae3s"], B["ae3s"] + tt + 1)
                    gp.dma_start(
                        g_t[tq * 8 + tt],
                        gt16[:, (tt % 2) * 512:(tt % 2 + 1) * 512]
                        ).then_inc(p.inc("dg%d" % (tt % 2), 16), 16)

        def p3t_pe_t(tensor, tt, B):
            tensor.wait_ge(p.sem["ve3m"], B["ve3m"] + tt + 1)
            if B["se3"] + tt > 0:
                tensor.wait_ge(p.sem["se3"], B["se3"] + tt)
            for rc in range(2):
                ti = tensor.transpose(
                    pbh[7][:, rc * 128:(rc + 1) * 128],
                    a16s[:, tt * 256 + rc * 128:tt * 256 + (rc + 1) * 128],
                    ident[:])
            ti.then_inc(p.inc("pe3t"), 1)

        def p3t_pe_f(tensor, tt, B):
            tensor.wait_ge(p.sem["se3"], B["se3"] + tt + 1)
            if B["ae3s"] + tt > 0:
                tensor.wait_ge(p.sem["ae3s"], B["ae3s"] + tt)
            for rc in range(2):
                mmx = tensor.matmul(
                    pb[6][:],
                    p3h[:, AT2 + rc * 128:AT2 + (rc + 1) * 128],
                    wdk16[:, 2048 + rc * 512:2048 + (rc + 1) * 512],
                    start=(rc == 0), stop=(rc == 1))
            mmx.then_inc(p.inc("pe3f"), 1)

        def p3t_sc_at2(scalar, tt, B):
            scalar.wait_ge(p.sem["pe3t"], B["pe3t"] + tt + 1)
            scalar.activation(p3h[:, AT2:AT2 + 256], pbh[7][:, 0:256],
                              AF.Copy).then_inc(p.inc("se3"), 1)

        def p3t_sc_silu(scalar, tt, B):
            scalar.wait_ge(p.sem["pe3f"], B["pe3f"] + tt + 1)
            par = tt % 2
            if tt >= 2:
                scalar.wait_ge(p.sem["dg%d" % par],
                               B["dg%d" % par] + 16 * ((tt - 2) // 2 + 1))
            elif B["dg%d" % par] > 0:
                scalar.wait_ge(p.sem["dg%d" % par], B["dg%d" % par])
            scalar.activation(gt16[:, par * 512:(par + 1) * 512],
                              pb[6][:], AF.Silu).then_inc(
                                  p.inc("ae3s"), 1)

        if wsrc is not None:
            (wdkvT_d, wdqT_d, wuqTh_d, wuqT_d, wdqh_d, wuk_d,
             wuv_d, woTh_d) = wsrc
            PIECES = (
                [(wdkvT_d, 2 * j, 2, j * 1024) for j in range(4)] +
                [(wdqT_d, 2 * j, 2, 4096 + j * 1024) for j in range(4)] +
                [(wuqTh_d, 2 * j, 2, 8192 + j * 1024) for j in range(2)] +
                [(wuqT_d, j, 1, 10240 + j * 1024) for j in range(4)] +
                [(wdqh_d, 2 * j, 2, 14336 + j * 1024) for j in range(2)] +
                [(wuk_d, 2 * j, 2, 16384 + j * 1024) for j in range(4)] +
                [(wuv_d, 2 * j, 2, 20480 + j * 1024) for j in range(4)] +
                [(woTh_d, 2 * j, 2, 24576 + j * 1024) for j in range(4)])
        else:
            PIECES = []

        wt32 = wqT_d.rearrange("(a q) r -> q a r", q=128)
        kt32 = keysT_d.rearrange("p h (a q) k -> q (p h a) k", q=128)
        ks16 = keys16_scr.rearrange("p h (a q) k -> q (p h a) k", q=128)
        dwq_b = (p.cnt["dwq0"], p.cnt["dwq1"])
        gwq_b = p.cnt["gwq"]
        pe0s = []

        def wq_load_wait(G, qp):
            u = 4 * G + qp
            return gwq_b + u - 1 if u >= 2 else None

        def wq_cast_unit(eng, G, qp):
            par = G % 2
            eng.wait_ge(p.sem["dwq%d" % par],
                        dwq_b[par] + 64 * (G // 2) + 16 * (qp + 1))
            dst = wq16[:, par * 4096 + qp * 1024:par * 4096 + (qp + 1) * 1024]
            src = wq32[:, (qp % 2) * 1024:(qp % 2 + 1) * 1024]
            if hasattr(eng, "activation"):
                eng.activation(dst, src, AF.Copy).then_inc(p.inc("gwq"), 1)
            else:
                eng.tensor_copy(dst, src).then_inc(p.inc("gwq"), 1)

        if True:
            p.barrier(block)
        prevB = None
        prev_tq = None
        for tq in range(4):
            t0 = tq * 1024
            dk0 = p.cnt["dkk"]
            pe0 = p.cnt["pe2"]
            pe0s.append(pe0)
            se0 = p.cnt["se2"]
            ses0 = p.cnt["se2s"]
            vem0 = p.cnt["ve2m"]

            @block.sync
            def _(sync, tq=tq, pe0=pe0):
                for ph in range(16):
                    pp, hh = ph // 8, ph % 8
                    par = ph % 2
                    r0 = pp * 4096 + hh * 512
                    G = 16 * tq + ph
                    for qp in range(4):
                        w = wq_load_wait(G, qp)
                        if w is not None:
                            sync.wait_ge(p.sem["gwq"], w)
                        sync.dma_start(
                            wq32[:, (qp % 2) * 1024:(qp % 2 + 1) * 1024]
                            .rearrange("q (a r) -> q a r", a=2),
                            wt32[:, 2 * qp:2 * qp + 2, r0:r0 + 512]
                            ).then_inc(p.inc("dwq%d" % par, 16), 16)
                    if tq >= 1:
                        if ph >= 2:
                            sync.wait_ge(p.sem["pe2"], pe0 + (ph - 1) * 16)
                        else:
                            sync.wait_ge(p.sem["pe2"], pe0 - 24 + 16 * ph)
                        if tq == 1 and ph == 0:
                            sync.wait_ge(p.sem["dks"], 256)
                        sync.dma_start(
                            keys16[:, par * 1024:(par + 1) * 1024].rearrange(
                                "q (a k) -> q a k", a=4),
                            keys16_scr[pp, hh].rearrange(
                                "(a q) k -> q a k", q=128)
                            ).then_inc(p.inc("dkk", 16), 16)
                    if tq >= 1 and ph < 10:
                        idx = 10 * (tq - 1) + ph
                        if idx < len(PIECES):
                            (src, clo, nch, dcol) = PIECES[idx]
                            st = src.rearrange("(a q) k -> q a k", q=128)
                            if idx == 0:
                                sync.wait_ge(p.sem["gks"], 16)
                                sync.wait_ge(p.sem["ge2"], 6)
                            else:
                                sync.wait_ge(p.sem["gpc"], idx)
                            sync.dma_start(
                                wqf[:].rearrange("q (a k) -> q a k", a=nch),
                                st[:, clo:clo + nch, :]).then_inc(
                                    p.inc("dpc", 16), 16)

            @block.gpsimd
            def _(gp, tq=tq, pe0=pe0):
                for ph in range(16):
                    par = ph % 2
                    G = 16 * tq + ph
                    if tq == 0:
                        gp.dma_start(wqf[:].rearrange("q (a k) -> q a k", a=4),
                                     kt32[:, 4 * ph:4 * ph + 4, :]
                                     ).then_inc(p.inc("dkl", 16), 16)
                        gp.wait_ge(p.sem["dkl"], p.cnt["dkl"])
                        if ph >= 2:
                            gp.wait_ge(p.sem["pe2"], pe0 + (ph - 1) * 16)
                            gp.wait_ge(p.sem["dks"], 16 * (ph - 1))
                        gp.tensor_copy(keys16[:, par * 1024:(par + 1) * 1024],
                                       wqf[:]).then_inc(p.inc("gks"), 1)
                        gp.dma_start(ks16[:, 4 * ph:4 * ph + 4, :],
                                     keys16[:, par * 1024:(par + 1) * 1024]
                                     .rearrange("q (a k) -> q a k", a=4)
                                     ).then_inc(p.inc("dks", 16), 16)
                    if ph >= 2:
                        gp.wait_ge(p.sem["pe2"], pe0 + (ph - 1) * 16)
                    elif tq >= 1:
                        gp.wait_ge(p.sem["pe2"], pe0 - 24 + 16 * ph)
                    for qp in range(4):
                        wq_cast_unit(gp, G, qp)
                    if tq == 0 and 10 <= ph and ph - 10 < len(WDK_PIECES):
                        src, nch, col = WDK_PIECES[ph - 10]
                        gp.dma_start(wqf[:, 0:512].rearrange(
                            "q (a k) -> q a k", a=nch), src).then_inc(
                                p.inc("dpc", 16), 16)
                        gp.wait_ge(p.sem["dpc"], p.cnt["dpc"])
                        gp.tensor_copy(wdk16[:, col:col + 512],
                                       wqf[:, 0:512]).then_inc(
                                           p.inc("ge2"), 1)

            @block.tensor
            def _(tensor, tq=tq, t0=t0, dk0=dk0, se0=se0, ses0=ses0):
                for ph in range(16):
                    par = ph % 2
                    G = 16 * tq + ph
                    if ph == 0:
                        tensor.wait_ge(p.sem["se1"], 8 * (tq + 1))
                    tensor.wait_ge(p.sem["gwq"], gwq_b + 4 * (G + 1))
                    if tq == 0:
                        tensor.wait_ge(p.sem["gks"], ph + 1)
                    else:
                        tensor.wait_ge(p.sem["dkk"], dk0 + 16 * (ph + 1))
                    for dkc in range(4):
                        for tc in range(2):
                            gi = ph * 8 + dkc * 2 + tc
                            if gi >= 2:
                                tensor.wait_ge(p.sem["se2"], se0 + gi - 1)
                            elif se0 + gi - 9 > 0:
                                tensor.wait_ge(p.sem["se2"], se0 + gi - 9)
                            for cc in range(8):
                                mm = tensor.matmul(
                                    pb[gi % 2][:],
                                    wq16[:, par * 4096 + cc * 512 + dkc * 128:
                                         par * 4096 + cc * 512 + (dkc + 1) * 128],
                                    xnT[:, cc * T + t0 + tc * 512:
                                        cc * T + t0 + (tc + 1) * 512],
                                    start=(cc == 0), stop=(cc == 7))
                            mm.then_inc(p.inc("pe2"), 1)
                    for tt in range(8):
                        si = ph * 8 + tt
                        tensor.wait_ge(p.sem["se2"], se0 + ph * 8 + 7 + tt // 4)
                        if si >= 2:
                            tensor.wait_ge(p.sem["se2s"], ses0 + si - 1)
                        elif ses0 + si - 1 > 0:
                            tensor.wait_ge(p.sem["se2s"], ses0 + si - 1)
                        for dkc in range(4):
                            mm = tensor.matmul(
                                pb[2 + si % 2][:, 0:256],
                                qph[:, dkc * 1024 + tt * 128:dkc * 1024 + (tt + 1) * 128],
                                keys16[:, par * 1024 + dkc * 256:
                                       par * 1024 + (dkc + 1) * 256],
                                start=(dkc == 0), stop=(dkc == 3))
                        mm.then_inc(p.inc("pe2"), 1)

            @block.scalar
            def _(scalar, pe0=pe0, vem0=vem0, prevB=prevB):
                for ph in range(16):
                    for gi8 in range(8):
                        dkc, tc = gi8 // 2, gi8 % 2
                        g = ph * 8 + gi8
                        scalar.wait_ge(p.sem["pe2"], pe0 + ph * 16 + gi8 + 1)
                        scalar.activation(
                            qph[:, dkc * 1024 + tc * 512:dkc * 1024 + (tc + 1) * 512],
                            pb[g % 2][:], AF.Copy).then_inc(p.inc("se2"), 1)
                    for tt in range(8):
                        si = ph * 8 + tt
                        scalar.wait_ge(p.sem["pe2"], pe0 + ph * 16 + 8 + tt + 1)
                        if si >= 8:
                            scalar.wait_ge(p.sem["ve2m"], vem0 + si - 7)
                        scalar.activation(
                            simf[:, (si % 8) * 256:(si % 8 + 1) * 256],
                            pb[2 + si % 2][:, 0:256],
                            AF.Copy).then_inc(p.inc("se2s"), 1)
                    if prevB is not None and ph % 2 == 1:
                        p3_scalar_a(scalar, ph // 2, prevB)
                    if tq >= 1 and ph < 10:
                        idx = 10 * (tq - 1) + ph
                        if idx < len(PIECES):
                            dcol = PIECES[idx][3]
                            scalar.wait_ge(p.sem["dpc"],
                                           16 * len(WDK_PIECES)
                                           + 16 * (idx + 1))
                            if idx >= 1:
                                scalar.wait_ge(p.sem["dps"], 16 * idx)
                            scalar.activation(f16st[:], wqf[:],
                                              AF.Copy).then_inc(
                                                  p.inc("gpc"), 1)
                            scalar.wait_ge(p.sem["gpc"], p.cnt["gpc"])
                            scalar.dma_start(scr16[:, dcol:dcol + 1024],
                                             f16st[:]).then_inc(
                                                 p.inc("dps", 16), 16)

            @block.vector
            def _(vector, tq=tq, ses0=ses0, prevB=prevB):
                sx = sxq[tq % 2]
                for ph in range(16):
                    pp, hh = ph // 8, ph % 8
                    for tt in range(8):
                        si = ph * 8 + tt
                        col = tt * 256 + pp * 128 + hh * 16
                        vector.wait_ge(p.sem["se2s"], ses0 + si + 1)
                        vector.max(out=sx[:, col:col + 8],
                                   in_=simf[:, (si % 8) * 256:(si % 8 + 1) * 256]
                                   ).then_inc(p.inc("ve2k"), 1)
                        vector.wait_ge(p.sem["ve2k"], p.cnt["ve2k"])
                        vector.match_replace(
                            out=mrscr[:], in_to_replace=sx[:, col:col + 8],
                            in_values=simf[:, (si % 8) * 256:(si % 8 + 1) * 256],
                            imm_value=-1e30).then_inc(p.inc("ve2m"), 1)
                        vector.max(out=sx[:, col + 8:col + 16],
                                   in_=mrscr[:]).then_inc(p.inc("ve2k"), 1)
                    if prevB is not None and ph % 2 == 1:
                        p3_vector_tile(vector, ph // 2, prevB, sxq[(tq + 1) % 2])

            if prevB is not None:
                p3_tail(block, prev_tq, prevB)

            peh0 = p.cnt["pe2"]
            seh0 = p.cnt["se2"]

            @block.tensor
            def _(tensor, t0=t0, peh0=peh0, seh0=seh0, ge2f=p.cnt["ge2"]):
                for tt in range(8):
                    if tt == 0:
                        tensor.wait_ge(p.sem["ge2"], ge2f)
                    if tt >= 2:
                        tensor.wait_ge(p.sem["se2"], seh0 + tt - 1)
                    elif seh0 - 129 + tt > 0:
                        tensor.wait_ge(p.sem["se2"], seh0 - 129 + tt)
                    for cc in range(8):
                        mm = tensor.matmul(
                            pb[4 + tt % 2][:, 0:256],
                            xnT[:, cc * T + t0 + tt * 128:cc * T + t0 + (tt + 1) * 128],
                            wdk16[:, cc * 256:(cc + 1) * 256],
                            start=(cc == 0), stop=(cc == 7))
                    mm.then_inc(p.inc("pe2"), 1)

            @block.scalar
            def _(scalar, peh0=peh0):
                for tt in range(8):
                    scalar.wait_ge(p.sem["pe2"], peh0 + tt + 1)
                    scalar.activation(hfull[:, tt * 256:(tt + 1) * 256],
                                      pb[4 + tt % 2][:, 0:256],
                                      AF.Copy).then_inc(p.inc("se2"), 1)

            prevB = p3_bases()
            prev_tq = tq
            if stage < 3:
                prevB = None

        if prevB is not None and stage < 4:
            @block.vector
            def _(vector, prevB=prevB):
                for tt in range(8):
                    p3_vector_tile(vector, tt, prevB, sxq[1])

            @block.scalar
            def _(scalar, prevB=prevB):
                for tt in range(8):
                    p3_scalar_a(scalar, tt, prevB)

            p3_tail(block, 3, prevB)
            es2.close()
        elif prevB is not None:
            B3 = prevB
            fin = {n: p.cnt[n] for n in ("se2", "se2s", "pe2", "ve2k", "dps")}
            es2.close()
            with (
                nc.sbuf_tensor("wdkv16", [128, 4096], F16,
                               side="right") as wdkv16,
                nc.sbuf_tensor("wdqT16", [128, 4096], F16,
                               side="right") as wdqT16,
                nc.sbuf_tensor("wuqh16", [128, 2048], F16,
                               side="right") as wuqh16,
                nc.sbuf_tensor("t1tmp", [128, 2 * 2048], F16,
                               side="right") as t1tmp,
            ):
                dxb = (p.cnt["dx0"], p.cnt["dx1"])

                @block.sync
                def _(sync):
                    sync.wait_ge(p.sem["dps"], fin["dps"])
                    sync.wait_ge(p.sem["pe2"], fin["pe2"])
                    sync.wait_ge(p.sem["ve2k"], fin["ve2k"])
                    for i, (dst, lo) in enumerate((
                            (wdkv16[:, 0:2048], 0),
                            (wdkv16[:, 2048:4096], 2048),
                            (wdqT16[:, 0:2048], 4096),
                            (wdqT16[:, 2048:4096], 6144),
                            (wuqh16[:, 0:2048], 8192))):
                        sync.dma_start(dst, scr16[:, lo:lo + 2048]).then_inc(
                            p.inc("dx%d" % (i % 2), 16), 16)

                mspec = []

                def add_ckv(tc, lc):
                    w = []
                    if not mspec:
                        w = [("dx0", dxb[0] + 16), ("dx1", dxb[1] + 16)]
                    if len(mspec) < 4:
                        w = w + [("se2", fin["se2"]), ("se2s", fin["se2s"])]
                    mspec.append((
                        [(wdkv16, cc * 512 + lc * 128, xnT,
                          cc * T + tc * 512) for cc in range(8)],
                        ckvT, lc * T + tc * 512, w))

                def add_t1(tc, qlc):
                    w = []
                    if tc == 0 and qlc == 0:
                        w = [("dx0", dxb[0] + 32), ("dx1", dxb[1] + 32)]
                    mspec.append((
                        [(wdqT16, cc * 512 + qlc * 128, xnT,
                          cc * T + tc * 512) for cc in range(8)],
                        t1tmp, (tc % 2) * 2048 + qlc * 512, w))

                def add_qatt(tc, rc):
                    w = []
                    if tc == 0 and rc == 0:
                        w = [("dx0", dxb[0] + 48)]
                    mspec.append((
                        [(wuqh16, qlc * 512 + rc * 128, t1tmp,
                          (tc % 2) * 2048 + qlc * 512) for qlc in range(4)],
                        big32, rc * T + tc * 512, w))

                for tc in range(8):
                    for lc in range(4):
                        add_ckv(tc, lc)
                for tc in range(2):
                    for qlc in range(4):
                        add_t1(tc, qlc)
                for tc in range(8):
                    for rc in range(4):
                        add_qatt(tc, rc)
                    if tc + 2 < 8:
                        for qlc in range(4):
                            add_t1(tc + 2, qlc)
                pe_hooks = {22: [(p3t_pe_t, 0)], 26: [(p3t_pe_f, 0)],
                            36: [(p3t_pe_t, 1)], 38: [(p3t_pe_f, 1)],
                            40: [(f, tt) for tt in range(2, 8)
                                 for f in (p3t_pe_t, p3t_pe_f)]}
                sc_hooks = {7: [(p3_scalar_a, 0)], 20: [(p3_scalar_a, 1)],
                            24: [(p3t_sc_at2, 0)],
                            26: [(p3t_sc_silu, 0)],
                            32: [(p3_scalar_a, 2)],
                            36: [(p3_scalar_a, 3)],
                            38: [(p3t_sc_at2, 1)],
                            40: ([(p3_scalar_a, tt) for tt in range(4, 8)] +
                                 [(p3t_sc_silu, 1)] +
                                 [(f, tt) for tt in range(2, 8)
                                  for f in (p3t_sc_at2, p3t_sc_silu)])}
                pe40 = p.cnt["pe4"]
                se40 = p.cnt["se4"]

                @block.tensor
                def _(tensor):
                    for gi, (mms, _, _c, waits) in enumerate(mspec):
                        for (fn, tt) in pe_hooks.get(gi, ()):
                            fn(tensor, tt, B3)
                        for (sname, val) in waits:
                            tensor.wait_ge(p.sem[sname], val)
                        if gi >= 4:
                            tensor.wait_ge(p.sem["se4"], se40 + gi - 3)
                        for mi, (lt, lcol, rt, rcol) in enumerate(mms):
                            m = tensor.matmul(
                                pb[gi % 4][:], lt[:, lcol:lcol + 128],
                                rt[:, rcol:rcol + 512], start=(mi == 0),
                                stop=(mi == len(mms) - 1))
                        m.then_inc(p.inc("pe4"), 1)

                @block.scalar
                def _(scalar):
                    for gi, (_, dt_, dcol, _w) in enumerate(mspec):
                        for (fn, tt) in sc_hooks.get(gi, ()):
                            fn(scalar, tt, B3)
                        scalar.wait_ge(p.sem["pe4"], pe40 + gi + 1)
                        scalar.activation(dt_[:, dcol:dcol + 512],
                                          pb[gi % 4][:],
                                          AF.Copy).then_inc(p.inc("se4"), 1)

                @block.vector
                def _(vector):
                    for tt in range(8):
                        p3_vector_tile(vector, tt, B3, sxq[1])

                @block.gpsimd
                def _(gp):
                    for tt in range(8):
                        gp.wait_ge(p.sem["ae3s"], B3["ae3s"] + tt + 1)
                        gp.dma_start(
                            g_t[3 * 8 + tt],
                            gt16[:, (tt % 2) * 512:(tt % 2 + 1) * 512]
                            ).then_inc(p.inc("dg%d" % (tt % 2), 16), 16)
        else:
            es2.close()
        p.barrier(block)



def _mm_groups(nc, block, p, glist, evacs, pb):
    pe0 = p.cnt["pe4"]
    se0 = p.cnt["se4"]

    @block.tensor
    def _(tensor):
        for gi, mms in enumerate(glist):
            for (sem, val) in mms[0][5]:
                tensor.wait_ge(p.sem[sem], val)
            if gi >= 4:
                tensor.wait_ge(p.sem["se4"], se0 + gi - 3)
            for mi, (lt, lcol, rt, rcol, w, _) in enumerate(mms):
                mm = tensor.matmul(pb[gi % 4][:, 0:w],
                                   lt[:, lcol:lcol + 128],
                                   rt[:, rcol:rcol + w],
                                   start=(mi == 0), stop=(mi == len(mms) - 1))
            mm.then_inc(p.inc("pe4"), 1)

    @block.scalar
    def _(scalar):
        for gi, (dt_, dcol, w) in enumerate(evacs):
            scalar.wait_ge(p.sem["pe4"], pe0 + gi + 1)
            scalar.activation(dt_[:, dcol:dcol + w], pb[gi % 4][:, 0:w],
                              AF.Copy).then_inc(p.inc("se4"), 1)


def emit_p4ab(nc, block, p, scr16, keff, veff, pb):
    with (
        nc.sbuf_tensor("wuqA16", [128, 4096], F16, side="right") as wuqA16,
        nc.sbuf_tensor("wdqh16", [128, 2048], F16, side="right") as wdqh16,
        nc.sbuf_tensor("wuk16", [128, 4096], F16, side="right") as wuk16,
        nc.sbuf_tensor("at16", [128, 4096], F16, side="right") as at16,
        nc.sbuf_tensor("wuv16", [128, 4096], F16, side="right") as wuv16,
        nc.sbuf_tensor("woTh16", [128, 4096], F16, side="right") as woTh16,
    ):
        pe4f = p.cnt["pe4"]
        se4f = p.cnt["se4"]
        qn = ("dx0", "dx1", "dwq0", "dwq1")
        loads = [(wuqA16[:, 0:2048], 10240), (wuqA16[:, 2048:4096], 12288),
                 (wdqh16[:, 0:2048], 14336),
                 (wuk16[:, 0:2048], 16384), (wuk16[:, 2048:4096], 18432),
                 (wuv16[:, 0:2048], 20480), (wuv16[:, 2048:4096], 22528),
                 (woTh16[:, 0:2048], 24576), (woTh16[:, 2048:4096], 26624)]
        dcnt = []

        @block.sync
        def _(sync):
            sync.wait_ge(p.sem["pe4"], pe4f)
            for i, (dst, lo) in enumerate(loads):
                nm = qn[i % 4]
                sync.dma_start(dst, scr16[:, lo:lo + 2048]).then_inc(
                    p.inc(nm, 16), 16)
                dcnt.append((nm, p.cnt[nm]))

        gwait = [dcnt[0], dcnt[1], dcnt[2], ("se4", se4f)]
        glist = []
        evacs = []
        for cpc in range(8):
            glist.append([(wuqA16, qlc * 1024 + cpc * 128, wdqh16,
                           qlc * 512, 512, gwait) for qlc in range(4)])
            evacs.append((at16, cpc * 512, 512))
        se_at = p.cnt["se4"] + 8
        kwait = [("se4", se_at), dcnt[3], dcnt[4]]
        for rc in range(4):
            glist.append([(at16, cpc * 512 + rc * 128, wuk16, cpc * 512,
                           512, kwait) for cpc in range(8)])
            evacs.append((keff, rc * 512, 512))
        vwait = [dcnt[5], dcnt[6], dcnt[7], dcnt[8]]
        for vlc in range(4):
            glist.append([(wuv16, cc * 512 + vlc * 128, woTh16, cc * 512,
                           512, vwait) for cc in range(8)])
            evacs.append((veff, vlc * 512, 512))
        _mm_groups(nc, block, p, glist, evacs, pb)


def emit_p4b(nc, block, p, ckvT, veff, u, pb):
    pe0 = p.cnt["pe4"]
    se0 = p.cnt["se4"]

    @block.tensor
    def _(tensor):
        for kc in range(32):
            if kc >= 4:
                tensor.wait_ge(p.sem["se4"], se0 + kc - 3)
            for lc in range(4):
                mm = tensor.matmul(pb[kc % 4][:],
                                   ckvT[:, lc * T + kc * 128:lc * T + (kc + 1) * 128],
                                   veff[:, lc * 512:(lc + 1) * 512],
                                   start=(lc == 0), stop=(lc == 3))
            mm.then_inc(p.inc("pe4"), 1)

    @block.scalar
    def _(scalar):
        for kc in range(32):
            scalar.wait_ge(p.sem["pe4"], pe0 + kc + 1)
            scalar.activation(
                u[:, kc * 520:kc * 520 + 520].rearrange(
                    "q (h w) -> q h w", h=8)[:, :, 0:64],
                pb[kc % 4][:].rearrange("q (h w) -> q h w", h=8),
                AF.Copy).then_inc(p.inc("se4"), 1)

    @block.vector
    def _(vector):
        vector.memset(
            u[:].rearrange("q (k h w) -> q k h w", k=32, h=8)[:, :, :, 64:65],
            1.0)



def emit_p5(nc, block, p, ckvT, qattT, keff, u, qlat, sT, masks, cst, pb,
            pbh, ident, xh_d, g_scr, out_d, do_p6=True):
    with (
        nc.sbuf_tensor("ptil", [128, 4 * 512], F16, side="right") as ptil,
        nc.sbuf_tensor("rcp", [1, 2 * 512], F16, side="right") as rcp,
        nc.sbuf_tensor("lnd", [1, 512], F32, side="right") as lnd,
        nc.sbuf_tensor("bcs", [64, 2 * 512], F16, side="right") as bcs,
        nc.sbuf_tensor("stile", [128, 2 * 512], F16, side="right") as stile,
        nc.sbuf_tensor("gtile", [128, 2 * 512], F16, side="right") as gtile,
        nc.sbuf_tensor("xtile", [128, 2 * 512], F32, side="right") as xtile,
        nc.sbuf_tensor("otile", [128, 2 * 512], F32, side="right") as otile,
    ):
        g_t = g_scr.ap().rearrange("(n q) c -> n q c", q=128)
        xh_t = xh_d.rearrange("(n q) c -> n q c", q=128)
        o_t = out_d.rearrange("(n q) c -> n q c", q=128)
        S0, S1, Y0, Y1, D0, D1, QL, BC = range(8)
        SB = [S0, S1, D0, D1]
        grans = []
        g = 0
        for hh in range(P5_HEADS):
            for qb in range(P5_QB):
                hq = hh * 8 + qb
                nkt = 4 * qb + 4
                for kt in range(nkt):
                    grans.append((hh, qb, kt, g, hq, kt == 0, kt == nkt - 1))
                    g += 1
        NG = g
        GSTART = [hh * 144 for hh in range(9)]
        pe60 = p.cnt["pe6"]
        ve60 = p.cnt["ve6"]
        se60 = p.cnt["se6"]
        dx60 = (p.cnt["dxh0"], p.cnt["dxh1"])
        dg60 = (p.cnt["dg0"], p.cnt["dg1"])
        do60 = (p.cnt["dout0"], p.cnt["dout1"])
        ae0 = p.cnt["ae5"]
        pt0 = p.cnt["pt"]
        ps0 = p.cnt["pe5s"]
        py0 = p.cnt["pe5y"]
        pd0 = p.cnt["pe5d"]
        vr0 = p.cnt["ve5r"]
        pb0 = p.cnt["pe5b"]
        dv0 = p.cnt["dvt"]
        vq0 = p.cnt["ve5q"]
        ones_col = masks[:, 4 * 512:4 * 512 + 1]
        ones_row64 = masks[0:1, 4 * 512:4 * 512 + 64]

        def p6_pair(tensor, k):
            if k >= 1:
                tensor.wait_ge(p.sem["se6"], se60 + 2 * k)
            for i in (2 * k, 2 * k + 1):
                for c4 in range(4):
                    ti = tensor.transpose(
                        pbh[QL][:, ((i % 2) * 4 + c4) * 128:
                                ((i % 2) * 4 + c4 + 1) * 128],
                        sT[:, c4 * T + i * 128:c4 * T + (i + 1) * 128],
                        ident[:])
                ti.then_inc(p.inc("pe6"), 1)

        def emit_bcast(tensor, pend, at_head=False):
            tensor.wait_ge(p.sem["ve5r"], vr0 + pend + 1)
            if pend >= 1:
                tensor.wait_ge(p.sem["dvt"], dv0 + pend)
            if at_head:
                tensor.wait_ge(p.sem["ve5q"], vq0 + ((pend + 1) // 8 + 1) * 32)
            tensor.matmul(pb[BC][0:64, :], ones_row64,
                          rcp[0:1, (pend % 2) * 512:(pend % 2 + 1) * 512],
                          start=True, stop=True).then_inc(p.inc("pe5b"), 1)

        @block.tensor
        def _(tensor):
            pending = []

            def emit_yd(tensor, pg):
                (hh, qb, kt, g, hq, first, last) = pg
                tensor.wait_ge(p.sem["pt"], pt0 + g + 1)
                tensor.matmul(pb[Y0 + hq % 2][0:65, :],
                              u[:, kt * 520 + hh * 65:kt * 520 + (hh + 1) * 65],
                              ptil[:, (g % 4) * 512:(g % 4 + 1) * 512],
                              start=first, stop=last).then_inc(p.inc("pe5y"), 1)

            for gran in grans:
                (hh, qb, kt, g, hq, first, last) = gran
                if first:
                    if qb == 0:
                        off = (hh % 2) * 64
                        rc = hh // 2
                        for lc in range(4):
                            for qc in range(8):
                                qi = hh * 32 + lc * 8 + qc
                                if qi >= 2:
                                    tensor.wait_ge(p.sem["ve5q"], vq0 + qi - 1)
                                if qi % 2 == 1 and hh >= 1 and lc == 0 and qc == 1:
                                    tensor.wait_ge(p.sem["dvt"],
                                                   dv0 + hh * 8 - 1)
                                tensor.matmul(
                                    pb[QL if qi % 2 == 0 else BC][:],
                                    keff[off:off + 64,
                                         rc * 512 + lc * 128:rc * 512 + (lc + 1) * 128],
                                    qattT[off:off + 64,
                                          rc * T + qc * 512:rc * T + (qc + 1) * 512],
                                    start=True, stop=True).then_inc(p.inc("pe5s"), 1)
                                if qc in (0, 1) and lc == 0 and pending:
                                    emit_yd(tensor, pending.pop(0))
                    tensor.wait_ge(p.sem["ve5q"], vq0 + hh * 32 + 25 + qb)
                    if hq >= 2:
                        tensor.wait_ge(p.sem["dvt"], dv0 + hq - 1)
                if hq >= 1 and kt == 2:
                    emit_bcast(tensor, hq - 1, at_head=(qb == 0))
                if do_p6 and hh == 7 and qb >= 1 and kt == 4:
                    tensor.wait_ge(p.sem["dvt"], dv0 + hq)
                    if qb == 1:
                        tensor.wait_ge(p.sem["ve5q"], vq0 + 256)
                    p6_pair(tensor, 2 * (qb - 1))
                if do_p6 and hh == 7 and qb >= 1 and kt == 6:
                    p6_pair(tensor, 2 * (qb - 1) + 1)
                if g >= 4:
                    tensor.wait_ge(p.sem["ae5"], ae0 + g - 3)
                for lc in range(4):
                    mm = tensor.matmul(
                        pb[SB[g % 4]][:],
                        ckvT[:, lc * T + kt * 128:lc * T + (kt + 1) * 128],
                        qlat[:, lc * T + qb * 512:lc * T + (qb + 1) * 512],
                        start=(lc == 0), stop=(lc == 3))
                mm.then_inc(p.inc("pe5s"), 1)
                pending.append(gran)
                if len(pending) > 2:
                    emit_yd(tensor, pending.pop(0))
            while pending:
                emit_yd(tensor, pending.pop(0))
            emit_bcast(tensor, 63)
            if do_p6:
                tensor.wait_ge(p.sem["dvt"], dv0 + 64)
                p6_pair(tensor, 14)
                p6_pair(tensor, 15)

        @block.scalar
        def _(scalar):
            def p6_evac(i):
                par = i % 2
                scalar.wait_ge(p.sem["pe6"], pe60 + 2 * (i // 2) + 2)
                if i >= 2:
                    scalar.wait_ge(p.sem["ve6"], ve60 + i - 1)
                scalar.activation(stile[:, par * 512:(par + 1) * 512],
                                  pbh[QL][:, par * 512:(par + 1) * 512],
                                  AF.Copy).then_inc(p.inc("se6"), 1)

            def tail_recip(hp, glast):
                par = hp % 2
                scalar.wait_ge(p.sem["pe5y"], py0 + glast + 1)
                scalar.activation(lnd[0:1, :], pb[Y0 + par][64:65, :], AF.Ln)
                scalar.activation(rcp[0:1, par * 512:(par + 1) * 512],
                                  lnd[0:1, :], AF.Exp,
                                  scale=-1.0).then_inc(p.inc("ve5r"), 1)

            for (hh, qb, kt, g, hq, first, last) in grans:
                if hq >= 1 and kt == 1:
                    tail_recip(hq - 1, g - 2)
                if do_p6 and hh == 7 and qb >= 1 and kt == 4:
                    for i6 in (4 * (qb - 1), 4 * (qb - 1) + 1):
                        p6_evac(i6)
                if do_p6 and hh == 7 and qb >= 1 and kt == 6:
                    for i6 in (4 * (qb - 1) + 2, 4 * (qb - 1) + 3):
                        p6_evac(i6)
                scalar.wait_ge(p.sem["pe5s"], ps0 + 32 * (hh + 1) + g + 1)
                if g >= 4:
                    scalar.wait_ge(p.sem["pe5y"], py0 + g - 3)
                scalar.activation(ptil[:, (g % 4) * 512:(g % 4 + 1) * 512],
                                  pb[SB[g % 4]][:], AF.Exp, bias=cst[:, 0:1],
                                  scale=0.125).then_inc(p.inc("ae5"), 1)
            tail_recip(63, NG - 1)
            if do_p6:
                for i6 in range(28, 32):
                    p6_evac(i6)

        if do_p6:
            NT6 = T // 128

            @block.gpsimd
            def _(gp):
                for i in range(NT6):
                    par = i % 2
                    if i >= 2:
                        gp.wait_ge(p.sem["ve6"], ve60 + i - 1)
                    gp.dma_start(gtile[:, par * 512:(par + 1) * 512],
                                 g_t[i]).then_inc(p.inc("dg%d" % par, 16), 16)
                    gp.dma_start(xtile[:, par * 512:(par + 1) * 512],
                                 xh_t[i]).then_inc(p.inc("dxh%d" % par, 16), 16)
                gp.wait_ge(p.sem["dout0"], do60[0] + 16 * (NT6 // 2))
                gp.wait_ge(p.sem["dout1"], do60[1] + 16 * (NT6 // 2))

            @block.sync
            def _(sync):
                for i in range(NT6):
                    par = i % 2
                    sync.wait_ge(p.sem["ve6"], ve60 + i + 1)
                    sync.dma_start(o_t[i], otile[:, par * 512:(par + 1) * 512]
                                   ).then_inc(p.inc("dout%d" % par, 16), 16)

        @block.vector
        def _(vector):
            def p6_combine(i):
                par = i % 2
                vector.wait_ge(p.sem["se6"], se60 + i + 1)
                vector.wait_ge(p.sem["dg%d" % par],
                               dg60[par] + 16 * (i // 2 + 1))
                vector.wait_ge(p.sem["dxh%d" % par],
                               dx60[par] + 16 * (i // 2 + 1))
                if i >= 2:
                    vector.wait_ge(p.sem["dout%d" % par],
                                   do60[par] + 16 * ((i - 2) // 2 + 1))
                vector.tensor_tensor(otile[:, par * 512:(par + 1) * 512],
                                     stile[:, par * 512:(par + 1) * 512],
                                     gtile[:, par * 512:(par + 1) * 512],
                                     op=ALU.mult)
                vector.tensor_add(otile[:, par * 512:(par + 1) * 512],
                                  otile[:, par * 512:(par + 1) * 512],
                                  xtile[:, par * 512:(par + 1) * 512]
                                  ).then_inc(p.inc("ve6"), 1)

            def tail_stt(hp):
                par = hp % 2
                hhp, qbp = hp // 8, hp % 8
                vector.wait_ge(p.sem["pe5b"], pb0 + hp + 1)
                vector.tensor_copy(bcs[:, par * 512:(par + 1) * 512],
                                   pb[BC][0:64, :])
                off = (hhp % 2) * 64
                vector.scalar_tensor_tensor(
                    sT[off:off + 64,
                       (hhp // 2) * T + qbp * 512:(hhp // 2) * T + (qbp + 1) * 512],
                    pb[Y0 + par][0:64, :], 1.0,
                    bcs[:, par * 512:(par + 1) * 512],
                    op0=ALU.mult, op1=ALU.mult).then_inc(p.inc("dvt"), 1)

            for (hh, qb, kt, g, hq, first, last) in grans:
                if qb == 0 and first:
                    for lc in range(4):
                        for qc in range(8):
                            qi = hh * 32 + lc * 8 + qc
                            vector.wait_ge(p.sem["pe5s"],
                                           ps0 + GSTART[hh] + qi + 1)
                            vector.tensor_copy(
                                qlat[:, lc * T + qc * 512:lc * T + (qc + 1) * 512],
                                pb[QL if qi % 2 == 0 else BC][:]
                                ).then_inc(p.inc("ve5q"), 1)
                if hq >= 1 and kt == 2:
                    tail_stt(hq - 1)
                vector.wait_ge(p.sem["ae5"], ae0 + g + 1)
                midx = kt - 4 * qb if kt >= 4 * qb else 4
                vector.tensor_tensor(ptil[:, (g % 4) * 512:(g % 4 + 1) * 512],
                                     ptil[:, (g % 4) * 512:(g % 4 + 1) * 512],
                                     masks[:, midx * 512:(midx + 1) * 512],
                                     op=ALU.mult).then_inc(p.inc("pt"), 1)
                if do_p6 and hh == 7 and qb >= 1 and kt == 5:
                    for i6 in (4 * (qb - 1), 4 * (qb - 1) + 1):
                        p6_combine(i6)
                if do_p6 and hh == 7 and qb >= 1 and kt == 7:
                    for i6 in (4 * (qb - 1) + 2, 4 * (qb - 1) + 3):
                        p6_combine(i6)
            tail_stt(63)
            if do_p6:
                for i6 in range(28, 32):
                    p6_combine(i6)



def emit_p6(nc, block, p, sT, xh_d, g_scr, out_d, ident, pbh):
    with (
        nc.sbuf_tensor("stile", [128, 2 * 512], F16, side="right") as stile,
        nc.sbuf_tensor("gtile", [128, 2 * 512], F16, side="right") as gtile,
        nc.sbuf_tensor("xtile", [128, 2 * 512], F32, side="right") as xtile,
        nc.sbuf_tensor("otile", [128, 2 * 512], F32, side="right") as otile,
    ):
        g_t = g_scr.ap().rearrange("(n q) c -> n q c", q=128)
        xh_t = xh_d.rearrange("(n q) c -> n q c", q=128)
        o_t = out_d.rearrange("(n q) c -> n q c", q=128)
        NT = T // 128
        pe0 = p.cnt["pe6"]
        ve0 = p.cnt["ve6"]
        se0 = p.cnt["se6"]
        dx0 = (p.cnt["dxh0"], p.cnt["dxh1"])
        dg0 = (p.cnt["dg0"], p.cnt["dg1"])
        do0 = (p.cnt["dout0"], p.cnt["dout1"])

        @block.gpsimd
        def _(gp):
            for i in range(NT):
                par = i % 2
                if i >= 2:
                    gp.wait_ge(p.sem["ve6"], ve0 + i - 1)
                gp.dma_start(gtile[:, par * 512:(par + 1) * 512],
                             g_t[i]).then_inc(p.inc("dg%d" % par, 16), 16)
                gp.dma_start(xtile[:, par * 512:(par + 1) * 512],
                             xh_t[i]).then_inc(p.inc("dxh%d" % par, 16), 16)
            gp.wait_ge(p.sem["dout0"], do0[0] + 16 * (NT // 2))
            gp.wait_ge(p.sem["dout1"], do0[1] + 16 * (NT // 2))

        @block.tensor
        def _(tensor):
            for i in range(NT):
                par = i % 2
                if i >= 2:
                    tensor.wait_ge(p.sem["se6"], se0 + i - 1)
                for c4 in range(4):
                    ti = tensor.transpose(
                        pbh[6 + par][:, c4 * 128:(c4 + 1) * 128],
                        sT[:, c4 * T + i * 128:c4 * T + (i + 1) * 128],
                        ident[:])
                ti.then_inc(p.inc("pe6"), 1)

        @block.scalar
        def _(scalar):
            for i in range(NT):
                par = i % 2
                scalar.wait_ge(p.sem["pe6"], pe0 + i + 1)
                if i >= 2:
                    scalar.wait_ge(p.sem["ve6"], ve0 + i - 1)
                scalar.activation(stile[:, par * 512:(par + 1) * 512],
                                  pbh[6 + par][:, 0:512],
                                  AF.Copy).then_inc(p.inc("se6"), 1)

        @block.vector
        def _(vector):
            for i in range(NT):
                par = i % 2
                vector.wait_ge(p.sem["se6"], se0 + i + 1)
                vector.wait_ge(p.sem["dg%d" % par], dg0[par] + 16 * (i // 2 + 1))
                vector.wait_ge(p.sem["dxh%d" % par], dx0[par] + 16 * (i // 2 + 1))
                if i >= 2:
                    vector.wait_ge(p.sem["dout%d" % par],
                                   do0[par] + 16 * ((i - 2) // 2 + 1))
                vector.tensor_tensor(otile[:, par * 512:(par + 1) * 512],
                                     stile[:, par * 512:(par + 1) * 512],
                                     gtile[:, par * 512:(par + 1) * 512],
                                     op=ALU.mult)
                vector.tensor_add(otile[:, par * 512:(par + 1) * 512],
                                  otile[:, par * 512:(par + 1) * 512],
                                  xtile[:, par * 512:(par + 1) * 512]
                                  ).then_inc(p.inc("ve6"), 1)

        @block.sync
        def _(sync):
            for i in range(NT):
                par = i % 2
                sync.wait_ge(p.sem["ve6"], ve0 + i + 1)
                sync.dma_start(o_t[i], otile[:, par * 512:(par + 1) * 512]
                               ).then_inc(p.inc("dout%d" % par, 16), 16)



def _host_inputs(ins):
    x = np.asarray(ins["L0_attn_x"], dtype=np.float32)
    W_dq = np.asarray(ins["L0_attn_W_dq"], dtype=np.float32)
    W_uq = np.asarray(ins["L0_attn_W_uq"], dtype=np.float32)
    W_dkv = np.asarray(ins["L0_attn_W_dkv"], dtype=np.float32)
    W_uk = np.asarray(ins["L0_attn_W_uk"], dtype=np.float32)
    W_uv = np.asarray(ins["L0_attn_W_uv"], dtype=np.float32)
    W_o = np.asarray(ins["L0_attn_W_o"], dtype=np.float32)
    Wq = np.asarray(ins["L0_ffn_Wq"], dtype=np.float32)
    keys = np.asarray(ins["L0_ffn_keys"], dtype=np.float32)
    W_down = np.asarray(ins["L0_ffn_W_down"], dtype=np.float32)
    W_up = np.asarray(ins["L0_ffn_W_up"], dtype=np.float32)

    wdkvT = np.ascontiguousarray(W_dkv.T)
    wdqT = np.ascontiguousarray(W_dq.T)
    wuqT = np.ascontiguousarray(W_uq.T)
    woT = np.ascontiguousarray(W_o.T)
    wqT = np.ascontiguousarray(Wq.T)
    keysT = np.ascontiguousarray(keys.transpose(2, 0, 3, 1))
    wd256T = np.ascontiguousarray(W_down[:256].T)

    ident = np.eye(128, dtype=np.float16)
    masks = np.zeros((5, 128, 512), dtype=np.float16)
    kk = np.arange(128)[:, None]
    qq = np.arange(512)[None, :]
    for j in range(4):
        masks[j] = (128 * j + kk <= qq).astype(np.float16)
    masks[4] = 1.0
    cst = np.zeros((128, 8), dtype=np.float32)
    cst[:, 0] = -4.0
    cst[:, 1] = 1e-5

    in_maps = []
    for c in range(8):
        b, half = c // 2, c % 2
        ch0 = half * 512
        in_maps.append({
            "x": np.ascontiguousarray(x[b]),
            "xh": np.ascontiguousarray(x[b][:, ch0:ch0 + 512]),
            "wdkvT": wdkvT,
            "wdqT": wdqT,
            "wuqT": wuqT,
            "wuqTh": np.ascontiguousarray(wuqT[:, ch0:ch0 + 512]),
            "wdqh": np.ascontiguousarray(W_dq[:, ch0:ch0 + 512]),
            "wuk": np.ascontiguousarray(W_uk),
            "wuv": np.ascontiguousarray(W_uv),
            "woTh": np.ascontiguousarray(woT[:, ch0:ch0 + 512]),
            "wqT": wqT,
            "keysT": keysT,
            "wd256T": wd256T,
            "wup256h": np.ascontiguousarray(W_up[:256, ch0:ch0 + 512]),
            "ident": ident,
            "masks": masks,
            "cst": cst,
        })
    return in_maps


def kernel(**ins):
    in_maps = _host_inputs(ins)
    nc = build()
    res = run_bass_kernel_spmd(nc, in_maps, core_ids=list(range(8)))
    out = np.empty((4, T, C), dtype=np.float32)
    for c in range(8):
        b, half = c // 2, c % 2
        out[b, :, half * 512:(half + 1) * 512] = res.results[c]["out"]
    return out


# version 24
# speedup vs baseline: 1.1961x; 1.1961x over previous
import numpy as np
from contextlib import ExitStack

import concourse.bass as bass
import concourse.mybir as mybir
from concourse.bass_utils import run_bass_kernel_spmd

F32 = mybir.dt.float32
F16 = mybir.dt.float16
AF = mybir.ActivationFunctionType
ALU = mybir.AluOpType
AX = mybir.AxisListType.X

T = 4096
C = 1024
STAGE = 6
DEBUG = ()
P5_HEADS = 8
P5_QB = 8
P5_YD = True
P5_TAIL = 4


class P:
    def __init__(self, nc, es):
        self.nc = nc
        self.sem = {}
        self.cnt = {}
        self.bar_lvl = 0
        self.es = es

    def mksem(self, *names):
        for n in names:
            self.sem[n] = self.es.enter_context(self.nc.semaphore(n))
            self.cnt[n] = 0

    def inc(self, name, amt=1):
        self.cnt[name] += amt
        return self.sem[name]

    def barrier(self, block):
        self.bar_lvl += 5
        h, lvl = self.sem["bar"], self.bar_lvl
        for dec, eng in ((block.sync, 0), (block.tensor, 0), (block.vector, 1),
                         (block.scalar, 0), (block.gpsimd, 1)):
            @dec
            def _(e, h=h, lvl=lvl, eng=eng):
                if eng:
                    e.engine_nop().then_inc(h, 1)
                else:
                    e.nop().then_inc(h, 1)
                e.wait_ge(h, lvl)


def build(stage=None, debug=None):
    stage = STAGE if stage is None else stage
    debug = DEBUG if debug is None else debug
    nc = bass.Bass(trn_type="TRN2", target_bir_lowering=False)

    def din(name, shape, dtype=F32):
        return nc.dram_tensor(name, shape, dtype, kind="ExternalInput").ap()

    x_d = din("x", [T, C])
    xh_d = din("xh", [T, 512])
    wdkvT_d = din("wdkvT", [C, 512])
    wdqT_d = din("wdqT", [C, 512])
    wuqT_d = din("wuqT", [512, C])
    wuqTh_d = din("wuqTh", [512, 512])
    wdqh_d = din("wdqh", [512, 512])
    wuk_d = din("wuk", [C, 512])
    wuv_d = din("wuv", [C, 512])
    woTh_d = din("woTh", [C, 512])
    wqT_d = din("wqT", [C, 8192])
    keysT_d = din("keysT", [2, 8, 512, 256])
    wd256T_d = din("wd256T", [C, 256])
    wup256h_d = din("wup256h", [256, 512])
    ident_d = din("ident", [128, 128], F16)
    masks_d = din("masks", [5, 128, 512], F16)
    cst_d = din("cst", [128, 8])

    out_d = nc.dram_tensor("out", [T, 512], F32, kind="ExternalOutput").ap()
    g_scr = nc.dram_tensor("g_scr", [T, 512], F16,
                           kind="ExternalOutput" if "g_scr" in debug else "Internal")
    keys16_scr = nc.dram_tensor("keys16_scr", [2, 8, 512, 256], F16).ap()
    wq16_scr = nc.dram_tensor("wq16_scr", [C, 8192], F16).ap()
    scr16 = nc.dram_tensor("scr16", [128, 28 * 1024], F16,
                           kind="ExternalOutput" if "scr16" in debug
                           else "Internal").ap()

    es = ExitStack()
    with es:
        block = es.enter_context(nc.Block())
        p = P(nc, es)
        p.mksem("bar", "dx0", "dx1", "dcst", "dwq0", "dwq1", "dkk", "dg0", "dg1", "dxh0", "dxh1", "dout0", "dout1",
                "a1", "a2", "v2", "v3", "ac1", "pe1", "se1",
                "ge2", "pe2", "se2", "se2s", "ve2m", "ve2k",
                "ve3a", "ae3", "ve3t", "ve3r", "ae3g", "ve3m", "pe3t", "se3",
                "pe3f", "ae3s",
                "ge4", "pe4", "se4",
                "ae5", "pt", "pe5s", "pe5y", "pe5d", "ve5r", "pe5b", "dvt",
                "ve5q", "pe6", "ve6", "se6", "vch",
                "dpc", "gpc", "dps", "gwq", "dkl", "dks", "gks", "dws")

        ident = es.enter_context(nc.sbuf_tensor("identb", [128, 128], F16))
        cst = es.enter_context(nc.sbuf_tensor("cstb", [128, 8], F32))
        st1 = es.enter_context(nc.sbuf_tensor("st1b", [128, 64], F32))
        pb = [es.enter_context(nc.psum_tensor("pb%d" % i, [128, 512], F32))
              for i in range(8)]
        pbh = [b.ap().bitcast(F16) for b in pb]

        dbg = {}

        @block.sync
        def _(sync):
            sync.dma_start(ident[:], ident_d).then_inc(p.inc("dcst", 16), 16)
            sync.dma_start(cst[:], cst_d).then_inc(p.inc("dcst", 16), 16)

        ckvT = es.enter_context(nc.sbuf_tensor("ckvT", [128, 4 * T], F16))
        qattT = es.enter_context(nc.sbuf_tensor("qattT", [128, 4 * T], F16))
        keff = es.enter_context(nc.sbuf_tensor("keff", [128, 4 * 512], F16))
        veff = es.enter_context(nc.sbuf_tensor("veff", [128, 4 * 512], F16))
        for nm, t in (("ckvT", ckvT), ("qattT", qattT),
                      ("keff", keff), ("veff", veff)):
            if nm in debug:
                dbg[nm] = t

        with nc.sbuf_tensor("xnT", [128, 8 * T], F16, side="right") as xnT:
            if "xnT" in debug:
                dbg["xnT"] = xnT
            qf32 = qattT.ap().bitcast(F32)
            xbuf = qf32[:, 0:8192]
            sqscr = ckvT.ap().bitcast(F32)[:, 0:1024]
            xnt = ckvT[:, 2048:6144]
            emit_p1(nc, block, p, x_d, xnT, xbuf, sqscr, xnt, st1,
                    cst, ident, pbh)
            if stage >= 2:
                wsrc = (wdkvT_d, wdqT_d, wuqTh_d, wuqT_d, wdqh_d, wuk_d,
                        wuv_d, woTh_d)
                emit_p23(nc, block, p, xnT, qattT, wqT_d, keysT_d,
                         keys16_scr, wq16_scr, wd256T_d, wup256h_d, g_scr,
                         pb, pbh, ident, dbg, debug, stage, ckvT, scr16,
                         wsrc)
        if stage >= 4:
            emit_p4ab(nc, block, p, scr16, keff, veff, pb)
            p.barrier(block)

        if stage >= 5:
            masks = es.enter_context(nc.sbuf_tensor("masksb", [128, 5 * 512], F16))

            @block.sync
            def _(sync):
                sync.dma_start(masks[:].rearrange("q (m w) -> q m w", m=5),
                               masks_d.rearrange("m q w -> q m w")
                               ).then_inc(p.inc("dcst", 16), 16)
                sync.nop().then_inc(p.inc("bar2" if False else "dcst", 0), 0) if False else None

            @block.tensor
            def _(tensor):
                tensor.wait_ge(p.sem["dcst"], p.cnt["dcst"])

            @block.vector
            def _(vector):
                vector.wait_ge(p.sem["dcst"], p.cnt["dcst"])

            sT = es.enter_context(nc.sbuf_tensor("sT", [128, 4 * T], F16))
            if "sT" in debug:
                dbg["sT"] = sT
            u = es.enter_context(nc.sbuf_tensor("u", [128, 32 * 520], F16))
            qlat = es.enter_context(nc.sbuf_tensor("qlat", [128, 4 * T], F16))
            if "u" in debug:
                dbg["u"] = u
            if "qlat" in debug:
                dbg["qlat"] = qlat
            emit_p4b(nc, block, p, ckvT, veff, u, pb)
            p.barrier(block)
            emit_p5(nc, block, p, ckvT, qattT, keff, u, qlat, sT, masks, cst,
                    pb, pbh, ident, xh_d, g_scr, out_d, do_p6=(stage >= 6))
            p.barrier(block)

        if dbg:
            p.barrier(block)
            dbga = {nm: (t if getattr(t, "tensor", None) is not None else t[:])
                    for nm, t in dbg.items()}
            aps = {nm: nc.dram_tensor("dbg_" + nm, list(a.shape), a.dtype,
                                      kind="ExternalOutput").ap()
                   for nm, a in dbga.items()}

            @block.sync
            def _(sync):
                for nm, a in dbga.items():
                    sync.dma_start(aps[nm], a).then_inc(p.inc("dout0", 16), 16)
                sync.wait_ge(p.sem["dout0"], p.cnt["dout0"])
        if stage < 6:
            @block.gpsimd
            def _(gp):
                gp.wait_ge(p.sem["bar"], p.bar_lvl)

    return nc



def emit_p1(nc, block, p, x_d, xnT, xbuf, sqscr, xnt, st1, cst, ident, pbh):
    NT = T // 128
    x_t = x_d.rearrange("(n q) c -> n q c", q=128)
    stv = st1[:].rearrange("q (j f) -> q j f", j=8)

    def xsl(i):
        return xbuf[:, (i % 8) * C:(i % 8 + 1) * C]

    @block.scalar
    def _(scalar):
        scalar.wait_ge(p.sem["dcst"], 32)
        for i in range(8):
            scalar.dma_start(xsl(i), x_t[i]).then_inc(p.inc("dx0", 16), 16)
        for g in range(NT // 8):
            for j in range(8):
                i = 8 * g + j
                scalar.wait_ge(p.sem["dx0"], 16 * (i + 1))
                if g >= 1 and j == 0:
                    scalar.wait_ge(p.sem["v2"], g)
                scalar.activation(sqscr[:], xsl(i), AF.Square,
                                  accum_out=st1[:, j * 8 + 1:j * 8 + 2]
                                  ).then_inc(p.inc("a1"), 1)
            scalar.wait_ge(p.sem["v2"], g + 1)
            scalar.activation(stv[:, :, 4:5], stv[:, :, 3:4],
                              AF.Ln, bias=cst[:, 1:2])
            scalar.activation(stv[:, :, 5:6], stv[:, :, 4:5],
                              AF.Exp, scale=-0.5).then_inc(p.inc("a2"), 1)
            for j in range(8):
                i = 8 * g + j
                scalar.wait_ge(p.sem["pe1"], i + 1)
                scalar.activation(
                    xnT[:].rearrange("q (c t) -> q c t", c=8)[:, :, i * 128:(i + 1) * 128],
                    pbh[4 + i % 4][:, 0:1024].rearrange("q (c t) -> q c t", c=8),
                    AF.Copy).then_inc(p.inc("se1"), 1)
                if i + 8 < NT:
                    scalar.dma_start(xsl(i + 8), x_t[i + 8]).then_inc(
                        p.inc("dx0", 16), 16)

    @block.vector
    def _(vector):
        def step(inst):
            inst.then_inc(p.inc("vch"), 1)
            vector.wait_ge(p.sem["vch"], p.cnt["vch"])
        for g in range(NT // 8):
            for j in range(8):
                i = 8 * g + j
                vector.wait_ge(p.sem["dx0"], 16 * (i + 1))
                rs = vector.reduce_sum(st1[:, j * 8:j * 8 + 1], xsl(i),
                                       axis=AX)
            step(rs)
            vector.wait_ge(p.sem["a1"], 8 * (g + 1))
            step(vector.tensor_scalar_mul(stv[:, :, 2:3], stv[:, :, 0:1],
                                          1.0 / C))
            step(vector.tensor_tensor(stv[:, :, 3:4], stv[:, :, 2:3],
                                      stv[:, :, 2:3], op=ALU.mult))
            vector.scalar_tensor_tensor(stv[:, :, 3:4], stv[:, :, 1:2],
                                        1.0 / C, stv[:, :, 3:4], op0=ALU.mult,
                                        op1=ALU.subtract).then_inc(
                                            p.inc("v2"), 1)
            vector.wait_ge(p.sem["a2"], g + 1)
            step(vector.scalar_tensor_tensor(
                stv[:, :, 6:7], stv[:, :, 2:3], -1.0,
                stv[:, :, 5:6], op0=ALU.mult, op1=ALU.mult))
            for j in range(8):
                i = 8 * g + j
                if i >= 4:
                    vector.wait_ge(p.sem["pe1"], i - 3)
                vector.scalar_tensor_tensor(
                    xnt[:, (i % 4) * C:(i % 4 + 1) * C], xsl(i),
                    st1[:, j * 8 + 5:j * 8 + 6],
                    st1[:, j * 8 + 6:j * 8 + 7].to_broadcast([128, C]),
                    op0=ALU.mult, op1=ALU.add).then_inc(p.inc("ac1"), 1)

    @block.tensor
    def _(tensor):
        tensor.wait_ge(p.sem["dcst"], p.cnt["dcst"])
        for i in range(NT):
            tensor.wait_ge(p.sem["ac1"], i + 1)
            if i >= 4:
                tensor.wait_ge(p.sem["se1"], i - 3)
            for j in range(8):
                ti = tensor.transpose(
                    pbh[4 + i % 4][:, j * 128:(j + 1) * 128],
                    xnt[:, (i % 4) * C + j * 128:(i % 4) * C + (j + 1) * 128],
                    ident[:])
            ti.then_inc(p.inc("pe1"), 1)



def emit_p23(nc, block, p, xnT, big32, wqT_d, keysT_d, keys16_scr, wq16_scr,
             wd256T_d, wup256h_d, g_scr, pb, pbh, ident, dbg, debug, stage,
             ckvT=None, scr16=None, wsrc=None):
    g_t = g_scr.ap().rearrange("(n q) c -> n q c", q=128)
    big_f = big32.ap().bitcast(F32)
    sxq = [big_f[:, 0:2048], big_f[:, 2048:4096]]
    p3f = big_f[:, 4096:6552]
    hfull = big32[:, 13104:15152]
    with (
        nc.sbuf_tensor("a16s", [128, 8 * 256], F16, side="right") as a16s,
        nc.sbuf_tensor("mrscr", [128, 256], F32, side="right") as mrscr,
        nc.sbuf_tensor("wdk16", [128, 3072], F16, side="right") as wdk16,
        nc.sbuf_tensor("gt16", [128, 2 * 512], F16, side="right") as gt16,
        nc.sbuf_tensor("p3h", [128, 2560], F16, side="right") as p3h,
        nc.sbuf_tensor("f16st", [128, 1024], F16, side="right") as f16st,
    ):
        es2 = ExitStack()
        wqf = es2.enter_context(
            nc.sbuf_tensor("wqf", [128, 2 * 512], F32, side="right"))
        wq32 = es2.enter_context(
            nc.sbuf_tensor("wq32", [128, 2 * 1024], F32, side="right"))
        wq16 = es2.enter_context(
            nc.sbuf_tensor("wq16", [128, 2 * 4096], F16, side="right"))
        keys16 = es2.enter_context(
            nc.sbuf_tensor("keys16", [128, 2 * 1024], F16, side="right"))
        qph = es2.enter_context(
            nc.sbuf_tensor("qph", [128, 4 * 1024], F16, side="right"))
        simf = es2.enter_context(
            nc.sbuf_tensor("simf", [128, 8 * 256], F32, side="right"))
        if "sx" in debug:
            dbg["sx"] = sxq[1]
        if "sx0" in debug:
            dbg["sx0"] = sxq[0]
        if "as" in debug:
            dbg["as"] = p3f
        if "hfull" in debug:
            dbg["hfull"] = hfull
        if "qph" in debug:
            dbg["qph"] = qph
        AS0, ACF, TH, NGM, DN, RC = 0, 2048, 2304, 2432, 2440, 2448
        WSEL, GH, AT2 = 0, 2048, 2304

        _wdt = wd256T_d.rearrange("(a q) k -> q a k", q=128)
        _wut = wup256h_d.rearrange("(a q) c -> q a c", q=128)
        WDK_PIECES = ([( _wdt[:, 2 * j:2 * j + 2, :], 2, 512 * j)
                       for j in range(4)] +
                      [(_wut[:, j:j + 1, :], 1, 2048 + 512 * j)
                       for j in range(2)])

        def p3_bases():
            return {n: p.cnt[n] for n in ("ve3a", "ae3", "ve3t", "ve3r",
                                          "ae3g", "ve3m", "pe3t", "se3",
                                          "pe3f", "ae3s", "dg0", "dg1")}

        def p3_vector_tile(vector, tt, B, sx):
            base = tt * 256
            v_i = sx[:, base:base + 128].rearrange(
                "q (h i) -> q h i", h=8).unsqueeze(3).to_broadcast(
                [128, 8, 16, 16])
            v_j = sx[:, base + 128:base + 256].rearrange(
                "q (h j) -> q h j", h=8).unsqueeze(2).to_broadcast(
                [128, 8, 16, 16])
            vector.tensor_tensor(
                p3f[:, AS0:AS0 + 2048].rearrange("q (h i j) -> q h i j",
                                                 h=8, i=16),
                v_i, v_j, op=ALU.add)
            sx_h = sx[:, base:base + 128].rearrange("q (h i) -> q h i",
                                                    h=8)[:, :, 0:1]
            sy_h = sx[:, base + 128:base + 256].rearrange("q (h i) -> q h i",
                                                          h=8)[:, :, 0:1]
            vector.scalar_tensor_tensor(
                p3f[:, NGM:NGM + 8].unsqueeze(2), sx_h, -1.0, sy_h,
                op0=ALU.mult, op1=ALU.subtract).then_inc(p.inc("ve3a"), 1)
            for hh in range(8):
                vi = vector.max(out=p3f[:, TH + hh * 16:TH + hh * 16 + 8],
                                in_=p3f[:, AS0 + hh * 256:AS0 + (hh + 1) * 256])
            vi.then_inc(p.inc("ve3t"), 1)
            vector.wait_ge(p.sem["ve3t"], p.cnt["ve3t"])
            for hh in range(8):
                vector.match_replace(
                    out=mrscr[:],
                    in_to_replace=p3f[:, TH + hh * 16:TH + hh * 16 + 8],
                    in_values=p3f[:, AS0 + hh * 256:AS0 + (hh + 1) * 256],
                    imm_value=-1e30)
                vi = vector.max(out=p3f[:, TH + hh * 16 + 8:TH + hh * 16 + 16],
                                in_=mrscr[:])
            vi.then_inc(p.inc("ve3t"), 1)
            vector.wait_ge(p.sem["ve3t"], p.cnt["ve3t"])
            vector.wait_ge(p.sem["ae3"], B["ae3"] + tt + 1)
            for hh in range(8):
                vi = vector.scalar_tensor_tensor(
                    p3h[:, WSEL + hh * 256:WSEL + (hh + 1) * 256],
                    p3f[:, AS0 + hh * 256:AS0 + (hh + 1) * 256],
                    p3f[:, TH + hh * 16 + 15:TH + hh * 16 + 16],
                    p3h[:, WSEL + hh * 256:WSEL + (hh + 1) * 256],
                    op0=ALU.is_ge, op1=ALU.mult,
                    accum_out=p3f[:, DN + hh:DN + hh + 1])
            vi.then_inc(p.inc("ve3r"), 1)
            vector.wait_ge(p.sem["ve3r"], p.cnt["ve3r"])
            vector.reciprocal(p3f[:, RC:RC + 8],
                              p3f[:, DN:DN + 8]).then_inc(p.inc("ve3r"), 1)
            vector.wait_ge(p.sem["ve3r"], p.cnt["ve3r"])
            for hh in range(8):
                if hh == 0:
                    vector.tensor_scalar(
                        p3f[:, ACF:ACF + 256], p3h[:, WSEL:WSEL + 256],
                        p3f[:, RC:RC + 1], None, op0=ALU.mult)
                else:
                    vector.scalar_tensor_tensor(
                        p3f[:, ACF:ACF + 256],
                        p3h[:, WSEL + hh * 256:WSEL + (hh + 1) * 256],
                        p3f[:, RC + hh:RC + hh + 1],
                        p3f[:, ACF:ACF + 256], op0=ALU.mult, op1=ALU.add)
            vector.wait_ge(p.sem["ae3g"], B["ae3g"] + tt + 1)
            if B["pe3t"] >= 8:
                vector.wait_ge(p.sem["pe3t"], B["pe3t"] - 8 + tt + 1)
            vector.tensor_tensor(a16s[:, tt * 256:(tt + 1) * 256],
                                 p3h[:, GH:GH + 256],
                                 p3f[:, ACF:ACF + 256],
                                 op=ALU.mult).then_inc(p.inc("ve3m"), 1)

        def p3_scalar_a(scalar, tt, B):
            scalar.wait_ge(p.sem["ve3a"], B["ve3a"] + tt + 1)
            for hh in range(8):
                si_ = scalar.activation(
                    p3h[:, WSEL + hh * 256:WSEL + (hh + 1) * 256],
                    p3f[:, AS0 + hh * 256:AS0 + (hh + 1) * 256],
                    AF.Exp, bias=p3f[:, NGM + hh:NGM + hh + 1])
            si_.then_inc(p.inc("ae3"), 1)
            scalar.activation(p3h[:, GH:GH + 256],
                              hfull[:, tt * 256:(tt + 1) * 256],
                              AF.Gelu).then_inc(p.inc("ae3g"), 1)

        def p3_tail(block, tq, B):
            @block.tensor
            def _(tensor):
                for tt in range(8):
                    tensor.wait_ge(p.sem["ve3m"], B["ve3m"] + tt + 1)
                    if B["se3"] + tt > 0:
                        tensor.wait_ge(p.sem["se3"], B["se3"] + tt)
                    for rc in range(2):
                        ti = tensor.transpose(
                            pbh[7][:, rc * 128:(rc + 1) * 128],
                            a16s[:, tt * 256 + rc * 128:tt * 256 + (rc + 1) * 128],
                            ident[:])
                    ti.then_inc(p.inc("pe3t"), 1)
                    tensor.wait_ge(p.sem["se3"], B["se3"] + tt + 1)
                    if B["ae3s"] + tt > 0:
                        tensor.wait_ge(p.sem["ae3s"], B["ae3s"] + tt)
                    for rc in range(2):
                        mm = tensor.matmul(
                            pb[6][:],
                            p3h[:, AT2 + rc * 128:AT2 + (rc + 1) * 128],
                            wdk16[:, 2048 + rc * 512:2048 + (rc + 1) * 512],
                            start=(rc == 0), stop=(rc == 1))
                    mm.then_inc(p.inc("pe3f"), 1)

            @block.scalar
            def _(scalar):
                for tt in range(8):
                    scalar.wait_ge(p.sem["pe3t"], B["pe3t"] + tt + 1)
                    scalar.activation(p3h[:, AT2:AT2 + 256], pbh[7][:, 0:256],
                                      AF.Copy).then_inc(p.inc("se3"), 1)
                    scalar.wait_ge(p.sem["pe3f"], B["pe3f"] + tt + 1)
                    par = tt % 2
                    if tt >= 2:
                        scalar.wait_ge(p.sem["dg%d" % par],
                                       B["dg%d" % par] + 16 * ((tt - 2) // 2 + 1))
                    elif B["dg%d" % par] > 0:
                        scalar.wait_ge(p.sem["dg%d" % par], B["dg%d" % par])
                    scalar.activation(gt16[:, par * 512:(par + 1) * 512],
                                      pb[6][:], AF.Silu).then_inc(
                                          p.inc("ae3s"), 1)

            @block.gpsimd
            def _(gp):
                for tt in range(8):
                    gp.wait_ge(p.sem["ae3s"], B["ae3s"] + tt + 1)
                    gp.dma_start(
                        g_t[tq * 8 + tt],
                        gt16[:, (tt % 2) * 512:(tt % 2 + 1) * 512]
                        ).then_inc(p.inc("dg%d" % (tt % 2), 16), 16)

        def p3t_pe_t(tensor, tt, B):
            tensor.wait_ge(p.sem["ve3m"], B["ve3m"] + tt + 1)
            if B["se3"] + tt > 0:
                tensor.wait_ge(p.sem["se3"], B["se3"] + tt)
            for rc in range(2):
                ti = tensor.transpose(
                    pbh[7][:, rc * 128:(rc + 1) * 128],
                    a16s[:, tt * 256 + rc * 128:tt * 256 + (rc + 1) * 128],
                    ident[:])
            ti.then_inc(p.inc("pe3t"), 1)

        def p3t_pe_f(tensor, tt, B):
            tensor.wait_ge(p.sem["se3"], B["se3"] + tt + 1)
            if B["ae3s"] + tt > 0:
                tensor.wait_ge(p.sem["ae3s"], B["ae3s"] + tt)
            for rc in range(2):
                mmx = tensor.matmul(
                    pb[6][:],
                    p3h[:, AT2 + rc * 128:AT2 + (rc + 1) * 128],
                    wdk16[:, 2048 + rc * 512:2048 + (rc + 1) * 512],
                    start=(rc == 0), stop=(rc == 1))
            mmx.then_inc(p.inc("pe3f"), 1)

        def p3t_sc_at2(scalar, tt, B):
            scalar.wait_ge(p.sem["pe3t"], B["pe3t"] + tt + 1)
            scalar.activation(p3h[:, AT2:AT2 + 256], pbh[7][:, 0:256],
                              AF.Copy).then_inc(p.inc("se3"), 1)

        def p3t_sc_silu(scalar, tt, B):
            scalar.wait_ge(p.sem["pe3f"], B["pe3f"] + tt + 1)
            par = tt % 2
            if tt >= 2:
                scalar.wait_ge(p.sem["dg%d" % par],
                               B["dg%d" % par] + 16 * ((tt - 2) // 2 + 1))
            elif B["dg%d" % par] > 0:
                scalar.wait_ge(p.sem["dg%d" % par], B["dg%d" % par])
            scalar.activation(gt16[:, par * 512:(par + 1) * 512],
                              pb[6][:], AF.Silu).then_inc(
                                  p.inc("ae3s"), 1)

        if wsrc is not None:
            (wdkvT_d, wdqT_d, wuqTh_d, wuqT_d, wdqh_d, wuk_d,
             wuv_d, woTh_d) = wsrc
            PIECES = (
                [(wdkvT_d, 2 * j, 2, j * 1024) for j in range(4)] +
                [(wdqT_d, 2 * j, 2, 4096 + j * 1024) for j in range(4)] +
                [(wuqTh_d, 2 * j, 2, 8192 + j * 1024) for j in range(2)] +
                [(wuqT_d, j, 1, 10240 + j * 1024) for j in range(4)] +
                [(wdqh_d, 2 * j, 2, 14336 + j * 1024) for j in range(2)] +
                [(wuk_d, 2 * j, 2, 16384 + j * 1024) for j in range(4)] +
                [(wuv_d, 2 * j, 2, 20480 + j * 1024) for j in range(4)] +
                [(woTh_d, 2 * j, 2, 24576 + j * 1024) for j in range(4)])
        else:
            PIECES = []

        wt32 = wqT_d.rearrange("(a q) r -> q a r", q=128)
        kt32 = keysT_d.rearrange("p h (a q) k -> q (p h a) k", q=128)
        ks16 = keys16_scr.rearrange("p h (a q) k -> q (p h a) k", q=128)
        dwq_b = (p.cnt["dwq0"], p.cnt["dwq1"])
        gwq_b = p.cnt["gwq"]
        pe0s = []

        def wq_load_wait(G, qp):
            u = 4 * G + qp
            return gwq_b + u - 1 if u >= 2 else None

        def wq_cast_unit(eng, G, qp):
            par = G % 2
            eng.wait_ge(p.sem["dwq%d" % par],
                        dwq_b[par] + 64 * (G // 2) + 16 * (qp + 1))
            dst = wq16[:, par * 4096 + qp * 1024:par * 4096 + (qp + 1) * 1024]
            src = wq32[:, (qp % 2) * 1024:(qp % 2 + 1) * 1024]
            if hasattr(eng, "activation"):
                eng.activation(dst, src, AF.Copy).then_inc(p.inc("gwq"), 1)
            else:
                eng.tensor_copy(dst, src).then_inc(p.inc("gwq"), 1)

        if True:
            p.barrier(block)
        prevB = None
        prev_tq = None
        for tq in range(4):
            t0 = tq * 1024
            dk0 = p.cnt["dkk"]
            pe0 = p.cnt["pe2"]
            pe0s.append(pe0)
            se0 = p.cnt["se2"]
            ses0 = p.cnt["se2s"]
            vem0 = p.cnt["ve2m"]

            dwqb = (p.cnt["dwq0"], p.cnt["dwq1"])

            @block.sync
            def _(sync, tq=tq, pe0=pe0):
                wq16s = wq16_scr.rearrange("(a q) r -> q a r", q=128)
                for ph in range(16):
                    pp, hh = ph // 8, ph % 8
                    par = ph % 2
                    r0 = pp * 4096 + hh * 512
                    G = 16 * tq + ph
                    if tq == 0:
                        for qp in range(4):
                            w = wq_load_wait(G, qp)
                            if w is not None:
                                sync.wait_ge(p.sem["gwq"], w)
                            sync.dma_start(
                                wq32[:, (qp % 2) * 1024:(qp % 2 + 1) * 1024]
                                .rearrange("q (a r) -> q a r", a=2),
                                wt32[:, 2 * qp:2 * qp + 2, r0:r0 + 512]
                                ).then_inc(p.inc("dwq%d" % par, 16), 16)
                    else:
                        if ph >= 2:
                            sync.wait_ge(p.sem["pe2"], pe0 + (ph - 1) * 16)
                        else:
                            sync.wait_ge(p.sem["pe2"], pe0 - 24 + 16 * ph)
                        if tq == 1 and ph == 0:
                            sync.wait_ge(p.sem["dws"], 256)
                        sync.dma_start(
                            wq16[:, par * 4096:(par + 1) * 4096].rearrange(
                                "q (a r) -> q a r", a=8),
                            wq16s[:, :, r0:r0 + 512]
                            ).then_inc(p.inc("dwq%d" % par, 16), 16)
                        if tq == 1 and ph == 0:
                            sync.wait_ge(p.sem["dks"], 256)
                        sync.dma_start(
                            keys16[:, par * 1024:(par + 1) * 1024].rearrange(
                                "q (a k) -> q a k", a=4),
                            keys16_scr[pp, hh].rearrange(
                                "(a q) k -> q a k", q=128)
                            ).then_inc(p.inc("dkk", 16), 16)
                    if tq >= 1 and ph < 10:
                        idx = 10 * (tq - 1) + ph
                        if idx < len(PIECES):
                            (src, clo, nch, dcol) = PIECES[idx]
                            st = src.rearrange("(a q) k -> q a k", q=128)
                            if idx == 0:
                                sync.wait_ge(p.sem["gks"], 16)
                                sync.wait_ge(p.sem["ge2"], 6)
                            else:
                                sync.wait_ge(p.sem["gpc"], idx)
                            sync.dma_start(
                                wqf[:].rearrange("q (a k) -> q a k", a=nch),
                                st[:, clo:clo + nch, :]).then_inc(
                                    p.inc("dpc", 16), 16)

            if tq == 0:
                @block.gpsimd
                def _(gp, pe0=pe0):
                    wq16s = wq16_scr.rearrange("(a q) r -> q a r", q=128)
                    for ph in range(16):
                        pp, hh = ph // 8, ph % 8
                        par = ph % 2
                        r0 = pp * 4096 + hh * 512
                        gp.dma_start(wqf[:].rearrange("q (a k) -> q a k", a=4),
                                     kt32[:, 4 * ph:4 * ph + 4, :]
                                     ).then_inc(p.inc("dkl", 16), 16)
                        gp.wait_ge(p.sem["dkl"], p.cnt["dkl"])
                        if ph >= 2:
                            gp.wait_ge(p.sem["pe2"], pe0 + (ph - 1) * 16)
                            gp.wait_ge(p.sem["dks"], 16 * (ph - 1))
                        gp.tensor_copy(keys16[:, par * 1024:(par + 1) * 1024],
                                       wqf[:]).then_inc(p.inc("gks"), 1)
                        gp.dma_start(ks16[:, 4 * ph:4 * ph + 4, :],
                                     keys16[:, par * 1024:(par + 1) * 1024]
                                     .rearrange("q (a k) -> q a k", a=4)
                                     ).then_inc(p.inc("dks", 16), 16)
                        if ph >= 2:
                            gp.wait_ge(p.sem["pe2"], pe0 + (ph - 1) * 16)
                            gp.wait_ge(p.sem["dws"], 16 * (ph - 1))
                        for qp in range(4):
                            wq_cast_unit(gp, ph, qp)
                        gp.dma_start(wq16s[:, :, r0:r0 + 512],
                                     wq16[:, par * 4096:(par + 1) * 4096]
                                     .rearrange("q (a r) -> q a r", a=8)
                                     ).then_inc(p.inc("dws", 16), 16)
                        if 10 <= ph and ph - 10 < len(WDK_PIECES):
                            src, nch, col = WDK_PIECES[ph - 10]
                            gp.dma_start(wqf[:, 0:512].rearrange(
                                "q (a k) -> q a k", a=nch), src).then_inc(
                                    p.inc("dpc", 16), 16)
                            gp.wait_ge(p.sem["dpc"], p.cnt["dpc"])
                            gp.tensor_copy(wdk16[:, col:col + 512],
                                           wqf[:, 0:512]).then_inc(
                                               p.inc("ge2"), 1)

            @block.tensor
            def _(tensor, tq=tq, t0=t0, dk0=dk0, se0=se0, ses0=ses0,
                  dwqb=dwqb):
                for ph in range(16):
                    par = ph % 2
                    if ph == 0:
                        tensor.wait_ge(p.sem["se1"], 8 * (tq + 1))
                    if tq == 0:
                        tensor.wait_ge(p.sem["gwq"], gwq_b + 4 * (ph + 1))
                        tensor.wait_ge(p.sem["gks"], ph + 1)
                    else:
                        tensor.wait_ge(p.sem["dwq%d" % par],
                                       dwqb[par] + 16 * (ph // 2 + 1))
                        tensor.wait_ge(p.sem["dkk"], dk0 + 16 * (ph + 1))
                    for dkc in range(4):
                        for tc in range(2):
                            gi = ph * 8 + dkc * 2 + tc
                            if gi >= 2:
                                tensor.wait_ge(p.sem["se2"], se0 + gi - 1)
                            elif se0 + gi - 9 > 0:
                                tensor.wait_ge(p.sem["se2"], se0 + gi - 9)
                            for cc in range(8):
                                mm = tensor.matmul(
                                    pb[gi % 2][:],
                                    wq16[:, par * 4096 + cc * 512 + dkc * 128:
                                         par * 4096 + cc * 512 + (dkc + 1) * 128],
                                    xnT[:, cc * T + t0 + tc * 512:
                                        cc * T + t0 + (tc + 1) * 512],
                                    start=(cc == 0), stop=(cc == 7))
                            mm.then_inc(p.inc("pe2"), 1)
                    for tt in range(8):
                        si = ph * 8 + tt
                        tensor.wait_ge(p.sem["se2"], se0 + ph * 8 + 7 + tt // 4)
                        if si >= 2:
                            tensor.wait_ge(p.sem["se2s"], ses0 + si - 1)
                        elif ses0 + si - 1 > 0:
                            tensor.wait_ge(p.sem["se2s"], ses0 + si - 1)
                        for dkc in range(4):
                            mm = tensor.matmul(
                                pb[2 + si % 2][:, 0:256],
                                qph[:, dkc * 1024 + tt * 128:dkc * 1024 + (tt + 1) * 128],
                                keys16[:, par * 1024 + dkc * 256:
                                       par * 1024 + (dkc + 1) * 256],
                                start=(dkc == 0), stop=(dkc == 3))
                        mm.then_inc(p.inc("pe2"), 1)

            @block.scalar
            def _(scalar, pe0=pe0, vem0=vem0, prevB=prevB):
                for ph in range(16):
                    for gi8 in range(8):
                        dkc, tc = gi8 // 2, gi8 % 2
                        g = ph * 8 + gi8
                        scalar.wait_ge(p.sem["pe2"], pe0 + ph * 16 + gi8 + 1)
                        scalar.activation(
                            qph[:, dkc * 1024 + tc * 512:dkc * 1024 + (tc + 1) * 512],
                            pb[g % 2][:], AF.Copy).then_inc(p.inc("se2"), 1)
                    for tt in range(8):
                        si = ph * 8 + tt
                        scalar.wait_ge(p.sem["pe2"], pe0 + ph * 16 + 8 + tt + 1)
                        if si >= 8:
                            scalar.wait_ge(p.sem["ve2m"], vem0 + si - 7)
                        scalar.activation(
                            simf[:, (si % 8) * 256:(si % 8 + 1) * 256],
                            pb[2 + si % 2][:, 0:256],
                            AF.Copy).then_inc(p.inc("se2s"), 1)
                    if prevB is not None and ph % 2 == 1:
                        p3_scalar_a(scalar, ph // 2, prevB)
                    if tq >= 1 and ph < 10:
                        idx = 10 * (tq - 1) + ph
                        if idx < len(PIECES):
                            dcol = PIECES[idx][3]
                            scalar.wait_ge(p.sem["dpc"],
                                           16 * len(WDK_PIECES)
                                           + 16 * (idx + 1))
                            if idx >= 1:
                                scalar.wait_ge(p.sem["dps"], 16 * idx)
                            scalar.activation(f16st[:], wqf[:],
                                              AF.Copy).then_inc(
                                                  p.inc("gpc"), 1)
                            scalar.wait_ge(p.sem["gpc"], p.cnt["gpc"])
                            scalar.dma_start(scr16[:, dcol:dcol + 1024],
                                             f16st[:]).then_inc(
                                                 p.inc("dps", 16), 16)

            @block.vector
            def _(vector, tq=tq, ses0=ses0, prevB=prevB):
                sx = sxq[tq % 2]
                for ph in range(16):
                    pp, hh = ph // 8, ph % 8
                    for tt in range(8):
                        si = ph * 8 + tt
                        col = tt * 256 + pp * 128 + hh * 16
                        vector.wait_ge(p.sem["se2s"], ses0 + si + 1)
                        vector.max(out=sx[:, col:col + 8],
                                   in_=simf[:, (si % 8) * 256:(si % 8 + 1) * 256]
                                   ).then_inc(p.inc("ve2k"), 1)
                        vector.wait_ge(p.sem["ve2k"], p.cnt["ve2k"])
                        vector.match_replace(
                            out=mrscr[:], in_to_replace=sx[:, col:col + 8],
                            in_values=simf[:, (si % 8) * 256:(si % 8 + 1) * 256],
                            imm_value=-1e30).then_inc(p.inc("ve2m"), 1)
                        vector.max(out=sx[:, col + 8:col + 16],
                                   in_=mrscr[:]).then_inc(p.inc("ve2k"), 1)
                    if prevB is not None and ph % 2 == 1:
                        p3_vector_tile(vector, ph // 2, prevB, sxq[(tq + 1) % 2])

            if prevB is not None:
                p3_tail(block, prev_tq, prevB)

            peh0 = p.cnt["pe2"]
            seh0 = p.cnt["se2"]

            @block.tensor
            def _(tensor, t0=t0, peh0=peh0, seh0=seh0, ge2f=p.cnt["ge2"]):
                for tt in range(8):
                    if tt == 0:
                        tensor.wait_ge(p.sem["ge2"], ge2f)
                    if tt >= 2:
                        tensor.wait_ge(p.sem["se2"], seh0 + tt - 1)
                    elif seh0 - 129 + tt > 0:
                        tensor.wait_ge(p.sem["se2"], seh0 - 129 + tt)
                    for cc in range(8):
                        mm = tensor.matmul(
                            pb[4 + tt % 2][:, 0:256],
                            xnT[:, cc * T + t0 + tt * 128:cc * T + t0 + (tt + 1) * 128],
                            wdk16[:, cc * 256:(cc + 1) * 256],
                            start=(cc == 0), stop=(cc == 7))
                    mm.then_inc(p.inc("pe2"), 1)

            @block.scalar
            def _(scalar, peh0=peh0):
                for tt in range(8):
                    scalar.wait_ge(p.sem["pe2"], peh0 + tt + 1)
                    scalar.activation(hfull[:, tt * 256:(tt + 1) * 256],
                                      pb[4 + tt % 2][:, 0:256],
                                      AF.Copy).then_inc(p.inc("se2"), 1)

            prevB = p3_bases()
            prev_tq = tq
            if stage < 3:
                prevB = None

        if prevB is not None and stage < 4:
            @block.vector
            def _(vector, prevB=prevB):
                for tt in range(8):
                    p3_vector_tile(vector, tt, prevB, sxq[1])

            @block.scalar
            def _(scalar, prevB=prevB):
                for tt in range(8):
                    p3_scalar_a(scalar, tt, prevB)

            p3_tail(block, 3, prevB)
            es2.close()
        elif prevB is not None:
            B3 = prevB
            fin = {n: p.cnt[n] for n in ("se2", "se2s", "pe2", "ve2k", "dps")}
            es2.close()
            with (
                nc.sbuf_tensor("wdkv16", [128, 4096], F16,
                               side="right") as wdkv16,
                nc.sbuf_tensor("wdqT16", [128, 4096], F16,
                               side="right") as wdqT16,
                nc.sbuf_tensor("wuqh16", [128, 2048], F16,
                               side="right") as wuqh16,
                nc.sbuf_tensor("t1tmp", [128, 2 * 2048], F16,
                               side="right") as t1tmp,
            ):
                dxb = (p.cnt["dx0"], p.cnt["dx1"])

                @block.sync
                def _(sync):
                    sync.wait_ge(p.sem["dps"], fin["dps"])
                    sync.wait_ge(p.sem["pe2"], fin["pe2"])
                    sync.wait_ge(p.sem["ve2k"], fin["ve2k"])
                    for i, (dst, lo) in enumerate((
                            (wdkv16[:, 0:2048], 0),
                            (wdkv16[:, 2048:4096], 2048),
                            (wdqT16[:, 0:2048], 4096),
                            (wdqT16[:, 2048:4096], 6144),
                            (wuqh16[:, 0:2048], 8192))):
                        sync.dma_start(dst, scr16[:, lo:lo + 2048]).then_inc(
                            p.inc("dx%d" % (i % 2), 16), 16)

                mspec = []

                def add_ckv(tc, lc):
                    w = []
                    if not mspec:
                        w = [("dx0", dxb[0] + 16), ("dx1", dxb[1] + 16)]
                    if len(mspec) < 4:
                        w = w + [("se2", fin["se2"]), ("se2s", fin["se2s"])]
                    mspec.append((
                        [(wdkv16, cc * 512 + lc * 128, xnT,
                          cc * T + tc * 512) for cc in range(8)],
                        ckvT, lc * T + tc * 512, w))

                def add_t1(tc, qlc):
                    w = []
                    if tc == 0 and qlc == 0:
                        w = [("dx0", dxb[0] + 32), ("dx1", dxb[1] + 32)]
                    mspec.append((
                        [(wdqT16, cc * 512 + qlc * 128, xnT,
                          cc * T + tc * 512) for cc in range(8)],
                        t1tmp, (tc % 2) * 2048 + qlc * 512, w))

                def add_qatt(tc, rc):
                    w = []
                    if tc == 0 and rc == 0:
                        w = [("dx0", dxb[0] + 48)]
                    mspec.append((
                        [(wuqh16, qlc * 512 + rc * 128, t1tmp,
                          (tc % 2) * 2048 + qlc * 512) for qlc in range(4)],
                        big32, rc * T + tc * 512, w))

                for tc in range(8):
                    for lc in range(4):
                        add_ckv(tc, lc)
                for tc in range(2):
                    for qlc in range(4):
                        add_t1(tc, qlc)
                for tc in range(8):
                    for rc in range(4):
                        add_qatt(tc, rc)
                    if tc + 2 < 8:
                        for qlc in range(4):
                            add_t1(tc + 2, qlc)
                pe_hooks = {22: [(p3t_pe_t, 0)], 26: [(p3t_pe_f, 0)],
                            36: [(p3t_pe_t, 1)], 38: [(p3t_pe_f, 1)],
                            40: [(f, tt) for tt in range(2, 8)
                                 for f in (p3t_pe_t, p3t_pe_f)]}
                sc_hooks = {7: [(p3_scalar_a, 0)], 20: [(p3_scalar_a, 1)],
                            24: [(p3t_sc_at2, 0)],
                            26: [(p3t_sc_silu, 0)],
                            32: [(p3_scalar_a, 2)],
                            36: [(p3_scalar_a, 3)],
                            38: [(p3t_sc_at2, 1)],
                            40: ([(p3_scalar_a, tt) for tt in range(4, 8)] +
                                 [(p3t_sc_silu, 1)] +
                                 [(f, tt) for tt in range(2, 8)
                                  for f in (p3t_sc_at2, p3t_sc_silu)])}
                pe40 = p.cnt["pe4"]
                se40 = p.cnt["se4"]

                @block.tensor
                def _(tensor):
                    for gi, (mms, _, _c, waits) in enumerate(mspec):
                        for (fn, tt) in pe_hooks.get(gi, ()):
                            fn(tensor, tt, B3)
                        for (sname, val) in waits:
                            tensor.wait_ge(p.sem[sname], val)
                        if gi >= 4:
                            tensor.wait_ge(p.sem["se4"], se40 + gi - 3)
                        for mi, (lt, lcol, rt, rcol) in enumerate(mms):
                            m = tensor.matmul(
                                pb[gi % 4][:], lt[:, lcol:lcol + 128],
                                rt[:, rcol:rcol + 512], start=(mi == 0),
                                stop=(mi == len(mms) - 1))
                        m.then_inc(p.inc("pe4"), 1)

                @block.scalar
                def _(scalar):
                    for gi, (_, dt_, dcol, _w) in enumerate(mspec):
                        for (fn, tt) in sc_hooks.get(gi, ()):
                            fn(scalar, tt, B3)
                        scalar.wait_ge(p.sem["pe4"], pe40 + gi + 1)
                        scalar.activation(dt_[:, dcol:dcol + 512],
                                          pb[gi % 4][:],
                                          AF.Copy).then_inc(p.inc("se4"), 1)

                @block.vector
                def _(vector):
                    for tt in range(8):
                        p3_vector_tile(vector, tt, B3, sxq[1])

                @block.gpsimd
                def _(gp):
                    for tt in range(8):
                        gp.wait_ge(p.sem["ae3s"], B3["ae3s"] + tt + 1)
                        gp.dma_start(
                            g_t[3 * 8 + tt],
                            gt16[:, (tt % 2) * 512:(tt % 2 + 1) * 512]
                            ).then_inc(p.inc("dg%d" % (tt % 2), 16), 16)
        else:
            es2.close()
        p.barrier(block)



def _mm_groups(nc, block, p, glist, evacs, pb):
    pe0 = p.cnt["pe4"]
    se0 = p.cnt["se4"]

    @block.tensor
    def _(tensor):
        for gi, mms in enumerate(glist):
            for (sem, val) in mms[0][5]:
                tensor.wait_ge(p.sem[sem], val)
            if gi >= 4:
                tensor.wait_ge(p.sem["se4"], se0 + gi - 3)
            for mi, (lt, lcol, rt, rcol, w, _) in enumerate(mms):
                mm = tensor.matmul(pb[gi % 4][:, 0:w],
                                   lt[:, lcol:lcol + 128],
                                   rt[:, rcol:rcol + w],
                                   start=(mi == 0), stop=(mi == len(mms) - 1))
            mm.then_inc(p.inc("pe4"), 1)

    @block.scalar
    def _(scalar):
        for gi, (dt_, dcol, w) in enumerate(evacs):
            scalar.wait_ge(p.sem["pe4"], pe0 + gi + 1)
            scalar.activation(dt_[:, dcol:dcol + w], pb[gi % 4][:, 0:w],
                              AF.Copy).then_inc(p.inc("se4"), 1)


def emit_p4ab(nc, block, p, scr16, keff, veff, pb):
    with (
        nc.sbuf_tensor("wuqA16", [128, 4096], F16, side="right") as wuqA16,
        nc.sbuf_tensor("wdqh16", [128, 2048], F16, side="right") as wdqh16,
        nc.sbuf_tensor("wuk16", [128, 4096], F16, side="right") as wuk16,
        nc.sbuf_tensor("at16", [128, 4096], F16, side="right") as at16,
        nc.sbuf_tensor("wuv16", [128, 4096], F16, side="right") as wuv16,
        nc.sbuf_tensor("woTh16", [128, 4096], F16, side="right") as woTh16,
    ):
        pe4f = p.cnt["pe4"]
        se4f = p.cnt["se4"]
        qn = ("dx0", "dx1", "dwq0", "dwq1")
        loads = [(wuqA16[:, 0:2048], 10240), (wuqA16[:, 2048:4096], 12288),
                 (wdqh16[:, 0:2048], 14336),
                 (wuk16[:, 0:2048], 16384), (wuk16[:, 2048:4096], 18432),
                 (wuv16[:, 0:2048], 20480), (wuv16[:, 2048:4096], 22528),
                 (woTh16[:, 0:2048], 24576), (woTh16[:, 2048:4096], 26624)]
        dcnt = []

        @block.sync
        def _(sync):
            sync.wait_ge(p.sem["pe4"], pe4f)
            for i, (dst, lo) in enumerate(loads):
                nm = qn[i % 4]
                sync.dma_start(dst, scr16[:, lo:lo + 2048]).then_inc(
                    p.inc(nm, 16), 16)
                dcnt.append((nm, p.cnt[nm]))

        gwait = [dcnt[0], dcnt[1], dcnt[2], ("se4", se4f)]
        glist = []
        evacs = []
        for cpc in range(8):
            glist.append([(wuqA16, qlc * 1024 + cpc * 128, wdqh16,
                           qlc * 512, 512, gwait) for qlc in range(4)])
            evacs.append((at16, cpc * 512, 512))
        se_at = p.cnt["se4"] + 8
        kwait = [("se4", se_at), dcnt[3], dcnt[4]]
        for rc in range(4):
            glist.append([(at16, cpc * 512 + rc * 128, wuk16, cpc * 512,
                           512, kwait) for cpc in range(8)])
            evacs.append((keff, rc * 512, 512))
        vwait = [dcnt[5], dcnt[6], dcnt[7], dcnt[8]]
        for vlc in range(4):
            glist.append([(wuv16, cc * 512 + vlc * 128, woTh16, cc * 512,
                           512, vwait) for cc in range(8)])
            evacs.append((veff, vlc * 512, 512))
        _mm_groups(nc, block, p, glist, evacs, pb)


def emit_p4b(nc, block, p, ckvT, veff, u, pb):
    pe0 = p.cnt["pe4"]
    se0 = p.cnt["se4"]

    @block.tensor
    def _(tensor):
        for kc in range(32):
            if kc >= 4:
                tensor.wait_ge(p.sem["se4"], se0 + kc - 3)
            for lc in range(4):
                mm = tensor.matmul(pb[kc % 4][:],
                                   ckvT[:, lc * T + kc * 128:lc * T + (kc + 1) * 128],
                                   veff[:, lc * 512:(lc + 1) * 512],
                                   start=(lc == 0), stop=(lc == 3))
            mm.then_inc(p.inc("pe4"), 1)

    @block.scalar
    def _(scalar):
        for kc in range(32):
            scalar.wait_ge(p.sem["pe4"], pe0 + kc + 1)
            scalar.activation(
                u[:, kc * 520:kc * 520 + 520].rearrange(
                    "q (h w) -> q h w", h=8)[:, :, 0:64],
                pb[kc % 4][:].rearrange("q (h w) -> q h w", h=8),
                AF.Copy).then_inc(p.inc("se4"), 1)

    @block.vector
    def _(vector):
        vector.memset(
            u[:].rearrange("q (k h w) -> q k h w", k=32, h=8)[:, :, :, 64:65],
            1.0)



def emit_p5(nc, block, p, ckvT, qattT, keff, u, qlat, sT, masks, cst, pb,
            pbh, ident, xh_d, g_scr, out_d, do_p6=True):
    with (
        nc.sbuf_tensor("ptil", [128, 4 * 512], F16, side="right") as ptil,
        nc.sbuf_tensor("rcp", [1, 2 * 512], F16, side="right") as rcp,
        nc.sbuf_tensor("lnd", [1, 512], F32, side="right") as lnd,
        nc.sbuf_tensor("bcs", [64, 2 * 512], F16, side="right") as bcs,
        nc.sbuf_tensor("stile", [128, 2 * 512], F16, side="right") as stile,
        nc.sbuf_tensor("gtile", [128, 2 * 512], F16, side="right") as gtile,
        nc.sbuf_tensor("xtile", [128, 2 * 512], F32, side="right") as xtile,
        nc.sbuf_tensor("otile", [128, 2 * 512], F32, side="right") as otile,
    ):
        g_t = g_scr.ap().rearrange("(n q) c -> n q c", q=128)
        xh_t = xh_d.rearrange("(n q) c -> n q c", q=128)
        o_t = out_d.rearrange("(n q) c -> n q c", q=128)
        S0, S1, Y0, Y1, D0, D1, QL, BC = range(8)
        SB = [S0, S1, D0, D1]
        grans = []
        g = 0
        for hh in range(P5_HEADS):
            for qb in range(P5_QB):
                hq = hh * 8 + qb
                nkt = 4 * qb + 4
                for kt in range(nkt):
                    grans.append((hh, qb, kt, g, hq, kt == 0, kt == nkt - 1))
                    g += 1
        NG = g
        GSTART = [hh * 144 for hh in range(9)]
        pe60 = p.cnt["pe6"]
        ve60 = p.cnt["ve6"]
        se60 = p.cnt["se6"]
        dx60 = (p.cnt["dxh0"], p.cnt["dxh1"])
        dg60 = (p.cnt["dg0"], p.cnt["dg1"])
        do60 = (p.cnt["dout0"], p.cnt["dout1"])
        ae0 = p.cnt["ae5"]
        pt0 = p.cnt["pt"]
        ps0 = p.cnt["pe5s"]
        py0 = p.cnt["pe5y"]
        pd0 = p.cnt["pe5d"]
        vr0 = p.cnt["ve5r"]
        pb0 = p.cnt["pe5b"]
        dv0 = p.cnt["dvt"]
        vq0 = p.cnt["ve5q"]
        ones_col = masks[:, 4 * 512:4 * 512 + 1]
        ones_row64 = masks[0:1, 4 * 512:4 * 512 + 64]

        def p6_pair(tensor, k):
            if k >= 1:
                tensor.wait_ge(p.sem["se6"], se60 + 2 * k)
            for i in (2 * k, 2 * k + 1):
                for c4 in range(4):
                    ti = tensor.transpose(
                        pbh[QL][:, ((i % 2) * 4 + c4) * 128:
                                ((i % 2) * 4 + c4 + 1) * 128],
                        sT[:, c4 * T + i * 128:c4 * T + (i + 1) * 128],
                        ident[:])
                ti.then_inc(p.inc("pe6"), 1)

        def emit_bcast(tensor, pend, at_head=False):
            tensor.wait_ge(p.sem["ve5r"], vr0 + pend + 1)
            if pend >= 1:
                tensor.wait_ge(p.sem["dvt"], dv0 + pend)
            if at_head:
                tensor.wait_ge(p.sem["ve5q"], vq0 + ((pend + 1) // 8 + 1) * 32)
            tensor.matmul(pb[BC][0:64, :], ones_row64,
                          rcp[0:1, (pend % 2) * 512:(pend % 2 + 1) * 512],
                          start=True, stop=True).then_inc(p.inc("pe5b"), 1)

        @block.tensor
        def _(tensor):
            pending = []

            def emit_yd(tensor, pg):
                (hh, qb, kt, g, hq, first, last) = pg
                tensor.wait_ge(p.sem["pt"], pt0 + g + 1)
                tensor.matmul(pb[Y0 + hq % 2][0:65, :],
                              u[:, kt * 520 + hh * 65:kt * 520 + (hh + 1) * 65],
                              ptil[:, (g % 4) * 512:(g % 4 + 1) * 512],
                              start=first, stop=last).then_inc(p.inc("pe5y"), 1)

            for gran in grans:
                (hh, qb, kt, g, hq, first, last) = gran
                if first:
                    if qb == 0:
                        off = (hh % 2) * 64
                        rc = hh // 2
                        for lc in range(4):
                            for qc in range(8):
                                qi = hh * 32 + lc * 8 + qc
                                if qi >= 2:
                                    tensor.wait_ge(p.sem["ve5q"], vq0 + qi - 1)
                                if qi % 2 == 1 and hh >= 1 and lc == 0 and qc == 1:
                                    tensor.wait_ge(p.sem["dvt"],
                                                   dv0 + hh * 8 - 1)
                                tensor.matmul(
                                    pb[QL if qi % 2 == 0 else BC][:],
                                    keff[off:off + 64,
                                         rc * 512 + lc * 128:rc * 512 + (lc + 1) * 128],
                                    qattT[off:off + 64,
                                          rc * T + qc * 512:rc * T + (qc + 1) * 512],
                                    start=True, stop=True).then_inc(p.inc("pe5s"), 1)
                                if qc in (0, 1) and lc == 0 and pending:
                                    emit_yd(tensor, pending.pop(0))
                    tensor.wait_ge(p.sem["ve5q"], vq0 + hh * 32 + 25 + qb)
                    if hq >= 2:
                        tensor.wait_ge(p.sem["dvt"], dv0 + hq - 1)
                if hq >= 1 and kt == 2:
                    emit_bcast(tensor, hq - 1, at_head=(qb == 0))
                if do_p6 and hh == 7 and qb >= 1 and kt == 4:
                    tensor.wait_ge(p.sem["dvt"], dv0 + hq)
                    if qb == 1:
                        tensor.wait_ge(p.sem["ve5q"], vq0 + 256)
                    p6_pair(tensor, 2 * (qb - 1))
                if do_p6 and hh == 7 and qb >= 1 and kt == 6:
                    p6_pair(tensor, 2 * (qb - 1) + 1)
                if g >= 4:
                    tensor.wait_ge(p.sem["ae5"], ae0 + g - 3)
                for lc in range(4):
                    mm = tensor.matmul(
                        pb[SB[g % 4]][:],
                        ckvT[:, lc * T + kt * 128:lc * T + (kt + 1) * 128],
                        qlat[:, lc * T + qb * 512:lc * T + (qb + 1) * 512],
                        start=(lc == 0), stop=(lc == 3))
                mm.then_inc(p.inc("pe5s"), 1)
                pending.append(gran)
                if len(pending) > 2:
                    emit_yd(tensor, pending.pop(0))
            while pending:
                emit_yd(tensor, pending.pop(0))
            emit_bcast(tensor, 63)
            if do_p6:
                tensor.wait_ge(p.sem["dvt"], dv0 + 64)
                p6_pair(tensor, 14)
                p6_pair(tensor, 15)

        @block.scalar
        def _(scalar):
            def p6_evac(i):
                par = i % 2
                scalar.wait_ge(p.sem["pe6"], pe60 + 2 * (i // 2) + 2)
                if i >= 2:
                    scalar.wait_ge(p.sem["ve6"], ve60 + i - 1)
                scalar.activation(stile[:, par * 512:(par + 1) * 512],
                                  pbh[QL][:, par * 512:(par + 1) * 512],
                                  AF.Copy).then_inc(p.inc("se6"), 1)

            def tail_recip(hp, glast):
                par = hp % 2
                scalar.wait_ge(p.sem["pe5y"], py0 + glast + 1)
                scalar.activation(lnd[0:1, :], pb[Y0 + par][64:65, :], AF.Ln)
                scalar.activation(rcp[0:1, par * 512:(par + 1) * 512],
                                  lnd[0:1, :], AF.Exp,
                                  scale=-1.0).then_inc(p.inc("ve5r"), 1)

            for (hh, qb, kt, g, hq, first, last) in grans:
                if hq >= 1 and kt == 1:
                    tail_recip(hq - 1, g - 2)
                if do_p6 and hh == 7 and qb >= 1 and kt == 4:
                    for i6 in (4 * (qb - 1), 4 * (qb - 1) + 1):
                        p6_evac(i6)
                if do_p6 and hh == 7 and qb >= 1 and kt == 6:
                    for i6 in (4 * (qb - 1) + 2, 4 * (qb - 1) + 3):
                        p6_evac(i6)
                scalar.wait_ge(p.sem["pe5s"], ps0 + 32 * (hh + 1) + g + 1)
                if g >= 4:
                    scalar.wait_ge(p.sem["pe5y"], py0 + g - 3)
                scalar.activation(ptil[:, (g % 4) * 512:(g % 4 + 1) * 512],
                                  pb[SB[g % 4]][:], AF.Exp, bias=cst[:, 0:1],
                                  scale=0.125).then_inc(p.inc("ae5"), 1)
            tail_recip(63, NG - 1)
            if do_p6:
                for i6 in range(28, 32):
                    p6_evac(i6)

        if do_p6:
            NT6 = T // 128

            @block.gpsimd
            def _(gp):
                for i in range(NT6):
                    par = i % 2
                    if i >= 2:
                        gp.wait_ge(p.sem["ve6"], ve60 + i - 1)
                    gp.dma_start(gtile[:, par * 512:(par + 1) * 512],
                                 g_t[i]).then_inc(p.inc("dg%d" % par, 16), 16)
                    gp.dma_start(xtile[:, par * 512:(par + 1) * 512],
                                 xh_t[i]).then_inc(p.inc("dxh%d" % par, 16), 16)
                gp.wait_ge(p.sem["dout0"], do60[0] + 16 * (NT6 // 2))
                gp.wait_ge(p.sem["dout1"], do60[1] + 16 * (NT6 // 2))

            @block.sync
            def _(sync):
                for i in range(NT6):
                    par = i % 2
                    sync.wait_ge(p.sem["ve6"], ve60 + i + 1)
                    sync.dma_start(o_t[i], otile[:, par * 512:(par + 1) * 512]
                                   ).then_inc(p.inc("dout%d" % par, 16), 16)

        @block.vector
        def _(vector):
            def p6_combine(i):
                par = i % 2
                vector.wait_ge(p.sem["se6"], se60 + i + 1)
                vector.wait_ge(p.sem["dg%d" % par],
                               dg60[par] + 16 * (i // 2 + 1))
                vector.wait_ge(p.sem["dxh%d" % par],
                               dx60[par] + 16 * (i // 2 + 1))
                if i >= 2:
                    vector.wait_ge(p.sem["dout%d" % par],
                                   do60[par] + 16 * ((i - 2) // 2 + 1))
                vector.tensor_tensor(otile[:, par * 512:(par + 1) * 512],
                                     stile[:, par * 512:(par + 1) * 512],
                                     gtile[:, par * 512:(par + 1) * 512],
                                     op=ALU.mult)
                vector.tensor_add(otile[:, par * 512:(par + 1) * 512],
                                  otile[:, par * 512:(par + 1) * 512],
                                  xtile[:, par * 512:(par + 1) * 512]
                                  ).then_inc(p.inc("ve6"), 1)

            def tail_stt(hp):
                par = hp % 2
                hhp, qbp = hp // 8, hp % 8
                vector.wait_ge(p.sem["pe5b"], pb0 + hp + 1)
                vector.tensor_copy(bcs[:, par * 512:(par + 1) * 512],
                                   pb[BC][0:64, :])
                off = (hhp % 2) * 64
                vector.scalar_tensor_tensor(
                    sT[off:off + 64,
                       (hhp // 2) * T + qbp * 512:(hhp // 2) * T + (qbp + 1) * 512],
                    pb[Y0 + par][0:64, :], 1.0,
                    bcs[:, par * 512:(par + 1) * 512],
                    op0=ALU.mult, op1=ALU.mult).then_inc(p.inc("dvt"), 1)

            for (hh, qb, kt, g, hq, first, last) in grans:
                if qb == 0 and first:
                    for lc in range(4):
                        for qc in range(8):
                            qi = hh * 32 + lc * 8 + qc
                            vector.wait_ge(p.sem["pe5s"],
                                           ps0 + GSTART[hh] + qi + 1)
                            vector.tensor_copy(
                                qlat[:, lc * T + qc * 512:lc * T + (qc + 1) * 512],
                                pb[QL if qi % 2 == 0 else BC][:]
                                ).then_inc(p.inc("ve5q"), 1)
                if hq >= 1 and kt == 2:
                    tail_stt(hq - 1)
                vector.wait_ge(p.sem["ae5"], ae0 + g + 1)
                midx = kt - 4 * qb if kt >= 4 * qb else 4
                vector.tensor_tensor(ptil[:, (g % 4) * 512:(g % 4 + 1) * 512],
                                     ptil[:, (g % 4) * 512:(g % 4 + 1) * 512],
                                     masks[:, midx * 512:(midx + 1) * 512],
                                     op=ALU.mult).then_inc(p.inc("pt"), 1)
                if do_p6 and hh == 7 and qb >= 1 and kt == 5:
                    for i6 in (4 * (qb - 1), 4 * (qb - 1) + 1):
                        p6_combine(i6)
                if do_p6 and hh == 7 and qb >= 1 and kt == 7:
                    for i6 in (4 * (qb - 1) + 2, 4 * (qb - 1) + 3):
                        p6_combine(i6)
            tail_stt(63)
            if do_p6:
                for i6 in range(28, 32):
                    p6_combine(i6)



def emit_p6(nc, block, p, sT, xh_d, g_scr, out_d, ident, pbh):
    with (
        nc.sbuf_tensor("stile", [128, 2 * 512], F16, side="right") as stile,
        nc.sbuf_tensor("gtile", [128, 2 * 512], F16, side="right") as gtile,
        nc.sbuf_tensor("xtile", [128, 2 * 512], F32, side="right") as xtile,
        nc.sbuf_tensor("otile", [128, 2 * 512], F32, side="right") as otile,
    ):
        g_t = g_scr.ap().rearrange("(n q) c -> n q c", q=128)
        xh_t = xh_d.rearrange("(n q) c -> n q c", q=128)
        o_t = out_d.rearrange("(n q) c -> n q c", q=128)
        NT = T // 128
        pe0 = p.cnt["pe6"]
        ve0 = p.cnt["ve6"]
        se0 = p.cnt["se6"]
        dx0 = (p.cnt["dxh0"], p.cnt["dxh1"])
        dg0 = (p.cnt["dg0"], p.cnt["dg1"])
        do0 = (p.cnt["dout0"], p.cnt["dout1"])

        @block.gpsimd
        def _(gp):
            for i in range(NT):
                par = i % 2
                if i >= 2:
                    gp.wait_ge(p.sem["ve6"], ve0 + i - 1)
                gp.dma_start(gtile[:, par * 512:(par + 1) * 512],
                             g_t[i]).then_inc(p.inc("dg%d" % par, 16), 16)
                gp.dma_start(xtile[:, par * 512:(par + 1) * 512],
                             xh_t[i]).then_inc(p.inc("dxh%d" % par, 16), 16)
            gp.wait_ge(p.sem["dout0"], do0[0] + 16 * (NT // 2))
            gp.wait_ge(p.sem["dout1"], do0[1] + 16 * (NT // 2))

        @block.tensor
        def _(tensor):
            for i in range(NT):
                par = i % 2
                if i >= 2:
                    tensor.wait_ge(p.sem["se6"], se0 + i - 1)
                for c4 in range(4):
                    ti = tensor.transpose(
                        pbh[6 + par][:, c4 * 128:(c4 + 1) * 128],
                        sT[:, c4 * T + i * 128:c4 * T + (i + 1) * 128],
                        ident[:])
                ti.then_inc(p.inc("pe6"), 1)

        @block.scalar
        def _(scalar):
            for i in range(NT):
                par = i % 2
                scalar.wait_ge(p.sem["pe6"], pe0 + i + 1)
                if i >= 2:
                    scalar.wait_ge(p.sem["ve6"], ve0 + i - 1)
                scalar.activation(stile[:, par * 512:(par + 1) * 512],
                                  pbh[6 + par][:, 0:512],
                                  AF.Copy).then_inc(p.inc("se6"), 1)

        @block.vector
        def _(vector):
            for i in range(NT):
                par = i % 2
                vector.wait_ge(p.sem["se6"], se0 + i + 1)
                vector.wait_ge(p.sem["dg%d" % par], dg0[par] + 16 * (i // 2 + 1))
                vector.wait_ge(p.sem["dxh%d" % par], dx0[par] + 16 * (i // 2 + 1))
                if i >= 2:
                    vector.wait_ge(p.sem["dout%d" % par],
                                   do0[par] + 16 * ((i - 2) // 2 + 1))
                vector.tensor_tensor(otile[:, par * 512:(par + 1) * 512],
                                     stile[:, par * 512:(par + 1) * 512],
                                     gtile[:, par * 512:(par + 1) * 512],
                                     op=ALU.mult)
                vector.tensor_add(otile[:, par * 512:(par + 1) * 512],
                                  otile[:, par * 512:(par + 1) * 512],
                                  xtile[:, par * 512:(par + 1) * 512]
                                  ).then_inc(p.inc("ve6"), 1)

        @block.sync
        def _(sync):
            for i in range(NT):
                par = i % 2
                sync.wait_ge(p.sem["ve6"], ve0 + i + 1)
                sync.dma_start(o_t[i], otile[:, par * 512:(par + 1) * 512]
                               ).then_inc(p.inc("dout%d" % par, 16), 16)



def _host_inputs(ins):
    x = np.asarray(ins["L0_attn_x"], dtype=np.float32)
    W_dq = np.asarray(ins["L0_attn_W_dq"], dtype=np.float32)
    W_uq = np.asarray(ins["L0_attn_W_uq"], dtype=np.float32)
    W_dkv = np.asarray(ins["L0_attn_W_dkv"], dtype=np.float32)
    W_uk = np.asarray(ins["L0_attn_W_uk"], dtype=np.float32)
    W_uv = np.asarray(ins["L0_attn_W_uv"], dtype=np.float32)
    W_o = np.asarray(ins["L0_attn_W_o"], dtype=np.float32)
    Wq = np.asarray(ins["L0_ffn_Wq"], dtype=np.float32)
    keys = np.asarray(ins["L0_ffn_keys"], dtype=np.float32)
    W_down = np.asarray(ins["L0_ffn_W_down"], dtype=np.float32)
    W_up = np.asarray(ins["L0_ffn_W_up"], dtype=np.float32)

    wdkvT = np.ascontiguousarray(W_dkv.T)
    wdqT = np.ascontiguousarray(W_dq.T)
    wuqT = np.ascontiguousarray(W_uq.T)
    woT = np.ascontiguousarray(W_o.T)
    wqT = np.ascontiguousarray(Wq.T)
    keysT = np.ascontiguousarray(keys.transpose(2, 0, 3, 1))
    wd256T = np.ascontiguousarray(W_down[:256].T)

    ident = np.eye(128, dtype=np.float16)
    masks = np.zeros((5, 128, 512), dtype=np.float16)
    kk = np.arange(128)[:, None]
    qq = np.arange(512)[None, :]
    for j in range(4):
        masks[j] = (128 * j + kk <= qq).astype(np.float16)
    masks[4] = 1.0
    cst = np.zeros((128, 8), dtype=np.float32)
    cst[:, 0] = -4.0
    cst[:, 1] = 1e-5

    in_maps = []
    for c in range(8):
        b, half = c // 2, c % 2
        ch0 = half * 512
        in_maps.append({
            "x": np.ascontiguousarray(x[b]),
            "xh": np.ascontiguousarray(x[b][:, ch0:ch0 + 512]),
            "wdkvT": wdkvT,
            "wdqT": wdqT,
            "wuqT": wuqT,
            "wuqTh": np.ascontiguousarray(wuqT[:, ch0:ch0 + 512]),
            "wdqh": np.ascontiguousarray(W_dq[:, ch0:ch0 + 512]),
            "wuk": np.ascontiguousarray(W_uk),
            "wuv": np.ascontiguousarray(W_uv),
            "woTh": np.ascontiguousarray(woT[:, ch0:ch0 + 512]),
            "wqT": wqT,
            "keysT": keysT,
            "wd256T": wd256T,
            "wup256h": np.ascontiguousarray(W_up[:256, ch0:ch0 + 512]),
            "ident": ident,
            "masks": masks,
            "cst": cst,
        })
    return in_maps


def kernel(**ins):
    in_maps = _host_inputs(ins)
    nc = build()
    res = run_bass_kernel_spmd(nc, in_maps, core_ids=list(range(8)))
    out = np.empty((4, T, C), dtype=np.float32)
    for c in range(8):
        b, half = c // 2, c % 2
        out[b, :, half * 512:(half + 1) * 512] = res.results[c]["out"]
    return out


# version 25 (final)
# speedup vs baseline: 1.2280x; 1.0267x over previous
import numpy as np
from contextlib import ExitStack

import concourse.bass as bass
import concourse.mybir as mybir
from concourse.bass_utils import run_bass_kernel_spmd

F32 = mybir.dt.float32
F16 = mybir.dt.float16
AF = mybir.ActivationFunctionType
ALU = mybir.AluOpType
AX = mybir.AxisListType.X

T = 4096
C = 1024
STAGE = 6
DEBUG = ()
P5_HEADS = 8
P5_QB = 8
P5_YD = True
P5_TAIL = 4


class P:
    def __init__(self, nc, es):
        self.nc = nc
        self.sem = {}
        self.cnt = {}
        self.bar_lvl = 0
        self.es = es

    def mksem(self, *names):
        for n in names:
            self.sem[n] = self.es.enter_context(self.nc.semaphore(n))
            self.cnt[n] = 0

    def inc(self, name, amt=1):
        self.cnt[name] += amt
        return self.sem[name]

    def barrier(self, block):
        self.bar_lvl += 5
        h, lvl = self.sem["bar"], self.bar_lvl
        for dec, eng in ((block.sync, 0), (block.tensor, 0), (block.vector, 1),
                         (block.scalar, 0), (block.gpsimd, 1)):
            @dec
            def _(e, h=h, lvl=lvl, eng=eng):
                if eng:
                    e.engine_nop().then_inc(h, 1)
                else:
                    e.nop().then_inc(h, 1)
                e.wait_ge(h, lvl)


def build(stage=None, debug=None):
    stage = STAGE if stage is None else stage
    debug = DEBUG if debug is None else debug
    nc = bass.Bass(trn_type="TRN2", target_bir_lowering=False)

    def din(name, shape, dtype=F32):
        return nc.dram_tensor(name, shape, dtype, kind="ExternalInput").ap()

    x_d = din("x", [T, C])
    xh_d = din("xh", [T, 512])
    wdkvT_d = din("wdkvT", [C, 512])
    wdqT_d = din("wdqT", [C, 512])
    wuqT_d = din("wuqT", [512, C])
    wuqTh_d = din("wuqTh", [512, 512])
    wdqh_d = din("wdqh", [512, 512])
    wuk_d = din("wuk", [C, 512])
    wuv_d = din("wuv", [C, 512])
    woTh_d = din("woTh", [C, 512])
    wqT_d = din("wqT", [C, 8192])
    keysT_d = din("keysT", [2, 8, 512, 256])
    wd256T_d = din("wd256T", [C, 256])
    wup256h_d = din("wup256h", [256, 512])
    ident_d = din("ident", [128, 128], F16)
    masks_d = din("masks", [5, 128, 512], F16)
    cst_d = din("cst", [128, 8])

    out_d = nc.dram_tensor("out", [T, 512], F32, kind="ExternalOutput").ap()
    g_scr = nc.dram_tensor("g_scr", [T, 512], F16,
                           kind="ExternalOutput" if "g_scr" in debug else "Internal")
    keys16_scr = nc.dram_tensor("keys16_scr", [2, 8, 512, 256], F16).ap()
    wq16_scr = nc.dram_tensor("wq16_scr", [C, 8192], F16).ap()
    scr16 = nc.dram_tensor("scr16", [128, 28 * 1024], F16,
                           kind="ExternalOutput" if "scr16" in debug
                           else "Internal").ap()

    es = ExitStack()
    with es:
        block = es.enter_context(nc.Block())
        p = P(nc, es)
        p.mksem("bar", "dx0", "dx1", "dcst", "dwq0", "dwq1", "dkk", "dg0", "dg1", "dxh0", "dxh1", "dout0", "dout1",
                "a1", "a2", "v2", "v3", "ac1", "pe1", "se1",
                "ge2", "pe2", "se2", "se2s", "ve2m", "ve2k",
                "ve3a", "ae3", "ve3t", "ve3r", "ae3g", "ve3m", "pe3t", "se3",
                "pe3f", "ae3s",
                "ge4", "pe4", "se4",
                "ae5", "pt", "pe5s", "pe5y", "pe5d", "ve5r", "pe5b", "dvt",
                "ve5q", "pe6", "ve6", "se6", "vch",
                "dpc", "gpc", "dps", "gwq", "dkl", "dks", "gks", "dws")

        ident = es.enter_context(nc.sbuf_tensor("identb", [128, 128], F16))
        cst = es.enter_context(nc.sbuf_tensor("cstb", [128, 8], F32))
        st1 = es.enter_context(nc.sbuf_tensor("st1b", [128, 64], F32))
        pb = [es.enter_context(nc.psum_tensor("pb%d" % i, [128, 512], F32))
              for i in range(8)]
        pbh = [b.ap().bitcast(F16) for b in pb]

        dbg = {}

        @block.sync
        def _(sync):
            sync.dma_start(ident[:], ident_d).then_inc(p.inc("dcst", 16), 16)
            sync.dma_start(cst[:], cst_d).then_inc(p.inc("dcst", 16), 16)

        ckvT = es.enter_context(nc.sbuf_tensor("ckvT", [128, 4 * T], F16))
        qattT = es.enter_context(nc.sbuf_tensor("qattT", [128, 4 * T], F16))
        keff = es.enter_context(nc.sbuf_tensor("keff", [128, 4 * 512], F16))
        veff = es.enter_context(nc.sbuf_tensor("veff", [128, 4 * 512], F16))
        for nm, t in (("ckvT", ckvT), ("qattT", qattT),
                      ("keff", keff), ("veff", veff)):
            if nm in debug:
                dbg[nm] = t

        with nc.sbuf_tensor("xnT", [128, 8 * T], F16, side="right") as xnT:
            if "xnT" in debug:
                dbg["xnT"] = xnT
            qf32 = qattT.ap().bitcast(F32)
            xbuf = qf32[:, 0:8192]
            sqscr = ckvT.ap().bitcast(F32)[:, 0:1024]
            xnt = ckvT[:, 2048:6144]
            emit_p1(nc, block, p, x_d, xnT, xbuf, sqscr, xnt, st1,
                    cst, ident, pbh)
            if stage >= 2:
                wsrc = (wdkvT_d, wdqT_d, wuqTh_d, wuqT_d, wdqh_d, wuk_d,
                        wuv_d, woTh_d)
                emit_p23(nc, block, p, xnT, qattT, wqT_d, keysT_d,
                         keys16_scr, wq16_scr, wd256T_d, wup256h_d, g_scr,
                         pb, pbh, ident, dbg, debug, stage, ckvT, scr16,
                         wsrc)
        if stage >= 4:
            emit_p4ab(nc, block, p, scr16, keff, veff, pb)
            p.barrier(block)

        if stage >= 5:
            masks = es.enter_context(nc.sbuf_tensor("masksb", [128, 5 * 512], F16))

            @block.sync
            def _(sync):
                sync.dma_start(masks[:].rearrange("q (m w) -> q m w", m=5),
                               masks_d.rearrange("m q w -> q m w")
                               ).then_inc(p.inc("dcst", 16), 16)
                sync.nop().then_inc(p.inc("bar2" if False else "dcst", 0), 0) if False else None

            @block.tensor
            def _(tensor):
                tensor.wait_ge(p.sem["dcst"], p.cnt["dcst"])

            @block.vector
            def _(vector):
                vector.wait_ge(p.sem["dcst"], p.cnt["dcst"])

            sT = es.enter_context(nc.sbuf_tensor("sT", [128, 4 * T], F16))
            if "sT" in debug:
                dbg["sT"] = sT
            u = es.enter_context(nc.sbuf_tensor("u", [128, 32 * 520], F16))
            qlat = es.enter_context(nc.sbuf_tensor("qlat", [128, 4 * T], F16))
            if "u" in debug:
                dbg["u"] = u
            if "qlat" in debug:
                dbg["qlat"] = qlat
            emit_p4b(nc, block, p, ckvT, veff, u, pb)
            p.barrier(block)
            emit_p5(nc, block, p, ckvT, qattT, keff, u, qlat, sT, masks, cst,
                    pb, pbh, ident, xh_d, g_scr, out_d, do_p6=(stage >= 6))
            p.barrier(block)

        if dbg:
            p.barrier(block)
            dbga = {nm: (t if getattr(t, "tensor", None) is not None else t[:])
                    for nm, t in dbg.items()}
            aps = {nm: nc.dram_tensor("dbg_" + nm, list(a.shape), a.dtype,
                                      kind="ExternalOutput").ap()
                   for nm, a in dbga.items()}

            @block.sync
            def _(sync):
                for nm, a in dbga.items():
                    sync.dma_start(aps[nm], a).then_inc(p.inc("dout0", 16), 16)
                sync.wait_ge(p.sem["dout0"], p.cnt["dout0"])
        if stage < 6:
            @block.gpsimd
            def _(gp):
                gp.wait_ge(p.sem["bar"], p.bar_lvl)

    return nc



def emit_p1(nc, block, p, x_d, xnT, xbuf, sqscr, xnt, st1, cst, ident, pbh):
    NT = T // 128
    x_t = x_d.rearrange("(n q) c -> n q c", q=128)
    stv = st1[:].rearrange("q (j f) -> q j f", j=8)

    def xsl(i):
        return xbuf[:, (i % 8) * C:(i % 8 + 1) * C]

    @block.scalar
    def _(scalar):
        scalar.wait_ge(p.sem["dcst"], 32)
        for i in range(8):
            scalar.dma_start(xsl(i), x_t[i]).then_inc(p.inc("dx0", 16), 16)
        for g in range(NT // 8):
            for j in range(8):
                i = 8 * g + j
                scalar.wait_ge(p.sem["dx0"], 16 * (i + 1))
                if g >= 1 and j == 0:
                    scalar.wait_ge(p.sem["v2"], g)
                scalar.activation(sqscr[:], xsl(i), AF.Square,
                                  accum_out=st1[:, j * 8 + 1:j * 8 + 2]
                                  ).then_inc(p.inc("a1"), 1)
            scalar.wait_ge(p.sem["v2"], g + 1)
            scalar.activation(stv[:, :, 4:5], stv[:, :, 3:4],
                              AF.Ln, bias=cst[:, 1:2])
            scalar.activation(stv[:, :, 5:6], stv[:, :, 4:5],
                              AF.Exp, scale=-0.5).then_inc(p.inc("a2"), 1)
            for j in range(8):
                i = 8 * g + j
                scalar.wait_ge(p.sem["pe1"], i + 1)
                scalar.activation(
                    xnT[:].rearrange("q (c t) -> q c t", c=8)[:, :, i * 128:(i + 1) * 128],
                    pbh[4 + i % 4][:, 0:1024].rearrange("q (c t) -> q c t", c=8),
                    AF.Copy).then_inc(p.inc("se1"), 1)
                if i + 8 < NT:
                    scalar.dma_start(xsl(i + 8), x_t[i + 8]).then_inc(
                        p.inc("dx0", 16), 16)

    @block.vector
    def _(vector):
        def step(inst):
            inst.then_inc(p.inc("vch"), 1)
            vector.wait_ge(p.sem["vch"], p.cnt["vch"])
        for g in range(NT // 8):
            for j in range(8):
                i = 8 * g + j
                vector.wait_ge(p.sem["dx0"], 16 * (i + 1))
                rs = vector.reduce_sum(st1[:, j * 8:j * 8 + 1], xsl(i),
                                       axis=AX)
            step(rs)
            vector.wait_ge(p.sem["a1"], 8 * (g + 1))
            step(vector.tensor_scalar_mul(stv[:, :, 2:3], stv[:, :, 0:1],
                                          1.0 / C))
            step(vector.tensor_tensor(stv[:, :, 3:4], stv[:, :, 2:3],
                                      stv[:, :, 2:3], op=ALU.mult))
            vector.scalar_tensor_tensor(stv[:, :, 3:4], stv[:, :, 1:2],
                                        1.0 / C, stv[:, :, 3:4], op0=ALU.mult,
                                        op1=ALU.subtract).then_inc(
                                            p.inc("v2"), 1)
            vector.wait_ge(p.sem["a2"], g + 1)
            step(vector.scalar_tensor_tensor(
                stv[:, :, 6:7], stv[:, :, 2:3], -1.0,
                stv[:, :, 5:6], op0=ALU.mult, op1=ALU.mult))
            for j in range(8):
                i = 8 * g + j
                if i >= 4:
                    vector.wait_ge(p.sem["pe1"], i - 3)
                vector.scalar_tensor_tensor(
                    xnt[:, (i % 4) * C:(i % 4 + 1) * C], xsl(i),
                    st1[:, j * 8 + 5:j * 8 + 6],
                    st1[:, j * 8 + 6:j * 8 + 7].to_broadcast([128, C]),
                    op0=ALU.mult, op1=ALU.add).then_inc(p.inc("ac1"), 1)

    @block.tensor
    def _(tensor):
        tensor.wait_ge(p.sem["dcst"], p.cnt["dcst"])
        for i in range(NT):
            tensor.wait_ge(p.sem["ac1"], i + 1)
            if i >= 4:
                tensor.wait_ge(p.sem["se1"], i - 3)
            for j in range(8):
                ti = tensor.transpose(
                    pbh[4 + i % 4][:, j * 128:(j + 1) * 128],
                    xnt[:, (i % 4) * C + j * 128:(i % 4) * C + (j + 1) * 128],
                    ident[:])
            ti.then_inc(p.inc("pe1"), 1)



def emit_p23(nc, block, p, xnT, big32, wqT_d, keysT_d, keys16_scr, wq16_scr,
             wd256T_d, wup256h_d, g_scr, pb, pbh, ident, dbg, debug, stage,
             ckvT=None, scr16=None, wsrc=None):
    g_t = g_scr.ap().rearrange("(n q) c -> n q c", q=128)
    big_f = big32.ap().bitcast(F32)
    sxq = [big_f[:, 0:2048], big_f[:, 2048:4096]]
    p3f = big_f[:, 4096:6552]
    hfull = big32[:, 13104:15152]
    with (
        nc.sbuf_tensor("a16s", [128, 8 * 256], F16, side="right") as a16s,
        nc.sbuf_tensor("mrscr", [128, 256], F32, side="right") as mrscr,
        nc.sbuf_tensor("wdk16", [128, 3072], F16, side="right") as wdk16,
        nc.sbuf_tensor("gt16", [128, 2 * 512], F16, side="right") as gt16,
        nc.sbuf_tensor("p3h", [128, 2560], F16, side="right") as p3h,
        nc.sbuf_tensor("f16st", [128, 1024], F16, side="right") as f16st,
    ):
        es2 = ExitStack()
        wqf = es2.enter_context(
            nc.sbuf_tensor("wqf", [128, 2 * 512], F32, side="right"))
        wq32 = es2.enter_context(
            nc.sbuf_tensor("wq32", [128, 2 * 1024], F32, side="right"))
        wq16 = es2.enter_context(
            nc.sbuf_tensor("wq16", [128, 2 * 4096], F16, side="right"))
        keys16 = es2.enter_context(
            nc.sbuf_tensor("keys16", [128, 2 * 1024], F16, side="right"))
        qph = es2.enter_context(
            nc.sbuf_tensor("qph", [128, 4 * 1024], F16, side="right"))
        simf = es2.enter_context(
            nc.sbuf_tensor("simf", [128, 8 * 256], F32, side="right"))
        if "sx" in debug:
            dbg["sx"] = sxq[1]
        if "sx0" in debug:
            dbg["sx0"] = sxq[0]
        if "as" in debug:
            dbg["as"] = p3f
        if "hfull" in debug:
            dbg["hfull"] = hfull
        if "qph" in debug:
            dbg["qph"] = qph
        AS0, ACF, TH, NGM, DN, RC = 0, 2048, 2304, 2432, 2440, 2448
        WSEL, GH, AT2 = 0, 2048, 2304

        _wdt = wd256T_d.rearrange("(a q) k -> q a k", q=128)
        _wut = wup256h_d.rearrange("(a q) c -> q a c", q=128)
        WDK_PIECES = ([( _wdt[:, 2 * j:2 * j + 2, :], 2, 512 * j)
                       for j in range(4)] +
                      [(_wut[:, j:j + 1, :], 1, 2048 + 512 * j)
                       for j in range(2)])

        def p3_bases():
            return {n: p.cnt[n] for n in ("ve3a", "ae3", "ve3t", "ve3r",
                                          "ae3g", "ve3m", "pe3t", "se3",
                                          "pe3f", "ae3s", "dg0", "dg1")}

        def p3_vector_tile(vector, tt, B, sx):
            base = tt * 256
            v_i = sx[:, base:base + 128].rearrange(
                "q (h i) -> q h i", h=8).unsqueeze(3).to_broadcast(
                [128, 8, 16, 16])
            v_j = sx[:, base + 128:base + 256].rearrange(
                "q (h j) -> q h j", h=8).unsqueeze(2).to_broadcast(
                [128, 8, 16, 16])
            vector.tensor_tensor(
                p3f[:, AS0:AS0 + 2048].rearrange("q (h i j) -> q h i j",
                                                 h=8, i=16),
                v_i, v_j, op=ALU.add)
            sx_h = sx[:, base:base + 128].rearrange("q (h i) -> q h i",
                                                    h=8)[:, :, 0:1]
            sy_h = sx[:, base + 128:base + 256].rearrange("q (h i) -> q h i",
                                                          h=8)[:, :, 0:1]
            vector.scalar_tensor_tensor(
                p3f[:, NGM:NGM + 8].unsqueeze(2), sx_h, -1.0, sy_h,
                op0=ALU.mult, op1=ALU.subtract).then_inc(p.inc("ve3a"), 1)
            for hh in range(8):
                vi = vector.max(out=p3f[:, TH + hh * 16:TH + hh * 16 + 8],
                                in_=p3f[:, AS0 + hh * 256:AS0 + (hh + 1) * 256])
            vi.then_inc(p.inc("ve3t"), 1)
            vector.wait_ge(p.sem["ve3t"], p.cnt["ve3t"])
            for hh in range(8):
                vector.match_replace(
                    out=mrscr[:],
                    in_to_replace=p3f[:, TH + hh * 16:TH + hh * 16 + 8],
                    in_values=p3f[:, AS0 + hh * 256:AS0 + (hh + 1) * 256],
                    imm_value=-1e30)
                vi = vector.max(out=p3f[:, TH + hh * 16 + 8:TH + hh * 16 + 16],
                                in_=mrscr[:])
            vi.then_inc(p.inc("ve3t"), 1)
            vector.wait_ge(p.sem["ve3t"], p.cnt["ve3t"])
            vector.wait_ge(p.sem["ae3"], B["ae3"] + tt + 1)
            for hh in range(8):
                vi = vector.scalar_tensor_tensor(
                    p3h[:, WSEL + hh * 256:WSEL + (hh + 1) * 256],
                    p3f[:, AS0 + hh * 256:AS0 + (hh + 1) * 256],
                    p3f[:, TH + hh * 16 + 15:TH + hh * 16 + 16],
                    p3h[:, WSEL + hh * 256:WSEL + (hh + 1) * 256],
                    op0=ALU.is_ge, op1=ALU.mult,
                    accum_out=p3f[:, DN + hh:DN + hh + 1])
            vi.then_inc(p.inc("ve3r"), 1)
            vector.wait_ge(p.sem["ve3r"], p.cnt["ve3r"])
            vector.reciprocal(p3f[:, RC:RC + 8],
                              p3f[:, DN:DN + 8]).then_inc(p.inc("ve3r"), 1)
            vector.wait_ge(p.sem["ve3r"], p.cnt["ve3r"])
            for hh in range(8):
                if hh == 0:
                    vector.tensor_scalar(
                        p3f[:, ACF:ACF + 256], p3h[:, WSEL:WSEL + 256],
                        p3f[:, RC:RC + 1], None, op0=ALU.mult)
                else:
                    vector.scalar_tensor_tensor(
                        p3f[:, ACF:ACF + 256],
                        p3h[:, WSEL + hh * 256:WSEL + (hh + 1) * 256],
                        p3f[:, RC + hh:RC + hh + 1],
                        p3f[:, ACF:ACF + 256], op0=ALU.mult, op1=ALU.add)
            vector.wait_ge(p.sem["ae3g"], B["ae3g"] + tt + 1)
            if B["pe3t"] >= 8:
                vector.wait_ge(p.sem["pe3t"], B["pe3t"] - 8 + tt + 1)
            vector.tensor_tensor(a16s[:, tt * 256:(tt + 1) * 256],
                                 p3h[:, GH:GH + 256],
                                 p3f[:, ACF:ACF + 256],
                                 op=ALU.mult).then_inc(p.inc("ve3m"), 1)

        def p3_scalar_a(scalar, tt, B):
            scalar.wait_ge(p.sem["ve3a"], B["ve3a"] + tt + 1)
            for hh in range(8):
                si_ = scalar.activation(
                    p3h[:, WSEL + hh * 256:WSEL + (hh + 1) * 256],
                    p3f[:, AS0 + hh * 256:AS0 + (hh + 1) * 256],
                    AF.Exp, bias=p3f[:, NGM + hh:NGM + hh + 1])
            si_.then_inc(p.inc("ae3"), 1)
            scalar.activation(p3h[:, GH:GH + 256],
                              hfull[:, tt * 256:(tt + 1) * 256],
                              AF.Gelu).then_inc(p.inc("ae3g"), 1)

        def p3_tail(block, tq, B):
            @block.tensor
            def _(tensor):
                for tt in range(8):
                    tensor.wait_ge(p.sem["ve3m"], B["ve3m"] + tt + 1)
                    if B["se3"] + tt > 0:
                        tensor.wait_ge(p.sem["se3"], B["se3"] + tt)
                    for rc in range(2):
                        ti = tensor.transpose(
                            pbh[7][:, rc * 128:(rc + 1) * 128],
                            a16s[:, tt * 256 + rc * 128:tt * 256 + (rc + 1) * 128],
                            ident[:])
                    ti.then_inc(p.inc("pe3t"), 1)
                    tensor.wait_ge(p.sem["se3"], B["se3"] + tt + 1)
                    if B["ae3s"] + tt > 0:
                        tensor.wait_ge(p.sem["ae3s"], B["ae3s"] + tt)
                    for rc in range(2):
                        mm = tensor.matmul(
                            pb[6][:],
                            p3h[:, AT2 + rc * 128:AT2 + (rc + 1) * 128],
                            wdk16[:, 2048 + rc * 512:2048 + (rc + 1) * 512],
                            start=(rc == 0), stop=(rc == 1))
                    mm.then_inc(p.inc("pe3f"), 1)

            @block.scalar
            def _(scalar):
                for tt in range(8):
                    scalar.wait_ge(p.sem["pe3t"], B["pe3t"] + tt + 1)
                    scalar.activation(p3h[:, AT2:AT2 + 256], pbh[7][:, 0:256],
                                      AF.Copy).then_inc(p.inc("se3"), 1)
                    scalar.wait_ge(p.sem["pe3f"], B["pe3f"] + tt + 1)
                    par = tt % 2
                    if tt >= 2:
                        scalar.wait_ge(p.sem["dg%d" % par],
                                       B["dg%d" % par] + 16 * ((tt - 2) // 2 + 1))
                    elif B["dg%d" % par] > 0:
                        scalar.wait_ge(p.sem["dg%d" % par], B["dg%d" % par])
                    scalar.activation(gt16[:, par * 512:(par + 1) * 512],
                                      pb[6][:], AF.Silu).then_inc(
                                          p.inc("ae3s"), 1)

            @block.gpsimd
            def _(gp):
                for tt in range(8):
                    gp.wait_ge(p.sem["ae3s"], B["ae3s"] + tt + 1)
                    gp.dma_start(
                        g_t[tq * 8 + tt],
                        gt16[:, (tt % 2) * 512:(tt % 2 + 1) * 512]
                        ).then_inc(p.inc("dg%d" % (tt % 2), 16), 16)

        def p3t_pe_t(tensor, tt, B):
            tensor.wait_ge(p.sem["ve3m"], B["ve3m"] + tt + 1)
            if B["se3"] + tt > 0:
                tensor.wait_ge(p.sem["se3"], B["se3"] + tt)
            for rc in range(2):
                ti = tensor.transpose(
                    pbh[7][:, rc * 128:(rc + 1) * 128],
                    a16s[:, tt * 256 + rc * 128:tt * 256 + (rc + 1) * 128],
                    ident[:])
            ti.then_inc(p.inc("pe3t"), 1)

        def p3t_pe_f(tensor, tt, B):
            tensor.wait_ge(p.sem["se3"], B["se3"] + tt + 1)
            if B["ae3s"] + tt > 0:
                tensor.wait_ge(p.sem["ae3s"], B["ae3s"] + tt)
            for rc in range(2):
                mmx = tensor.matmul(
                    pb[6][:],
                    p3h[:, AT2 + rc * 128:AT2 + (rc + 1) * 128],
                    wdk16[:, 2048 + rc * 512:2048 + (rc + 1) * 512],
                    start=(rc == 0), stop=(rc == 1))
            mmx.then_inc(p.inc("pe3f"), 1)

        def p3t_sc_at2(scalar, tt, B):
            scalar.wait_ge(p.sem["pe3t"], B["pe3t"] + tt + 1)
            scalar.activation(p3h[:, AT2:AT2 + 256], pbh[7][:, 0:256],
                              AF.Copy).then_inc(p.inc("se3"), 1)

        def p3t_sc_silu(scalar, tt, B):
            scalar.wait_ge(p.sem["pe3f"], B["pe3f"] + tt + 1)
            par = tt % 2
            if tt >= 2:
                scalar.wait_ge(p.sem["dg%d" % par],
                               B["dg%d" % par] + 16 * ((tt - 2) // 2 + 1))
            elif B["dg%d" % par] > 0:
                scalar.wait_ge(p.sem["dg%d" % par], B["dg%d" % par])
            scalar.activation(gt16[:, par * 512:(par + 1) * 512],
                              pb[6][:], AF.Silu).then_inc(
                                  p.inc("ae3s"), 1)

        if wsrc is not None:
            (wdkvT_d, wdqT_d, wuqTh_d, wuqT_d, wdqh_d, wuk_d,
             wuv_d, woTh_d) = wsrc
            PIECES = (
                [(wdkvT_d, 2 * j, 2, j * 1024) for j in range(4)] +
                [(wdqT_d, 2 * j, 2, 4096 + j * 1024) for j in range(4)] +
                [(wuqTh_d, 2 * j, 2, 8192 + j * 1024) for j in range(2)] +
                [(wuqT_d, j, 1, 10240 + j * 1024) for j in range(4)] +
                [(wdqh_d, 2 * j, 2, 14336 + j * 1024) for j in range(2)] +
                [(wuk_d, 2 * j, 2, 16384 + j * 1024) for j in range(4)] +
                [(wuv_d, 2 * j, 2, 20480 + j * 1024) for j in range(4)] +
                [(woTh_d, 2 * j, 2, 24576 + j * 1024) for j in range(4)])
        else:
            PIECES = []

        def precast_gap(pieces):
            @block.gpsimd
            def _(gp, pieces=pieces):
                for (src, clo, nch, dcol) in pieces:
                    st = src.rearrange("(a q) k -> q a k", q=128)
                    gp.dma_start(
                        wqf[:].rearrange("q (a k) -> q a k", a=nch),
                        st[:, clo:clo + nch, :]).then_inc(p.inc("dpc", 16), 16)
                    gp.wait_ge(p.sem["dpc"], p.cnt["dpc"])
                    if p.cnt["dps"] > 0:
                        gp.wait_ge(p.sem["dps"], p.cnt["dps"])
                    gp.tensor_copy(f16st[:], wqf[:]).then_inc(p.inc("gpc"), 1)
                    gp.dma_start(scr16[:, dcol:dcol + 1024],
                                 f16st[:]).then_inc(p.inc("dps", 16), 16)

        wt32 = wqT_d.rearrange("(a q) r -> q a r", q=128)
        kt32 = keysT_d.rearrange("p h (a q) k -> q (p h a) k", q=128)
        ks16 = keys16_scr.rearrange("p h (a q) k -> q (p h a) k", q=128)
        dwq_b = (p.cnt["dwq0"], p.cnt["dwq1"])
        gwq_b = p.cnt["gwq"]
        pe0s = []

        def wq_load_wait(G, qp):
            u = 4 * G + qp
            return gwq_b + u - 1 if u >= 2 else None

        def wq_cast_unit(eng, G, qp):
            par = G % 2
            eng.wait_ge(p.sem["dwq%d" % par],
                        dwq_b[par] + 64 * (G // 2) + 16 * (qp + 1))
            dst = wq16[:, par * 4096 + qp * 1024:par * 4096 + (qp + 1) * 1024]
            src = wq32[:, (qp % 2) * 1024:(qp % 2 + 1) * 1024]
            if hasattr(eng, "activation"):
                eng.activation(dst, src, AF.Copy).then_inc(p.inc("gwq"), 1)
            else:
                eng.tensor_copy(dst, src).then_inc(p.inc("gwq"), 1)

        if True:
            p.barrier(block)
        prevB = None
        prev_tq = None
        for tq in range(4):
            t0 = tq * 1024
            dk0 = p.cnt["dkk"]
            pe0 = p.cnt["pe2"]
            pe0s.append(pe0)
            se0 = p.cnt["se2"]
            ses0 = p.cnt["se2s"]
            vem0 = p.cnt["ve2m"]

            dwqb = (p.cnt["dwq0"], p.cnt["dwq1"])

            @block.sync
            def _(sync, tq=tq, pe0=pe0):
                wq16s = wq16_scr.rearrange("(a q) r -> q a r", q=128)
                for ph in range(16):
                    pp, hh = ph // 8, ph % 8
                    par = ph % 2
                    r0 = pp * 4096 + hh * 512
                    G = 16 * tq + ph
                    if tq == 0:
                        for qp in range(4):
                            w = wq_load_wait(G, qp)
                            if w is not None:
                                sync.wait_ge(p.sem["gwq"], w)
                            sync.dma_start(
                                wq32[:, (qp % 2) * 1024:(qp % 2 + 1) * 1024]
                                .rearrange("q (a r) -> q a r", a=2),
                                wt32[:, 2 * qp:2 * qp + 2, r0:r0 + 512]
                                ).then_inc(p.inc("dwq%d" % par, 16), 16)
                    else:
                        if ph >= 2:
                            sync.wait_ge(p.sem["pe2"], pe0 + (ph - 1) * 16)
                        else:
                            sync.wait_ge(p.sem["pe2"], pe0 - 24 + 16 * ph)
                        if tq == 1 and ph == 0:
                            sync.wait_ge(p.sem["dws"], 256)
                        sync.dma_start(
                            wq16[:, par * 4096:(par + 1) * 4096].rearrange(
                                "q (a r) -> q a r", a=8),
                            wq16s[:, :, r0:r0 + 512]
                            ).then_inc(p.inc("dwq%d" % par, 16), 16)
                        if tq == 1 and ph == 0:
                            sync.wait_ge(p.sem["dks"], 256)
                        sync.dma_start(
                            keys16[:, par * 1024:(par + 1) * 1024].rearrange(
                                "q (a k) -> q a k", a=4),
                            keys16_scr[pp, hh].rearrange(
                                "(a q) k -> q a k", q=128)
                            ).then_inc(p.inc("dkk", 16), 16)
            if tq == 0:
                @block.gpsimd
                def _(gp, pe0=pe0):
                    wq16s = wq16_scr.rearrange("(a q) r -> q a r", q=128)
                    for ph in range(16):
                        pp, hh = ph // 8, ph % 8
                        par = ph % 2
                        r0 = pp * 4096 + hh * 512
                        gp.dma_start(wqf[:].rearrange("q (a k) -> q a k", a=4),
                                     kt32[:, 4 * ph:4 * ph + 4, :]
                                     ).then_inc(p.inc("dkl", 16), 16)
                        gp.wait_ge(p.sem["dkl"], p.cnt["dkl"])
                        if ph >= 2:
                            gp.wait_ge(p.sem["pe2"], pe0 + (ph - 1) * 16)
                            gp.wait_ge(p.sem["dks"], 16 * (ph - 1))
                        gp.tensor_copy(keys16[:, par * 1024:(par + 1) * 1024],
                                       wqf[:]).then_inc(p.inc("gks"), 1)
                        gp.dma_start(ks16[:, 4 * ph:4 * ph + 4, :],
                                     keys16[:, par * 1024:(par + 1) * 1024]
                                     .rearrange("q (a k) -> q a k", a=4)
                                     ).then_inc(p.inc("dks", 16), 16)
                        if ph < 2:
                            for qp in range(4):
                                wq_cast_unit(gp, ph, qp)
                        else:
                            gp.wait_ge(p.sem["gwq"], gwq_b + 4 * (ph + 1))
                        gp.dma_start(wq16s[:, :, r0:r0 + 512],
                                     wq16[:, par * 4096:(par + 1) * 4096]
                                     .rearrange("q (a r) -> q a r", a=8)
                                     ).then_inc(p.inc("dws", 16), 16)
                        if 10 <= ph and ph - 10 < len(WDK_PIECES):
                            src, nch, col = WDK_PIECES[ph - 10]
                            gp.dma_start(wqf[:, 0:512].rearrange(
                                "q (a k) -> q a k", a=nch), src).then_inc(
                                    p.inc("dpc", 16), 16)
                            gp.wait_ge(p.sem["dpc"], p.cnt["dpc"])
                            gp.tensor_copy(wdk16[:, col:col + 512],
                                           wqf[:, 0:512]).then_inc(
                                               p.inc("ge2"), 1)

            @block.tensor
            def _(tensor, tq=tq, t0=t0, dk0=dk0, se0=se0, ses0=ses0,
                  dwqb=dwqb):
                for ph in range(16):
                    par = ph % 2
                    if ph == 0:
                        tensor.wait_ge(p.sem["se1"], 8 * (tq + 1))
                    if tq == 0:
                        tensor.wait_ge(p.sem["gwq"], gwq_b + 4 * (ph + 1))
                        tensor.wait_ge(p.sem["gks"], ph + 1)
                    else:
                        tensor.wait_ge(p.sem["dwq%d" % par],
                                       dwqb[par] + 16 * (ph // 2 + 1))
                        tensor.wait_ge(p.sem["dkk"], dk0 + 16 * (ph + 1))
                    for dkc in range(4):
                        for tc in range(2):
                            gi = ph * 8 + dkc * 2 + tc
                            if gi >= 2:
                                tensor.wait_ge(p.sem["se2"], se0 + gi - 1)
                            elif se0 + gi - 9 > 0:
                                tensor.wait_ge(p.sem["se2"], se0 + gi - 9)
                            for cc in range(8):
                                mm = tensor.matmul(
                                    pb[gi % 2][:],
                                    wq16[:, par * 4096 + cc * 512 + dkc * 128:
                                         par * 4096 + cc * 512 + (dkc + 1) * 128],
                                    xnT[:, cc * T + t0 + tc * 512:
                                        cc * T + t0 + (tc + 1) * 512],
                                    start=(cc == 0), stop=(cc == 7))
                            mm.then_inc(p.inc("pe2"), 1)
                    for tt in range(8):
                        si = ph * 8 + tt
                        tensor.wait_ge(p.sem["se2"], se0 + ph * 8 + 7 + tt // 4)
                        if si >= 2:
                            tensor.wait_ge(p.sem["se2s"], ses0 + si - 1)
                        elif ses0 + si - 1 > 0:
                            tensor.wait_ge(p.sem["se2s"], ses0 + si - 1)
                        for dkc in range(4):
                            mm = tensor.matmul(
                                pb[2 + si % 2][:, 0:256],
                                qph[:, dkc * 1024 + tt * 128:dkc * 1024 + (tt + 1) * 128],
                                keys16[:, par * 1024 + dkc * 256:
                                       par * 1024 + (dkc + 1) * 256],
                                start=(dkc == 0), stop=(dkc == 3))
                        mm.then_inc(p.inc("pe2"), 1)

            @block.scalar
            def _(scalar, pe0=pe0, vem0=vem0, prevB=prevB):
                for ph in range(16):
                    for gi8 in range(8):
                        dkc, tc = gi8 // 2, gi8 % 2
                        g = ph * 8 + gi8
                        scalar.wait_ge(p.sem["pe2"], pe0 + ph * 16 + gi8 + 1)
                        scalar.activation(
                            qph[:, dkc * 1024 + tc * 512:dkc * 1024 + (tc + 1) * 512],
                            pb[g % 2][:], AF.Copy).then_inc(p.inc("se2"), 1)
                    for tt in range(8):
                        si = ph * 8 + tt
                        scalar.wait_ge(p.sem["pe2"], pe0 + ph * 16 + 8 + tt + 1)
                        if si >= 8:
                            scalar.wait_ge(p.sem["ve2m"], vem0 + si - 7)
                        scalar.activation(
                            simf[:, (si % 8) * 256:(si % 8 + 1) * 256],
                            pb[2 + si % 2][:, 0:256],
                            AF.Copy).then_inc(p.inc("se2s"), 1)
                    if prevB is not None and ph % 2 == 1:
                        p3_scalar_a(scalar, ph // 2, prevB)
                    if tq == 0 and ph + 2 <= 15:
                        G2 = ph + 2
                        if G2 == 2:
                            scalar.wait_ge(p.sem["gwq"], gwq_b + 8)
                        scalar.wait_ge(p.sem["dws"], 16 * (G2 - 1))
                        for qp in range(4):
                            wq_cast_unit(scalar, G2, qp)

            @block.vector
            def _(vector, tq=tq, ses0=ses0, prevB=prevB):
                sx = sxq[tq % 2]
                for ph in range(16):
                    pp, hh = ph // 8, ph % 8
                    for tt in range(8):
                        si = ph * 8 + tt
                        col = tt * 256 + pp * 128 + hh * 16
                        vector.wait_ge(p.sem["se2s"], ses0 + si + 1)
                        vector.max(out=sx[:, col:col + 8],
                                   in_=simf[:, (si % 8) * 256:(si % 8 + 1) * 256]
                                   ).then_inc(p.inc("ve2k"), 1)
                        vector.wait_ge(p.sem["ve2k"], p.cnt["ve2k"])
                        vector.match_replace(
                            out=mrscr[:], in_to_replace=sx[:, col:col + 8],
                            in_values=simf[:, (si % 8) * 256:(si % 8 + 1) * 256],
                            imm_value=-1e30).then_inc(p.inc("ve2m"), 1)
                        vector.max(out=sx[:, col + 8:col + 16],
                                   in_=mrscr[:]).then_inc(p.inc("ve2k"), 1)
                    if prevB is not None and ph % 2 == 1:
                        p3_vector_tile(vector, ph // 2, prevB, sxq[(tq + 1) % 2])

            if prevB is not None:
                p3_tail(block, prev_tq, prevB)

            peh0 = p.cnt["pe2"]
            seh0 = p.cnt["se2"]

            @block.tensor
            def _(tensor, t0=t0, peh0=peh0, seh0=seh0, ge2f=p.cnt["ge2"]):
                for tt in range(8):
                    if tt == 0:
                        tensor.wait_ge(p.sem["ge2"], ge2f)
                    if tt >= 2:
                        tensor.wait_ge(p.sem["se2"], seh0 + tt - 1)
                    elif seh0 - 129 + tt > 0:
                        tensor.wait_ge(p.sem["se2"], seh0 - 129 + tt)
                    for cc in range(8):
                        mm = tensor.matmul(
                            pb[4 + tt % 2][:, 0:256],
                            xnT[:, cc * T + t0 + tt * 128:cc * T + t0 + (tt + 1) * 128],
                            wdk16[:, cc * 256:(cc + 1) * 256],
                            start=(cc == 0), stop=(cc == 7))
                    mm.then_inc(p.inc("pe2"), 1)

            @block.scalar
            def _(scalar, peh0=peh0):
                for tt in range(8):
                    scalar.wait_ge(p.sem["pe2"], peh0 + tt + 1)
                    scalar.activation(hfull[:, tt * 256:(tt + 1) * 256],
                                      pb[4 + tt % 2][:, 0:256],
                                      AF.Copy).then_inc(p.inc("se2"), 1)

            if PIECES and tq < 3:
                precast_gap(PIECES[(0, 10, 20)[tq]:(10, 20, 28)[tq]])
            prevB = p3_bases()
            prev_tq = tq
            if stage < 3:
                prevB = None

        if prevB is not None and stage < 4:
            @block.vector
            def _(vector, prevB=prevB):
                for tt in range(8):
                    p3_vector_tile(vector, tt, prevB, sxq[1])

            @block.scalar
            def _(scalar, prevB=prevB):
                for tt in range(8):
                    p3_scalar_a(scalar, tt, prevB)

            p3_tail(block, 3, prevB)
            es2.close()
        elif prevB is not None:
            B3 = prevB
            fin = {n: p.cnt[n] for n in ("se2", "se2s", "pe2", "ve2k", "dps")}
            es2.close()
            with (
                nc.sbuf_tensor("wdkv16", [128, 4096], F16,
                               side="right") as wdkv16,
                nc.sbuf_tensor("wdqT16", [128, 4096], F16,
                               side="right") as wdqT16,
                nc.sbuf_tensor("wuqh16", [128, 2048], F16,
                               side="right") as wuqh16,
                nc.sbuf_tensor("t1tmp", [128, 2 * 2048], F16,
                               side="right") as t1tmp,
            ):
                dxb = (p.cnt["dx0"], p.cnt["dx1"])

                @block.sync
                def _(sync):
                    sync.wait_ge(p.sem["dps"], fin["dps"])
                    sync.wait_ge(p.sem["pe2"], fin["pe2"])
                    sync.wait_ge(p.sem["ve2k"], fin["ve2k"])
                    for i, (dst, lo) in enumerate((
                            (wdkv16[:, 0:2048], 0),
                            (wdkv16[:, 2048:4096], 2048),
                            (wdqT16[:, 0:2048], 4096),
                            (wdqT16[:, 2048:4096], 6144),
                            (wuqh16[:, 0:2048], 8192))):
                        sync.dma_start(dst, scr16[:, lo:lo + 2048]).then_inc(
                            p.inc("dx%d" % (i % 2), 16), 16)

                mspec = []

                def add_ckv(tc, lc):
                    w = []
                    if not mspec:
                        w = [("dx0", dxb[0] + 16), ("dx1", dxb[1] + 16)]
                    if len(mspec) < 4:
                        w = w + [("se2", fin["se2"]), ("se2s", fin["se2s"])]
                    mspec.append((
                        [(wdkv16, cc * 512 + lc * 128, xnT,
                          cc * T + tc * 512) for cc in range(8)],
                        ckvT, lc * T + tc * 512, w))

                def add_t1(tc, qlc):
                    w = []
                    if tc == 0 and qlc == 0:
                        w = [("dx0", dxb[0] + 32), ("dx1", dxb[1] + 32)]
                    mspec.append((
                        [(wdqT16, cc * 512 + qlc * 128, xnT,
                          cc * T + tc * 512) for cc in range(8)],
                        t1tmp, (tc % 2) * 2048 + qlc * 512, w))

                def add_qatt(tc, rc):
                    w = []
                    if tc == 0 and rc == 0:
                        w = [("dx0", dxb[0] + 48)]
                    mspec.append((
                        [(wuqh16, qlc * 512 + rc * 128, t1tmp,
                          (tc % 2) * 2048 + qlc * 512) for qlc in range(4)],
                        big32, rc * T + tc * 512, w))

                for tc in range(8):
                    for lc in range(4):
                        add_ckv(tc, lc)
                for tc in range(2):
                    for qlc in range(4):
                        add_t1(tc, qlc)
                for tc in range(8):
                    for rc in range(4):
                        add_qatt(tc, rc)
                    if tc + 2 < 8:
                        for qlc in range(4):
                            add_t1(tc + 2, qlc)
                pe_hooks = {22: [(p3t_pe_t, 0)], 26: [(p3t_pe_f, 0)],
                            36: [(p3t_pe_t, 1)], 38: [(p3t_pe_f, 1)],
                            40: [(f, tt) for tt in range(2, 8)
                                 for f in (p3t_pe_t, p3t_pe_f)]}
                sc_hooks = {7: [(p3_scalar_a, 0)], 20: [(p3_scalar_a, 1)],
                            24: [(p3t_sc_at2, 0)],
                            26: [(p3t_sc_silu, 0)],
                            32: [(p3_scalar_a, 2)],
                            36: [(p3_scalar_a, 3)],
                            38: [(p3t_sc_at2, 1)],
                            40: ([(p3_scalar_a, tt) for tt in range(4, 8)] +
                                 [(p3t_sc_silu, 1)] +
                                 [(f, tt) for tt in range(2, 8)
                                  for f in (p3t_sc_at2, p3t_sc_silu)])}
                pe40 = p.cnt["pe4"]
                se40 = p.cnt["se4"]

                @block.tensor
                def _(tensor):
                    for gi, (mms, _, _c, waits) in enumerate(mspec):
                        for (fn, tt) in pe_hooks.get(gi, ()):
                            fn(tensor, tt, B3)
                        for (sname, val) in waits:
                            tensor.wait_ge(p.sem[sname], val)
                        if gi >= 4:
                            tensor.wait_ge(p.sem["se4"], se40 + gi - 3)
                        for mi, (lt, lcol, rt, rcol) in enumerate(mms):
                            m = tensor.matmul(
                                pb[gi % 4][:], lt[:, lcol:lcol + 128],
                                rt[:, rcol:rcol + 512], start=(mi == 0),
                                stop=(mi == len(mms) - 1))
                        m.then_inc(p.inc("pe4"), 1)

                @block.scalar
                def _(scalar):
                    for gi, (_, dt_, dcol, _w) in enumerate(mspec):
                        for (fn, tt) in sc_hooks.get(gi, ()):
                            fn(scalar, tt, B3)
                        scalar.wait_ge(p.sem["pe4"], pe40 + gi + 1)
                        scalar.activation(dt_[:, dcol:dcol + 512],
                                          pb[gi % 4][:],
                                          AF.Copy).then_inc(p.inc("se4"), 1)

                @block.vector
                def _(vector):
                    for tt in range(8):
                        p3_vector_tile(vector, tt, B3, sxq[1])

                @block.gpsimd
                def _(gp):
                    for tt in range(8):
                        gp.wait_ge(p.sem["ae3s"], B3["ae3s"] + tt + 1)
                        gp.dma_start(
                            g_t[3 * 8 + tt],
                            gt16[:, (tt % 2) * 512:(tt % 2 + 1) * 512]
                            ).then_inc(p.inc("dg%d" % (tt % 2), 16), 16)
        else:
            es2.close()
        p.barrier(block)



def _mm_groups(nc, block, p, glist, evacs, pb):
    pe0 = p.cnt["pe4"]
    se0 = p.cnt["se4"]

    @block.tensor
    def _(tensor):
        for gi, mms in enumerate(glist):
            for (sem, val) in mms[0][5]:
                tensor.wait_ge(p.sem[sem], val)
            if gi >= 4:
                tensor.wait_ge(p.sem["se4"], se0 + gi - 3)
            for mi, (lt, lcol, rt, rcol, w, _) in enumerate(mms):
                mm = tensor.matmul(pb[gi % 4][:, 0:w],
                                   lt[:, lcol:lcol + 128],
                                   rt[:, rcol:rcol + w],
                                   start=(mi == 0), stop=(mi == len(mms) - 1))
            mm.then_inc(p.inc("pe4"), 1)

    @block.scalar
    def _(scalar):
        for gi, (dt_, dcol, w) in enumerate(evacs):
            scalar.wait_ge(p.sem["pe4"], pe0 + gi + 1)
            scalar.activation(dt_[:, dcol:dcol + w], pb[gi % 4][:, 0:w],
                              AF.Copy).then_inc(p.inc("se4"), 1)


def emit_p4ab(nc, block, p, scr16, keff, veff, pb):
    with (
        nc.sbuf_tensor("wuqA16", [128, 4096], F16, side="right") as wuqA16,
        nc.sbuf_tensor("wdqh16", [128, 2048], F16, side="right") as wdqh16,
        nc.sbuf_tensor("wuk16", [128, 4096], F16, side="right") as wuk16,
        nc.sbuf_tensor("at16", [128, 4096], F16, side="right") as at16,
        nc.sbuf_tensor("wuv16", [128, 4096], F16, side="right") as wuv16,
        nc.sbuf_tensor("woTh16", [128, 4096], F16, side="right") as woTh16,
    ):
        pe4f = p.cnt["pe4"]
        se4f = p.cnt["se4"]
        qn = ("dx0", "dx1", "dwq0", "dwq1")
        loads = [(wuqA16[:, 0:2048], 10240), (wuqA16[:, 2048:4096], 12288),
                 (wdqh16[:, 0:2048], 14336),
                 (wuk16[:, 0:2048], 16384), (wuk16[:, 2048:4096], 18432),
                 (wuv16[:, 0:2048], 20480), (wuv16[:, 2048:4096], 22528),
                 (woTh16[:, 0:2048], 24576), (woTh16[:, 2048:4096], 26624)]
        dcnt = []

        @block.sync
        def _(sync):
            sync.wait_ge(p.sem["pe4"], pe4f)
            for i, (dst, lo) in enumerate(loads):
                nm = qn[i % 4]
                sync.dma_start(dst, scr16[:, lo:lo + 2048]).then_inc(
                    p.inc(nm, 16), 16)
                dcnt.append((nm, p.cnt[nm]))

        gwait = [dcnt[0], dcnt[1], dcnt[2], ("se4", se4f)]
        glist = []
        evacs = []
        for cpc in range(8):
            glist.append([(wuqA16, qlc * 1024 + cpc * 128, wdqh16,
                           qlc * 512, 512, gwait) for qlc in range(4)])
            evacs.append((at16, cpc * 512, 512))
        se_at = p.cnt["se4"] + 8
        kwait = [("se4", se_at), dcnt[3], dcnt[4]]
        for rc in range(4):
            glist.append([(at16, cpc * 512 + rc * 128, wuk16, cpc * 512,
                           512, kwait) for cpc in range(8)])
            evacs.append((keff, rc * 512, 512))
        vwait = [dcnt[5], dcnt[6], dcnt[7], dcnt[8]]
        for vlc in range(4):
            glist.append([(wuv16, cc * 512 + vlc * 128, woTh16, cc * 512,
                           512, vwait) for cc in range(8)])
            evacs.append((veff, vlc * 512, 512))
        _mm_groups(nc, block, p, glist, evacs, pb)


def emit_p4b(nc, block, p, ckvT, veff, u, pb):
    pe0 = p.cnt["pe4"]
    se0 = p.cnt["se4"]

    @block.tensor
    def _(tensor):
        for kc in range(32):
            if kc >= 4:
                tensor.wait_ge(p.sem["se4"], se0 + kc - 3)
            for lc in range(4):
                mm = tensor.matmul(pb[kc % 4][:],
                                   ckvT[:, lc * T + kc * 128:lc * T + (kc + 1) * 128],
                                   veff[:, lc * 512:(lc + 1) * 512],
                                   start=(lc == 0), stop=(lc == 3))
            mm.then_inc(p.inc("pe4"), 1)

    @block.scalar
    def _(scalar):
        for kc in range(32):
            scalar.wait_ge(p.sem["pe4"], pe0 + kc + 1)
            scalar.activation(
                u[:, kc * 520:kc * 520 + 520].rearrange(
                    "q (h w) -> q h w", h=8)[:, :, 0:64],
                pb[kc % 4][:].rearrange("q (h w) -> q h w", h=8),
                AF.Copy).then_inc(p.inc("se4"), 1)

    @block.vector
    def _(vector):
        vector.memset(
            u[:].rearrange("q (k h w) -> q k h w", k=32, h=8)[:, :, :, 64:65],
            1.0)



def emit_p5(nc, block, p, ckvT, qattT, keff, u, qlat, sT, masks, cst, pb,
            pbh, ident, xh_d, g_scr, out_d, do_p6=True):
    with (
        nc.sbuf_tensor("ptil", [128, 4 * 512], F16, side="right") as ptil,
        nc.sbuf_tensor("rcp", [1, 2 * 512], F16, side="right") as rcp,
        nc.sbuf_tensor("lnd", [1, 512], F32, side="right") as lnd,
        nc.sbuf_tensor("bcs", [64, 2 * 512], F16, side="right") as bcs,
        nc.sbuf_tensor("stile", [128, 2 * 512], F16, side="right") as stile,
        nc.sbuf_tensor("gtile", [128, 2 * 512], F16, side="right") as gtile,
        nc.sbuf_tensor("xtile", [128, 2 * 512], F32, side="right") as xtile,
        nc.sbuf_tensor("otile", [128, 2 * 512], F32, side="right") as otile,
    ):
        g_t = g_scr.ap().rearrange("(n q) c -> n q c", q=128)
        xh_t = xh_d.rearrange("(n q) c -> n q c", q=128)
        o_t = out_d.rearrange("(n q) c -> n q c", q=128)
        S0, S1, Y0, Y1, D0, D1, QL, BC = range(8)
        SB = [S0, S1, D0, D1]
        grans = []
        g = 0
        for hh in range(P5_HEADS):
            for qb in range(P5_QB):
                hq = hh * 8 + qb
                nkt = 4 * qb + 4
                for kt in range(nkt):
                    grans.append((hh, qb, kt, g, hq, kt == 0, kt == nkt - 1))
                    g += 1
        NG = g
        GSTART = [hh * 144 for hh in range(9)]
        pe60 = p.cnt["pe6"]
        ve60 = p.cnt["ve6"]
        se60 = p.cnt["se6"]
        dx60 = (p.cnt["dxh0"], p.cnt["dxh1"])
        dg60 = (p.cnt["dg0"], p.cnt["dg1"])
        do60 = (p.cnt["dout0"], p.cnt["dout1"])
        ae0 = p.cnt["ae5"]
        pt0 = p.cnt["pt"]
        ps0 = p.cnt["pe5s"]
        py0 = p.cnt["pe5y"]
        pd0 = p.cnt["pe5d"]
        vr0 = p.cnt["ve5r"]
        pb0 = p.cnt["pe5b"]
        dv0 = p.cnt["dvt"]
        vq0 = p.cnt["ve5q"]
        ones_col = masks[:, 4 * 512:4 * 512 + 1]
        ones_row64 = masks[0:1, 4 * 512:4 * 512 + 64]

        def p6_pair(tensor, k):
            if k >= 1:
                tensor.wait_ge(p.sem["se6"], se60 + 2 * k)
            for i in (2 * k, 2 * k + 1):
                for c4 in range(4):
                    ti = tensor.transpose(
                        pbh[QL][:, ((i % 2) * 4 + c4) * 128:
                                ((i % 2) * 4 + c4 + 1) * 128],
                        sT[:, c4 * T + i * 128:c4 * T + (i + 1) * 128],
                        ident[:])
                ti.then_inc(p.inc("pe6"), 1)

        def emit_bcast(tensor, pend, at_head=False):
            tensor.wait_ge(p.sem["ve5r"], vr0 + pend + 1)
            if pend >= 1:
                tensor.wait_ge(p.sem["dvt"], dv0 + pend)
            if at_head:
                tensor.wait_ge(p.sem["ve5q"], vq0 + ((pend + 1) // 8 + 1) * 32)
            tensor.matmul(pb[BC][0:64, :], ones_row64,
                          rcp[0:1, (pend % 2) * 512:(pend % 2 + 1) * 512],
                          start=True, stop=True).then_inc(p.inc("pe5b"), 1)

        @block.tensor
        def _(tensor):
            pending = []

            def emit_yd(tensor, pg):
                (hh, qb, kt, g, hq, first, last) = pg
                tensor.wait_ge(p.sem["pt"], pt0 + g + 1)
                tensor.matmul(pb[Y0 + hq % 2][0:65, :],
                              u[:, kt * 520 + hh * 65:kt * 520 + (hh + 1) * 65],
                              ptil[:, (g % 4) * 512:(g % 4 + 1) * 512],
                              start=first, stop=last).then_inc(p.inc("pe5y"), 1)

            for gran in grans:
                (hh, qb, kt, g, hq, first, last) = gran
                if first:
                    if qb == 0:
                        off = (hh % 2) * 64
                        rc = hh // 2
                        for lc in range(4):
                            for qc in range(8):
                                qi = hh * 32 + lc * 8 + qc
                                if qi >= 2:
                                    tensor.wait_ge(p.sem["ve5q"], vq0 + qi - 1)
                                if qi % 2 == 1 and hh >= 1 and lc == 0 and qc == 1:
                                    tensor.wait_ge(p.sem["dvt"],
                                                   dv0 + hh * 8 - 1)
                                tensor.matmul(
                                    pb[QL if qi % 2 == 0 else BC][:],
                                    keff[off:off + 64,
                                         rc * 512 + lc * 128:rc * 512 + (lc + 1) * 128],
                                    qattT[off:off + 64,
                                          rc * T + qc * 512:rc * T + (qc + 1) * 512],
                                    start=True, stop=True).then_inc(p.inc("pe5s"), 1)
                                if qc in (0, 1) and lc == 0 and pending:
                                    emit_yd(tensor, pending.pop(0))
                    tensor.wait_ge(p.sem["ve5q"], vq0 + hh * 32 + 25 + qb)
                    if hq >= 2:
                        tensor.wait_ge(p.sem["dvt"], dv0 + hq - 1)
                if hq >= 1 and kt == 2:
                    emit_bcast(tensor, hq - 1, at_head=(qb == 0))
                if do_p6 and hh == 7 and qb >= 1 and kt == 4:
                    tensor.wait_ge(p.sem["dvt"], dv0 + hq)
                    if qb == 1:
                        tensor.wait_ge(p.sem["ve5q"], vq0 + 256)
                    p6_pair(tensor, 2 * (qb - 1))
                if do_p6 and hh == 7 and qb >= 1 and kt == 6:
                    p6_pair(tensor, 2 * (qb - 1) + 1)
                if g >= 4:
                    tensor.wait_ge(p.sem["ae5"], ae0 + g - 3)
                for lc in range(4):
                    mm = tensor.matmul(
                        pb[SB[g % 4]][:],
                        ckvT[:, lc * T + kt * 128:lc * T + (kt + 1) * 128],
                        qlat[:, lc * T + qb * 512:lc * T + (qb + 1) * 512],
                        start=(lc == 0), stop=(lc == 3))
                mm.then_inc(p.inc("pe5s"), 1)
                pending.append(gran)
                if len(pending) > 2:
                    emit_yd(tensor, pending.pop(0))
            while pending:
                emit_yd(tensor, pending.pop(0))
            emit_bcast(tensor, 63)
            if do_p6:
                tensor.wait_ge(p.sem["dvt"], dv0 + 64)
                p6_pair(tensor, 14)
                p6_pair(tensor, 15)

        @block.scalar
        def _(scalar):
            def p6_evac(i):
                par = i % 2
                scalar.wait_ge(p.sem["pe6"], pe60 + 2 * (i // 2) + 2)
                if i >= 2:
                    scalar.wait_ge(p.sem["ve6"], ve60 + i - 1)
                scalar.activation(stile[:, par * 512:(par + 1) * 512],
                                  pbh[QL][:, par * 512:(par + 1) * 512],
                                  AF.Copy).then_inc(p.inc("se6"), 1)

            def tail_recip(hp, glast):
                par = hp % 2
                scalar.wait_ge(p.sem["pe5y"], py0 + glast + 1)
                scalar.activation(lnd[0:1, :], pb[Y0 + par][64:65, :], AF.Ln)
                scalar.activation(rcp[0:1, par * 512:(par + 1) * 512],
                                  lnd[0:1, :], AF.Exp,
                                  scale=-1.0).then_inc(p.inc("ve5r"), 1)

            for (hh, qb, kt, g, hq, first, last) in grans:
                if hq >= 1 and kt == 1:
                    tail_recip(hq - 1, g - 2)
                if do_p6 and hh == 7 and qb >= 1 and kt == 4:
                    for i6 in (4 * (qb - 1), 4 * (qb - 1) + 1):
                        p6_evac(i6)
                if do_p6 and hh == 7 and qb >= 1 and kt == 6:
                    for i6 in (4 * (qb - 1) + 2, 4 * (qb - 1) + 3):
                        p6_evac(i6)
                scalar.wait_ge(p.sem["pe5s"], ps0 + 32 * (hh + 1) + g + 1)
                if g >= 4:
                    scalar.wait_ge(p.sem["pe5y"], py0 + g - 3)
                scalar.activation(ptil[:, (g % 4) * 512:(g % 4 + 1) * 512],
                                  pb[SB[g % 4]][:], AF.Exp, bias=cst[:, 0:1],
                                  scale=0.125).then_inc(p.inc("ae5"), 1)
            tail_recip(63, NG - 1)
            if do_p6:
                for i6 in range(28, 32):
                    p6_evac(i6)

        if do_p6:
            NT6 = T // 128

            @block.gpsimd
            def _(gp):
                for i in range(NT6):
                    par = i % 2
                    if i >= 2:
                        gp.wait_ge(p.sem["ve6"], ve60 + i - 1)
                    gp.dma_start(gtile[:, par * 512:(par + 1) * 512],
                                 g_t[i]).then_inc(p.inc("dg%d" % par, 16), 16)
                    gp.dma_start(xtile[:, par * 512:(par + 1) * 512],
                                 xh_t[i]).then_inc(p.inc("dxh%d" % par, 16), 16)
                gp.wait_ge(p.sem["dout0"], do60[0] + 16 * (NT6 // 2))
                gp.wait_ge(p.sem["dout1"], do60[1] + 16 * (NT6 // 2))

            @block.sync
            def _(sync):
                for i in range(NT6):
                    par = i % 2
                    sync.wait_ge(p.sem["ve6"], ve60 + i + 1)
                    sync.dma_start(o_t[i], otile[:, par * 512:(par + 1) * 512]
                                   ).then_inc(p.inc("dout%d" % par, 16), 16)

        @block.vector
        def _(vector):
            def p6_combine(i):
                par = i % 2
                vector.wait_ge(p.sem["se6"], se60 + i + 1)
                vector.wait_ge(p.sem["dg%d" % par],
                               dg60[par] + 16 * (i // 2 + 1))
                vector.wait_ge(p.sem["dxh%d" % par],
                               dx60[par] + 16 * (i // 2 + 1))
                if i >= 2:
                    vector.wait_ge(p.sem["dout%d" % par],
                                   do60[par] + 16 * ((i - 2) // 2 + 1))
                vector.tensor_tensor(otile[:, par * 512:(par + 1) * 512],
                                     stile[:, par * 512:(par + 1) * 512],
                                     gtile[:, par * 512:(par + 1) * 512],
                                     op=ALU.mult)
                vector.tensor_add(otile[:, par * 512:(par + 1) * 512],
                                  otile[:, par * 512:(par + 1) * 512],
                                  xtile[:, par * 512:(par + 1) * 512]
                                  ).then_inc(p.inc("ve6"), 1)

            def tail_stt(hp):
                par = hp % 2
                hhp, qbp = hp // 8, hp % 8
                vector.wait_ge(p.sem["pe5b"], pb0 + hp + 1)
                vector.tensor_copy(bcs[:, par * 512:(par + 1) * 512],
                                   pb[BC][0:64, :])
                off = (hhp % 2) * 64
                vector.scalar_tensor_tensor(
                    sT[off:off + 64,
                       (hhp // 2) * T + qbp * 512:(hhp // 2) * T + (qbp + 1) * 512],
                    pb[Y0 + par][0:64, :], 1.0,
                    bcs[:, par * 512:(par + 1) * 512],
                    op0=ALU.mult, op1=ALU.mult).then_inc(p.inc("dvt"), 1)

            for (hh, qb, kt, g, hq, first, last) in grans:
                if qb == 0 and first:
                    for lc in range(4):
                        for qc in range(8):
                            qi = hh * 32 + lc * 8 + qc
                            vector.wait_ge(p.sem["pe5s"],
                                           ps0 + GSTART[hh] + qi + 1)
                            vector.tensor_copy(
                                qlat[:, lc * T + qc * 512:lc * T + (qc + 1) * 512],
                                pb[QL if qi % 2 == 0 else BC][:]
                                ).then_inc(p.inc("ve5q"), 1)
                if hq >= 1 and kt == 2:
                    tail_stt(hq - 1)
                vector.wait_ge(p.sem["ae5"], ae0 + g + 1)
                midx = kt - 4 * qb if kt >= 4 * qb else 4
                vector.tensor_tensor(ptil[:, (g % 4) * 512:(g % 4 + 1) * 512],
                                     ptil[:, (g % 4) * 512:(g % 4 + 1) * 512],
                                     masks[:, midx * 512:(midx + 1) * 512],
                                     op=ALU.mult).then_inc(p.inc("pt"), 1)
                if do_p6 and hh == 7 and qb >= 1 and kt == 5:
                    for i6 in (4 * (qb - 1), 4 * (qb - 1) + 1):
                        p6_combine(i6)
                if do_p6 and hh == 7 and qb >= 1 and kt == 7:
                    for i6 in (4 * (qb - 1) + 2, 4 * (qb - 1) + 3):
                        p6_combine(i6)
            tail_stt(63)
            if do_p6:
                for i6 in range(28, 32):
                    p6_combine(i6)



def emit_p6(nc, block, p, sT, xh_d, g_scr, out_d, ident, pbh):
    with (
        nc.sbuf_tensor("stile", [128, 2 * 512], F16, side="right") as stile,
        nc.sbuf_tensor("gtile", [128, 2 * 512], F16, side="right") as gtile,
        nc.sbuf_tensor("xtile", [128, 2 * 512], F32, side="right") as xtile,
        nc.sbuf_tensor("otile", [128, 2 * 512], F32, side="right") as otile,
    ):
        g_t = g_scr.ap().rearrange("(n q) c -> n q c", q=128)
        xh_t = xh_d.rearrange("(n q) c -> n q c", q=128)
        o_t = out_d.rearrange("(n q) c -> n q c", q=128)
        NT = T // 128
        pe0 = p.cnt["pe6"]
        ve0 = p.cnt["ve6"]
        se0 = p.cnt["se6"]
        dx0 = (p.cnt["dxh0"], p.cnt["dxh1"])
        dg0 = (p.cnt["dg0"], p.cnt["dg1"])
        do0 = (p.cnt["dout0"], p.cnt["dout1"])

        @block.gpsimd
        def _(gp):
            for i in range(NT):
                par = i % 2
                if i >= 2:
                    gp.wait_ge(p.sem["ve6"], ve0 + i - 1)
                gp.dma_start(gtile[:, par * 512:(par + 1) * 512],
                             g_t[i]).then_inc(p.inc("dg%d" % par, 16), 16)
                gp.dma_start(xtile[:, par * 512:(par + 1) * 512],
                             xh_t[i]).then_inc(p.inc("dxh%d" % par, 16), 16)
            gp.wait_ge(p.sem["dout0"], do0[0] + 16 * (NT // 2))
            gp.wait_ge(p.sem["dout1"], do0[1] + 16 * (NT // 2))

        @block.tensor
        def _(tensor):
            for i in range(NT):
                par = i % 2
                if i >= 2:
                    tensor.wait_ge(p.sem["se6"], se0 + i - 1)
                for c4 in range(4):
                    ti = tensor.transpose(
                        pbh[6 + par][:, c4 * 128:(c4 + 1) * 128],
                        sT[:, c4 * T + i * 128:c4 * T + (i + 1) * 128],
                        ident[:])
                ti.then_inc(p.inc("pe6"), 1)

        @block.scalar
        def _(scalar):
            for i in range(NT):
                par = i % 2
                scalar.wait_ge(p.sem["pe6"], pe0 + i + 1)
                if i >= 2:
                    scalar.wait_ge(p.sem["ve6"], ve0 + i - 1)
                scalar.activation(stile[:, par * 512:(par + 1) * 512],
                                  pbh[6 + par][:, 0:512],
                                  AF.Copy).then_inc(p.inc("se6"), 1)

        @block.vector
        def _(vector):
            for i in range(NT):
                par = i % 2
                vector.wait_ge(p.sem["se6"], se0 + i + 1)
                vector.wait_ge(p.sem["dg%d" % par], dg0[par] + 16 * (i // 2 + 1))
                vector.wait_ge(p.sem["dxh%d" % par], dx0[par] + 16 * (i // 2 + 1))
                if i >= 2:
                    vector.wait_ge(p.sem["dout%d" % par],
                                   do0[par] + 16 * ((i - 2) // 2 + 1))
                vector.tensor_tensor(otile[:, par * 512:(par + 1) * 512],
                                     stile[:, par * 512:(par + 1) * 512],
                                     gtile[:, par * 512:(par + 1) * 512],
                                     op=ALU.mult)
                vector.tensor_add(otile[:, par * 512:(par + 1) * 512],
                                  otile[:, par * 512:(par + 1) * 512],
                                  xtile[:, par * 512:(par + 1) * 512]
                                  ).then_inc(p.inc("ve6"), 1)

        @block.sync
        def _(sync):
            for i in range(NT):
                par = i % 2
                sync.wait_ge(p.sem["ve6"], ve0 + i + 1)
                sync.dma_start(o_t[i], otile[:, par * 512:(par + 1) * 512]
                               ).then_inc(p.inc("dout%d" % par, 16), 16)



def _host_inputs(ins):
    x = np.asarray(ins["L0_attn_x"], dtype=np.float32)
    W_dq = np.asarray(ins["L0_attn_W_dq"], dtype=np.float32)
    W_uq = np.asarray(ins["L0_attn_W_uq"], dtype=np.float32)
    W_dkv = np.asarray(ins["L0_attn_W_dkv"], dtype=np.float32)
    W_uk = np.asarray(ins["L0_attn_W_uk"], dtype=np.float32)
    W_uv = np.asarray(ins["L0_attn_W_uv"], dtype=np.float32)
    W_o = np.asarray(ins["L0_attn_W_o"], dtype=np.float32)
    Wq = np.asarray(ins["L0_ffn_Wq"], dtype=np.float32)
    keys = np.asarray(ins["L0_ffn_keys"], dtype=np.float32)
    W_down = np.asarray(ins["L0_ffn_W_down"], dtype=np.float32)
    W_up = np.asarray(ins["L0_ffn_W_up"], dtype=np.float32)

    wdkvT = np.ascontiguousarray(W_dkv.T)
    wdqT = np.ascontiguousarray(W_dq.T)
    wuqT = np.ascontiguousarray(W_uq.T)
    woT = np.ascontiguousarray(W_o.T)
    wqT = np.ascontiguousarray(Wq.T)
    keysT = np.ascontiguousarray(keys.transpose(2, 0, 3, 1))
    wd256T = np.ascontiguousarray(W_down[:256].T)

    ident = np.eye(128, dtype=np.float16)
    masks = np.zeros((5, 128, 512), dtype=np.float16)
    kk = np.arange(128)[:, None]
    qq = np.arange(512)[None, :]
    for j in range(4):
        masks[j] = (128 * j + kk <= qq).astype(np.float16)
    masks[4] = 1.0
    cst = np.zeros((128, 8), dtype=np.float32)
    cst[:, 0] = -4.0
    cst[:, 1] = 1e-5

    in_maps = []
    for c in range(8):
        b, half = c // 2, c % 2
        ch0 = half * 512
        in_maps.append({
            "x": np.ascontiguousarray(x[b]),
            "xh": np.ascontiguousarray(x[b][:, ch0:ch0 + 512]),
            "wdkvT": wdkvT,
            "wdqT": wdqT,
            "wuqT": wuqT,
            "wuqTh": np.ascontiguousarray(wuqT[:, ch0:ch0 + 512]),
            "wdqh": np.ascontiguousarray(W_dq[:, ch0:ch0 + 512]),
            "wuk": np.ascontiguousarray(W_uk),
            "wuv": np.ascontiguousarray(W_uv),
            "woTh": np.ascontiguousarray(woT[:, ch0:ch0 + 512]),
            "wqT": wqT,
            "keysT": keysT,
            "wd256T": wd256T,
            "wup256h": np.ascontiguousarray(W_up[:256, ch0:ch0 + 512]),
            "ident": ident,
            "masks": masks,
            "cst": cst,
        })
    return in_maps


def kernel(**ins):
    in_maps = _host_inputs(ins)
    nc = build()
    res = run_bass_kernel_spmd(nc, in_maps, core_ids=list(range(8)))
    out = np.empty((4, T, C), dtype=np.float32)
    for c in range(8):
        b, half = c // 2, c % 2
        out[b, :, half * 512:(half + 1) * 512] = res.results[c]["out"]
    return out
